# version 04
# baseline (speedup 1.0000x reference)
import numpy as np
import ml_dtypes
from contextlib import ExitStack

import concourse.bass as bass
import concourse.mybir as mybir
from concourse.bass_utils import run_bass_kernel_spmd

F32 = mybir.dt.float32
BF16 = mybir.dt.bfloat16
I16 = mybir.dt.int16
U8 = mybir.dt.uint8
AF = mybir.ActivationFunctionType
ALU = mybir.AluOpType
AX = mybir.AxisListType

D = 1024
IN = 4096
H = 8
CH = 128
NS = 64
ACT0 = 32
NACT = NS - ACT0
MAGIC = float(np.float32(3 << 22))
QK = float(0x5F37)

BODY_OF = {"pe": "tensor", "dve": "vector", "act": "scalar", "pool": "gpsimd",
           "qs": "sync", "qa": "scalar", "qg": "gpsimd"}
IS_DMA = {"pe": False, "dve": False, "act": False, "pool": False,
          "qs": True, "qa": True, "qg": True}

DEBUG_PHASE = None
SELF_DEPS = True
CHAIN_SYNC = True


class Task:
    __slots__ = ("ekey", "idx", "emit", "deps", "name", "reads", "writes")

    def __init__(self, ekey, idx, emit, name):
        self.ekey = ekey
        self.idx = idx
        self.emit = emit
        self.deps = {}
        self.name = name
        self.reads = set()
        self.writes = set()


class Sched:
    def __init__(self):
        self.counts = {k: 0 for k in BODY_OF}
        self.body_items = {b: [] for b in set(BODY_OF.values())}
        self.res = {}
        self.fence_snap = None

    def fence(self):
        self.fence_snap = dict(self.counts)

    def begin_stages(self):
        self._stq = []

    def flush_stages(self):
        q = self._stq
        self._stq = None
        if not q:
            return
        from collections import defaultdict
        bych = defaultdict(list)
        maxc = maxs = 0
        for c, st_, rec in q:
            bych[(c, st_)].append(rec)
            maxc = max(maxc, c)
            maxs = max(maxs, st_)
        for t in range(maxc + maxs + 1):
            for st_ in range(maxs, -1, -1):
                c = t - st_
                for rec in bych.get((c, st_), ()):
                    self.task(*rec)

    def task(self, ekey, emit, reads=(), writes=(), name="", nofence=False,
             stage=None, chunk=None):
        if stage is not None and getattr(self, "_stq", None) is not None:
            self._stq.append((chunk, stage,
                              (ekey, emit, reads, writes, name, nofence)))
            return None
        idx = self.counts[ekey]
        self.counts[ekey] += 1
        t = Task(ekey, idx, emit, name)

        def dep(tok):
            if tok is None or tok == mytok:
                return
            ek, ix = tok
            if ek == ekey and (IS_DMA[ek] or ek == "pe" or not SELF_DEPS):
                return
            need = ix + 1
            if t.deps.get(ek, 0) < need:
                t.deps[ek] = need

        mytok = (ekey, idx)
        if self.fence_snap is not None and not nofence:
            for ek, cnt in self.fence_snap.items():
                if cnt > 0 and ek != ekey and t.deps.get(ek, 0) < cnt:
                    t.deps[ek] = cnt

        t.reads = set(reads)
        t.writes = set(writes)
        for r in reads:
            e = self.res.get(r)
            if e is None:
                e = self.res[r] = [None, []]
            dep(e[0])
            e[1].append(mytok)
        for wname in writes:
            e = self.res.get(wname)
            if e is not None:
                dep(e[0])
                for rt in e[1]:
                    dep(rt)
            self.res[wname] = [mytok, []]
        self.body_items[BODY_OF[ekey]].append(t)
        return mytok

    def emit_all(self, nc, es, junk):
        sems = {k: es.enter_context(nc.semaphore(f"sem_{k}")) for k in BODY_OF}
        blk = es.enter_context(nc.Block())
        total = {k: self.counts[k] * (16 if IS_DMA[k] else 1) for k in BODY_OF}
        watermark = {b: {k: 0 for k in BODY_OF} for b in self.body_items}

        def run_body(body, engobj):
            wm = watermark[body]
            for t in self.body_items[body]:
                for ek in sorted(t.deps):
                    cnt = t.deps[ek]
                    val = cnt * 16 if IS_DMA[ek] else cnt
                    if wm[ek] >= val:
                        continue
                    engobj.wait_ge(sems[ek], val)
                    wm[ek] = val
                last = t.emit(engobj)
                assert last is not None, f"task {t.name} returned no instruction"
                last.then_inc(sems[t.ekey], 16 if IS_DMA[t.ekey] else 1)
            if body == "gpsimd":
                for k in ("qs", "qa", "qg", "pe", "dve", "act", "pool"):
                    if total[k] and wm[k] < total[k]:
                        engobj.wait_ge(sems[k], total[k])

        blk.sync(lambda e: run_body("sync", e))
        blk.vector(lambda e: run_body("vector", e))
        blk.scalar(lambda e: run_body("scalar", e))
        blk.tensor(lambda e: run_body("tensor", e))
        blk.gpsimd(lambda e: run_body("gpsimd", e))


class Arena:
    def __init__(self, kb, name, nbytes):
        self.name = name
        self.t = kb.es.enter_context(kb.nc.sbuf_tensor(name, [128, nbytes], U8))
        self.nbytes = nbytes
        self.off = 0

    def reset(self, off=0):
        self.off = off

    def view(self, shape, dt):
        esz = 2 if dt in (BF16, I16) else 4
        ncols0 = int(np.prod(shape[1:]))
        align = 64 if ncols0 * esz >= 512 else 8
        self.off = (self.off + align - 1) & ~(align - 1)
        ncols = ncols0
        need = ncols * esz
        assert self.off + need <= self.nbytes, \
            f"arena {self.name} overflow: {self.off}+{need}>{self.nbytes}"
        base = self.off // esz
        self.off += need
        ap = self.t.ap().bitcast(dt)
        v = ap[0:shape[0], base:base + ncols]
        if len(shape) == 3:
            v = v.rearrange("p (a b) -> p a b", a=shape[1])
        return v


class KB:
    def __init__(self, debug=None):
        self.nc = bass.Bass("TRN2", target_bir_lowering=False)
        self.s = Sched()
        self.es = ExitStack()
        self.debug = debug
        self.dram = {}
        self.pb = []

    def sb(self, name, shape, dt):
        return self.es.enter_context(self.nc.sbuf_tensor(name, shape, dt))

    def dram_t(self, name, shape, dt, kind):
        t = self.nc.dram_tensor(name, shape, dt, kind=kind)
        self.dram[name] = t.ap()
        return self.dram[name]


_FILL = {"t": None}


def set_filler(ap):
    _FILL["t"] = ap


def vsep(nc):
    nc.vector.memset(_FILL["t"], 0.0)


_CHAIN = {"sem": None, "cnt": 0}


def set_chain_sem(sem):
    _CHAIN["sem"] = sem
    _CHAIN["cnt"] = 0


class VG:

    def __init__(self, nc):
        self._nc = nc
        self._pending = None

    def __getattr__(self, name):
        f = getattr(self._nc.vector, name)
        nc = self._nc

        def g(*a, **k):
            if self._pending is not None:
                if CHAIN_SYNC:
                    self._pending.then_inc(_CHAIN["sem"], 1)
                    _CHAIN["cnt"] += 1
                    nc.vector.wait_ge(_CHAIN["sem"], _CHAIN["cnt"])
                else:
                    vsep(nc)
            r = f(*a, **k)
            self._pending = r
            return r
        return g


def emit_rsqrt(nc, v_ap, y_ap, s0, s1, s2, e=None):
    if e is None:
        e = VG(nc)
    e.memset(s0, 0.0)
    hi = v_ap.bitcast(I16)[:, 1::2]
    e.tensor_copy(out=s1, in_=hi)
    e.tensor_scalar(out=s1, in0=s1, scalar1=-0.5, scalar2=QK,
                    op0=ALU.mult, op1=ALU.add)
    e.tensor_copy(out=s0.bitcast(I16)[:, 1::2], in_=s1)
    e.tensor_scalar(out=s2, in0=v_ap, scalar1=-0.5, scalar2=None,
                    op0=ALU.mult)
    cur = s0
    last = None
    for it in range(3):
        dst = y_ap if it == 2 else (s1 if cur is s0 else s0)
        e.tensor_tensor(out=dst, in0=cur, in1=cur, op=ALU.mult)
        e.tensor_tensor(out=dst, in0=dst, in1=s2, op=ALU.mult)
        e.tensor_scalar(out=dst, in0=dst, scalar1=1.5, scalar2=None,
                        op0=ALU.add)
        last = e.tensor_tensor(out=dst, in0=cur, in1=dst, op=ALU.mult)
        cur = dst
    return last



def build_program(debug=None):
    kb = KB(debug)
    nc = kb.nc

    kb.dram_t("x", [NS * CH, D], F32, "ExternalInput")
    kb.dram_t("maskc", [CH, NS], F32, "ExternalInput")
    kb.dram_t("consts", [128, 8], F32, "ExternalInput")
    kb.dram_t("constsb", [128, 4], BF16, "ExternalInput")
    kb.dram_t("identb", [128, 128], BF16, "ExternalInput")
    kb.dram_t("trimaskT", [128, 128], BF16, "ExternalInput")
    kb.dram_t("ones2", [2, D], BF16, "ExternalInput")
    kb.dram_t("mnwb", [128, D], BF16, "ExternalInput")
    kb.dram_t("anw", [128, 8], F32, "ExternalInput")
    kb.dram_t("onw", [128, 8], F32, "ExternalInput")
    for nm, shp in (("wq", [D, D]), ("wk", [D, D]), ("wv", [D, D]),
                    ("wo", [D, D]), ("wgate", [D, IN]), ("wup", [D, IN]),
                    ("wdown", [IN, D]), ("w1t", [D, IN]), ("w2t", [IN, D]),
                    ("b1", [1, IN]), ("b2", [1, D])):
        kb.dram_t(nm, shp, F32, "ExternalInput")

    dbg = debug
    kb.dram_t("X1", [NS * CH, D], F32,
              "ExternalOutput" if dbg == "P1" else "Internal")
    kb.dram_t("Y1", [NACT * CH, D], F32,
              "ExternalOutput" if dbg == "A1" else "Internal")
    kb.dram_t("S1", [NACT * CH, D], F32,
              "ExternalOutput" if dbg == "A1" else "Internal")
    kb.dram_t("G", [NACT * CH, IN], BF16, "Internal")
    kb.dram_t("X2", [NACT * CH, D], F32,
              "ExternalOutput" if dbg == "A2" else "Internal")
    if dbg is None:
        kb.dram_t("out", [NACT * CH, D], F32, "ExternalOutput")
    kb.dram_t("WQ1d", [D, IN], BF16, "Internal")
    kb.dram_t("WQ2d", [IN, D], BF16, "Internal")
    if dbg == "A1":
        kb.dram_t("dbg_K0", [128, D], BF16, "ExternalOutput")
        kb.dram_t("dbg_V0", [128, D], BF16, "ExternalOutput")
        kb.dram_t("dbg_t0", [128, D], BF16, "ExternalOutput")
        kb.dram_t("dbg_S32", [128, D], BF16, "ExternalOutput")
        kb.dram_t("dbg_q32", [128, D], BF16, "ExternalOutput")
        kb.dram_t("dbg_at32", [128, D], BF16, "ExternalOutput")
        kb.dram_t("dbg_on32", [128, D], BF16, "ExternalOutput")
    if dbg == "P1":
        kb.dram_t("dbg_xq", [2 * 128, D], BF16, "ExternalOutput")
        kb.dram_t("dbg_xqT", [2 * 128, D], BF16, "ExternalOutput")
        kb.dram_t("dbg_h", [2 * 128, IN], F32, "ExternalOutput")
        kb.dram_t("dbg_st", [2 * 128, 32], F32, "ExternalOutput")
        kb.dram_t("dbg_st2", [2 * 128, 32], F32, "ExternalOutput")
        kb.dram_t("dbg_strow", [8, 128], BF16, "ExternalOutput")
        kb.dram_t("dbg_wq1", [128, 1024], BF16, "ExternalOutput")
        kb.dram_t("dbg_swb", [128, 2], F32, "ExternalOutput")

    with kb.es:
        kb.csb = kb.sb("s_csb", [128, 8], F32)
        kb.csbb = kb.sb("s_csbb", [128, 4], BF16)
        nc.const_aps.register(F32, 0.0, kb.csb[:, 0:1])
        nc.const_aps.register(F32, 1.0, kb.csb[:, 1:2])
        nc.const_aps.register(BF16, 0.0, kb.csbb[:, 0:1])
        nc.const_aps.register(BF16, 1.0, kb.csbb[:, 1:2])
        kb.identb = kb.sb("s_identb", [128, 128], BF16)
        kb.trimaskT = kb.sb("s_trimaskT", [128, 128], BF16)
        kb.maskc = kb.sb("s_maskc", [128, NS], F32)
        kb.ones2 = kb.sb("s_ones2", [2, D], BF16)
        kb.mnwb = kb.sb("s_mnwb", [128, D], BF16)
        kb.anw = kb.sb("s_anw", [128, 8], F32)
        kb.onw = kb.sb("s_onw", [128, 8], F32)
        kb.swb = kb.sb("swb", [128, 2], F32)
        kb.bfscr = kb.sb("bfscr", [128, 4], BF16)
        kb.b1x = kb.sb("b1x", [3, IN], BF16)
        kb.b2x = kb.sb("b2x", [3, D], BF16)
        kb.brow = kb.sb("brow", [1, 512], F32)
        kb.brow2 = kb.sb("brow2", [1, 512], F32)
        kb.browb = kb.sb("browb", [1, 512], BF16)
        kb.macc = kb.sb("macc", [128, 4], F32)
        kb.onescol = kb.sb("onescol", [128, 1], F32)
        kb.cell = kb.sb("cell", [1, 4], F32)
        kb.onesrow_f = kb.sb("onesrow_f", [1, 128], F32)
        kb.jdve = kb.sb("jdve", [128, 1], F32)
        kb.jact = kb.sb("jact", [128, 1], F32)
        kb.jpool = kb.sb("jpool", [128, 1], F32)
        set_filler(kb.jdve[:])
        kb.vch = kb.es.enter_context(nc.semaphore("vch"))
        set_chain_sem(kb.vch)

        for i in range(6):
            kb.pb.append(kb.es.enter_context(
                nc.psum_tensor(f"pb{i}", [128, 512], F32)))
        for i in range(6, 8):
            kb.pb.append(kb.es.enter_context(
                nc.psum_tensor(f"pb{i}", [128, 1024], BF16)))

        kb.WA = Arena(kb, "WA", 65536)
        kb.WB = Arena(kb, "WB", 65536)
        kb.SC = Arena(kb, "SC", 60608)

        startup(kb)
        prep_ffn_weights(kb)
        phase_ffn(kb, src="x", dst="X1", nslots=NS, tag="P1", first=True)
        if dbg != "P1":
            phase_a1(kb)
            if dbg != "A1":
                phase_a2a(kb)
                phase_a2b(kb)
                if dbg != "A2":
                    phase_ffn(kb, src="X2", dst="out", nslots=NACT, tag="P2",
                              first=False)
        kb.s.emit_all(nc, kb.es, (kb.jdve[:], kb.jact[:], kb.jpool[:]))
    return kb


def startup(kb):
    nc, s, dr = kb.nc, kb.s, kb.dram

    def ld(dst, src_name, res):
        s.task("qs", lambda e, d=dst, sn=src_name: e.dma_start(out=d,
                                                               in_=dr[sn]),
               writes=(res,), name=f"ld_{src_name}")

    ld(kb.csb[:], "consts", "csb")
    ld(kb.csbb[:], "constsb", "csbb")
    ld(kb.identb[:], "identb", "identb")
    ld(kb.trimaskT[:], "trimaskT", "trimaskT")
    ld(kb.maskc[:], "maskc", "maskc")
    ld(kb.ones2[:], "ones2", "ones2")
    ld(kb.mnwb[:], "mnwb", "mnwb")
    ld(kb.anw[:], "anw", "anw")
    ld(kb.onw[:], "onw", "onw")
    s.task("dve", lambda e: e.memset(kb.onescol[:], 1.0), writes=("onescol",),
           name="ones_col")
    s.task("dve", lambda e: e.memset(kb.onesrow_f[:], 1.0), writes=("onesrow",),
           name="ones_row")

    for bname, btile, wdt in (("b1", kb.b1x, IN), ("b2", kb.b2x, D)):
        for sl in range(wdt // 512):
            c0 = sl * 512
            s.task("qs", lambda e, bn=bname, c0=c0: e.dma_start(
                out=kb.brow[:], in_=dr[bn][0:1, c0:c0 + 512]),
                writes=("brow",), name=f"ldb_{bname}_{sl}")
            def hl(e):
                e = VG(nc)
                e.tensor_copy(out=kb.browb[:], in_=kb.brow[:])
                e.tensor_copy(out=kb.brow2[:], in_=kb.browb[:])
                return e.tensor_tensor(out=kb.brow2[:], in0=kb.brow[:],
                                       in1=kb.brow2[:], op=ALU.subtract)
            s.task("dve", hl, reads=("brow",), writes=("browb", "brow2"),
                   name=f"bhl_{bname}_{sl}")
            def p_hi(e, bt=btile, c0=c0):
                return e.dma_start(out=bt[0:1, c0:c0 + 512],
                                   in_=kb.browb[0:1, :])
            s.task("qg", p_hi, reads=("browb",), writes=(f"bx_{bname}",),
                   name=f"bhi_{bname}_{sl}")
            def p_hi2(e, bt=btile, c0=c0):
                return e.dma_start(out=bt[2:3, c0:c0 + 512],
                                   in_=kb.browb[0:1, :])
            s.task("qg", p_hi2, reads=("browb",), writes=(f"bx_{bname}",),
                   name=f"bhi2_{bname}_{sl}")
            def lo_conv(e):
                return e.tensor_copy(out=kb.browb[:], in_=kb.brow2[:])
            s.task("dve", lo_conv, reads=("brow2",), writes=("browb",),
                   name=f"blc_{bname}_{sl}")
            def p_lo(e, bt=btile, c0=c0):
                return e.dma_start(out=bt[1:2, c0:c0 + 512],
                                   in_=kb.browb[0:1, :])
            s.task("qg", p_lo, reads=("browb",), writes=(f"bx_{bname}",),
                   name=f"blo_{bname}_{sl}")


def prep_ffn_weights(kb):
    nc, s, dr = kb.nc, kb.s, kb.dram
    kb.wq1s = kb.WA.view([128, 8, IN], BF16)
    kb.wq1f = kb.wq1s.rearrange("p a b -> p (a b)")
    kb.wq2s = kb.WB.view([128, 32, D], BF16)
    kb.wq2f = kb.wq2s.rearrange("p a b -> p (a b)")
    stage = kb.SC.view([128, 1024], F32)
    kb.stage_view = stage
    kb.p1_sc_base = kb.SC.off

    w1f = dr["w1t"].rearrange("(c p) n -> p c n", p=128)
    w2f = dr["w2t"].rearrange("(c p) n -> p c n", p=128)
    sp1 = dr["WQ1d"].rearrange("(c p) n -> p c n", p=128)
    sp2 = dr["WQ2d"].rearrange("(c p) n -> p c n", p=128)

    def dtile(w3, wdt, hh):
        if wdt >= 1024:
            ch, off = (hh * 1024) // wdt, (hh * 1024) % wdt
            return w3[:, ch, off:off + 1024]
        raise AssertionError

    s.task("dve", lambda e: e.memset(kb.macc[:], 0.0), writes=("macc",),
           name="macc0")
    NT = (8 * IN) // 1024
    for wi, (wf, wdt) in ((0, (w1f, IN)), (1, (w2f, D))):
        for hh in range(NT):
            srcv = dtile(wf, wdt, hh)
            s.task("qs", lambda e, sc=srcv: e.dma_start(out=stage[:], in_=sc),
                   writes=("stage",), name=f"mw_ld{wi}_{hh}", nofence=True)
            def red(e, wi=wi):
                e = VG(nc)
                e.tensor_reduce(out=kb.macc[:, 2 + wi:3 + wi], in_=stage[:],
                                axis=AX.X, op=ALU.add, apply_absolute_value=True)
                return e.tensor_tensor(out=kb.macc[:, wi:wi + 1],
                                       in0=kb.macc[:, wi:wi + 1],
                                       in1=kb.macc[:, 2 + wi:3 + wi],
                                       op=ALU.add)
            s.task("dve", red, reads=("stage",), writes=("macc",),
                   name=f"mw_red{wi}_{hh}")
    def totmm(e):
        return e.matmul(out=kb.pb[0][0:1, 0:2], lhsT=kb.onescol[:],
                        rhs=kb.macc[:, 0:2], start=True, stop=True)
    s.task("pe", totmm, reads=("macc", "onescol"), writes=("pb0",),
           name="mw_tot")
    def swcalc(e):
        e = VG(nc)
        e.tensor_copy(out=kb.cell[0:1, 0:2], in_=kb.pb[0][0:1, 0:2])
        e.tensor_scalar(out=kb.cell[0:1, 0:2], in0=kb.cell[0:1, 0:2],
                        scalar1=1.0 / (D * IN), scalar2=1e-5,
                        op0=ALU.mult, op1=ALU.max)
        return e.reciprocal(out=kb.cell[0:1, 0:2], in_=kb.cell[0:1, 0:2])
    s.task("dve", swcalc, reads=("pb0",), writes=("cell",), name="sw_calc")
    def swbmm(e):
        return e.matmul(out=kb.pb[1][:, 0:2], lhsT=kb.onesrow_f[:],
                        rhs=kb.cell[0:1, 0:2], start=True, stop=True)
    s.task("pe", swbmm, reads=("cell", "onesrow"), writes=("pb1",),
           name="swb_mm")
    s.task("dve", lambda e: e.tensor_copy(out=kb.swb[:], in_=kb.pb[1][:, 0:2]),
           reads=("pb1",), writes=("swb",), name="swb_ev")

    for wi, (wf, wdt, dstf, spf, arena, spn) in enumerate((
            (w1f, IN, kb.wq1f, sp1, "WA", "dWQ1d"),
            (w2f, D, kb.wq2f, sp2, "WB", "dWQ2d"))):
        for hh in range(NT):
            sl = slice(hh * 1024, (hh + 1) * 1024)
            srcv = dtile(wf, wdt, hh)
            s.task("qs", lambda e, sc=srcv: e.dma_start(out=stage[:], in_=sc),
                   writes=("stage",), name=f"qw_ld{wi}_{hh}", nofence=True)
            def qa(e, wi=wi):
                return e.tensor_scalar(out=stage[:], in0=stage[:],
                                       scalar1=kb.swb[:, wi:wi + 1],
                                       scalar2=MAGIC, op0=ALU.mult,
                                       op1=ALU.add)
            s.task("pool", qa, reads=("stage", "swb"), writes=("stage",),
                   name=f"qw_a{wi}_{hh}")
            def qb(e):
                return e.tensor_scalar(out=stage[:], in0=stage[:],
                                       scalar1=-MAGIC, scalar2=-1.0,
                                       op0=ALU.add, op1=ALU.max)
            s.task("dve", qb, reads=("stage",), writes=("stage",),
                   name=f"qw_b{wi}_{hh}")
            def qc(e, dsl=dstf[:, sl]):
                return e.tensor_scalar(out=dsl, in0=stage[:], scalar1=1.0,
                                       scalar2=None, op0=ALU.min)
            s.task("dve", qc, reads=("stage",), writes=(arena,),
                   name=f"qw_c{wi}_{hh}")
            s.task("qg", lambda e, dsl=dstf[:, sl], ssl=dtile(spf, wdt, hh):
                   e.dma_start(out=ssl, in_=dsl), reads=(arena,),
                   writes=(spn,), name=f"qw_sp{wi}_{hh}", nofence=True)


def phase_ffn(kb, src, dst, nslots, tag, first):
    nc, s, dr = kb.nc, kb.s, kb.dram
    SC = kb.SC
    pb = kb.pb
    SC.reset(kb.p1_sc_base)
    if not first:
        s.fence()
        w1r = dr["WQ1d"].rearrange("(c p) n -> p c n", p=128)
        w2r = dr["WQ2d"].rearrange("(c p) n -> p c n", p=128)
        s.task("qs", lambda e: e.dma_start(out=kb.wq1s[:], in_=w1r),
               reads=("dWQ1d",), writes=("WA",), name="p2_ldw1", nofence=True)
        s.task("qs", lambda e: e.dma_start(out=kb.wq2s[:], in_=w2r),
               reads=("dWQ2d",), writes=("WB",), name="p2_ldw2", nofence=True)

    Xs = [SC.view([128, D], F32) for _ in range(2)]
    Xr = SC.view([128, D], F32)
    X1o = SC.view([128, D], F32)
    h = SC.view([128, IN], F32)
    xt = SC.view([128, D], F32)
    xq = SC.view([128, D], BF16)
    xqT = SC.view([128, D], BF16)
    ht = kb.stage_view
    hq = SC.view([128, 1024], BF16)
    hqT = SC.view([128, IN], BF16)
    st = [SC.view([128, 32], F32) for _ in range(2)]
    st2 = [SC.view([128, 32], F32) for _ in range(2)]
    stq1 = SC.view([128, 4], BF16)
    stq2 = SC.view([128, 4], BF16)
    strow1 = [SC.view([4, 128], BF16) for _ in range(2)]
    strow2 = [SC.view([4, 128], BF16) for _ in range(2)]

    if first:
        s.fence()

    xr = dr[src]
    dstr = dr[dst]

    C_SUM, C_SQ0, C_SQ1, C_MAX, C_MIN = 0, 1, 2, 3, 4
    C_M, C_V, C_A, C_ARG, C_RLN, C_RRMS = 5, 6, 7, 8, 9, 10
    C_RA, C_AM, C_S, C_C, C_MC, C_SV, C_SVH, C_SVL, C_INV = \
        11, 12, 13, 14, 15, 16, 17, 18, 19
    C_X0, C_X1, C_X2 = 20, 22, 24
    Hs_SUM, Hs_AM, Hs_ARG, Hs_RR, Hs_S, Hs_C, Hs_SV, Hs_SVH, Hs_SVL, Hs_INV = \
        8, 9, 10, 11, 12, 13, 14, 15, 16, 17
    Hs_X0, Hs_X1, Hs_X2 = 18, 19, 20

    s.begin_stages()
    for c in range(nslots):
        p = c % 2
        x3 = c % 2
        t = f"{tag}.{c}"
        rsrc = f"d{src}.{c}" if src != "x" else f"dx.{c}"
        s.task("qs", lambda e, c=c, x3=x3: e.dma_start(
            out=Xs[x3][:], in_=xr[c * CH:(c + 1) * CH, :]),
            reads=(rsrc,), writes=(f"{tag}Xs{x3}",), name=f"{t}.ldX",
            stage=0, chunk=c)
        def stats1(e, p=p, x3=x3):
            e.reduce_sum(out=st[p][:, C_SUM:C_SUM + 1], in_=Xs[x3][:],
                         axis=AX.X)
            e.reduce_max(out=st[p][:, C_MAX:C_MAX + 1], in_=Xs[x3][:],
                         axis=AX.X)
            return e.tensor_reduce(out=st[p][:, C_MIN:C_MIN + 1],
                                   in_=Xs[x3][:], axis=AX.X, op=ALU.min)
        s.task("dve", stats1, reads=(f"{tag}Xs{x3}",),
               writes=(f"{tag}st{p}",), name=f"{t}.stats1", stage=1, chunk=c)
        def sq(e, p=p, x3=x3):
            e.activation(out=ht[:, 0:512], in_=Xs[x3][:, 0:512],
                         func=AF.Square,
                         accum_out=st[p][:, C_SQ0:C_SQ0 + 1])
            return e.activation(out=ht[:, 512:1024], in_=Xs[x3][:, 512:1024],
                                func=AF.Square,
                                accum_out=st[p][:, C_SQ1:C_SQ1 + 1])
        s.task("act", sq, reads=(f"{tag}Xs{x3}", "csb"),
               writes=("stage", f"{tag}st{p}"), name=f"{t}.sq",
               stage=1, chunk=c)
        def stats2(e, p=p):
            e = VG(nc)
            stp = st[p]
            col = lambda i, w=1: stp[:, i:i + w]
            e.tensor_scalar(out=col(C_M), in0=col(C_SUM), scalar1=1.0 / D,
                            scalar2=None, op0=ALU.mult)
            e.tensor_tensor(out=col(C_V), in0=col(C_SQ0), in1=col(C_SQ1),
                            op=ALU.add)
            e.tensor_scalar(out=col(C_V), in0=col(C_V), scalar1=1.0 / D,
                            scalar2=None, op0=ALU.mult)
            e.tensor_tensor(out=col(C_A), in0=col(C_M), in1=col(C_M),
                            op=ALU.mult)
            e.tensor_tensor(out=col(C_V), in0=col(C_V), in1=col(C_A),
                            op=ALU.subtract)
            e.tensor_scalar(out=col(C_A), in0=col(C_V), scalar1=1e-5,
                            scalar2=None, op0=ALU.add)
            e.reciprocal(out=col(C_RA), in_=col(C_A))
            e.tensor_tensor(out=col(C_ARG), in0=col(C_V), in1=col(C_RA),
                            op=ALU.mult)
            e.tensor_scalar(out=col(C_ARG), in0=col(C_ARG), scalar1=1e-6,
                            scalar2=None, op0=ALU.add)
            emit_rsqrt(nc, stp[:, C_A:C_A + 2], stp[:, C_RLN:C_RLN + 2],
                       stp[:, C_X0:C_X0 + 2], stp[:, C_X1:C_X1 + 2],
                       stp[:, C_X2:C_X2 + 2], e=e)
            e.tensor_tensor(out=col(C_AM), in0=col(C_MAX), in1=col(C_M),
                            op=ALU.subtract)
            e.tensor_tensor(out=col(C_MIN), in0=col(C_M), in1=col(C_MIN),
                            op=ALU.subtract)
            e.tensor_tensor(out=col(C_AM), in0=col(C_AM), in1=col(C_MIN),
                            op=ALU.max)
            e.tensor_tensor(out=col(C_AM), in0=col(C_AM), in1=col(C_RLN),
                            op=ALU.mult)
            e.tensor_tensor(out=col(C_AM), in0=col(C_AM), in1=col(C_RRMS),
                            op=ALU.mult)
            e.tensor_scalar(out=col(C_AM), in0=col(C_AM), scalar1=1e-5,
                            scalar2=None, op0=ALU.max)
            e.reciprocal(out=col(C_S), in_=col(C_AM))
            e.tensor_scalar(out=col(C_S), in0=col(C_S), scalar1=127.0,
                            scalar2=None, op0=ALU.mult)
            e.tensor_tensor(out=col(C_C), in0=col(C_RLN), in1=col(C_RRMS),
                            op=ALU.mult)
            e.tensor_tensor(out=col(C_C), in0=col(C_C), in1=col(C_S),
                            op=ALU.mult)
            e.tensor_tensor(out=col(C_MC), in0=col(C_M), in1=col(C_C),
                            op=ALU.mult)
            e.tensor_scalar(out=col(C_MC), in0=col(C_MC), scalar1=-1.0,
                            scalar2=None, op0=ALU.mult)
            e.tensor_tensor(out=col(C_SV), in0=col(C_S), in1=kb.swb[:, 0:1],
                            op=ALU.mult)
            e.reciprocal(out=col(C_INV), in_=col(C_SV))
            e.tensor_copy(out=kb.bfscr[:, 0:1], in_=col(C_SV))
            e.tensor_copy(out=col(C_SVH), in_=kb.bfscr[:, 0:1])
            e.tensor_tensor(out=col(C_SVL), in0=col(C_SV), in1=col(C_SVH),
                            op=ALU.subtract)
            e.tensor_copy(out=stq1[:, 0:1], in_=col(C_SVH))
            e.tensor_copy(out=stq1[:, 1:2], in_=col(C_SVH))
            return e.tensor_copy(out=stq1[:, 2:3], in_=col(C_SVL))
        s.task("dve", stats2, reads=(f"{tag}st{p}", "swb"),
               writes=(f"{tag}st{p}", f"{tag}stq1", "bfscr"),
               name=f"{t}.stats2", stage=1, chunk=c)
        def qa(e, p=p, x3=x3):
            return e.tensor_scalar(out=xt[:], in0=Xs[x3][:],
                                   scalar1=st[p][:, C_C:C_C + 1],
                                   scalar2=st[p][:, C_MC:C_MC + 1],
                                   op0=ALU.mult, op1=ALU.add)
        s.task("pool", qa, reads=(f"{tag}Xs{x3}", f"{tag}st{p}"),
               writes=(f"{tag}xt",), name=f"{t}.qa", stage=2, chunk=c)
        def qb(e):
            return e.tensor_scalar(out=xq[:], in0=xt[:], scalar1=MAGIC,
                                   scalar2=-MAGIC, op0=ALU.add, op1=ALU.add)
        s.task("dve", qb, reads=(f"{tag}xt",), writes=(f"{tag}xq",),
               name=f"{t}.qb", stage=2, chunk=c)
        def txq(e):
            last = None
            for k in range(8):
                last = e.transpose(out=pb[6][:, k * 128:(k + 1) * 128],
                                   in_=xq[:, k * 128:(k + 1) * 128],
                                   identity=kb.identb[:])
            return last
        s.task("pe", txq, reads=(f"{tag}xq", "identb"), writes=("pb6",),
               name=f"{t}.txq", stage=2, chunk=c)
        def evxqT(e):
            e.copy(out=xqT[:, 0:512], in_=pb[6][:, 0:512])
            return e.copy(out=xqT[:, 512:1024], in_=pb[6][:, 512:1024])
        s.task("act", evxqT, reads=("pb6", "csb", "csbb"),
               writes=(f"{tag}xqT",),
               name=f"{t}.evxqT", stage=2, chunk=c)
        def mint(e):
            return e.transpose(out=pb[6][0:4, 0:128], in_=stq1[:],
                               identity=kb.identb[:])
        s.task("pe", mint, reads=(f"{tag}stq1", "identb"), writes=("pb6",),
               name=f"{t}.minT", stage=2, chunk=c)
        def evrow(e, p=p):
            return e.tensor_copy(out=strow1[p][:], in_=pb[6][0:4, 0:128])
        s.task("dve", evrow, reads=("pb6",), writes=(f"{tag}strow1{p}",),
               name=f"{t}.evrow", stage=2, chunk=c)
        for q in range(4):
            bA, bB = (0, 1) if q % 2 == 0 else (2, 3)
            def mm1(e, p=p, q=q, bA=bA, bB=bB):
                last = None
                for k in range(8):
                    lt = xqT[:, k * 128:(k + 1) * 128]
                    last = e.matmul(out=pb[bA][:], lhsT=lt,
                                    rhs=kb.wq1s[:, k, q * 1024:q * 1024 + 512],
                                    start=(k == 0), stop=False,
                                    skip_group_check=True)
                    last = e.matmul(
                        out=pb[bB][:], lhsT=lt,
                        rhs=kb.wq1s[:, k, q * 1024 + 512:(q + 1) * 1024],
                        start=(k == 0), stop=False, skip_group_check=True)
                last = e.matmul(out=pb[bA][:], lhsT=strow1[p][0:3, :],
                                rhs=kb.b1x[0:3, q * 1024:q * 1024 + 512],
                                start=False, stop=True, skip_group_check=True)
                last = e.matmul(out=pb[bB][:], lhsT=strow1[p][0:3, :],
                                rhs=kb.b1x[0:3, q * 1024 + 512:(q + 1) * 1024],
                                start=False, stop=True, skip_group_check=True)
                return last
            s.task("pe", mm1, reads=(f"{tag}xqT", f"{tag}strow1{p}", "WA",
                                     "bx_b1"),
                   writes=(f"pb{bA}", f"pb{bB}"), name=f"{t}.mm1q{q}",
                   stage=3, chunk=c)
            def gel(e, p=p, q=q, bA=bA, bB=bB):
                e.activation(out=h[:, q * 1024:q * 1024 + 512], in_=pb[bA][:],
                             func=AF.Gelu, scale=st[p][:, C_INV:C_INV + 1])
                return e.activation(out=h[:, q * 1024 + 512:(q + 1) * 1024],
                                    in_=pb[bB][:], func=AF.Gelu,
                                    scale=st[p][:, C_INV:C_INV + 1])
            s.task("act", gel, reads=(f"pb{bA}", f"pb{bB}", f"{tag}st{p}",
                                      "csb"),
                   writes=(f"{tag}h",), name=f"{t}.gelu{q}", stage=3, chunk=c)
            def hsq_q(e, p=p, q=q):
                e.activation(out=ht[:, 0:512],
                             in_=h[:, q * 1024:q * 1024 + 512],
                             func=AF.Square,
                             accum_out=st2[p][:, 2 * q:2 * q + 1])
                return e.activation(out=ht[:, 512:1024],
                                    in_=h[:, q * 1024 + 512:(q + 1) * 1024],
                                    func=AF.Square,
                                    accum_out=st2[p][:, 2 * q + 1:2 * q + 2])
            s.task("act", hsq_q, reads=(f"{tag}h", "csb"),
                   writes=("stage", f"{tag}st2{p}"), name=f"{t}.hsq{q}",
                   stage=3, chunk=c)
            def ham_q(e, p=p, q=q):
                return e.reduce_max(out=st2[p][:, 21 + q:22 + q],
                                    in_=h[:, q * 1024:(q + 1) * 1024],
                                    axis=AX.X, apply_absolute_value=True)
            s.task("dve", ham_q, reads=(f"{tag}h",), writes=(f"{tag}st2{p}",),
                   name=f"{t}.ham{q}", stage=3, chunk=c)
        def hstats(e, p=p):
            e = VG(nc)
            stp = st2[p]
            col = lambda i, w=1: stp[:, i:i + w]
            e.reduce_sum(out=col(Hs_SUM), in_=stp[:, 0:8], axis=AX.X)
            e.reduce_max(out=col(Hs_AM), in_=stp[:, 21:25], axis=AX.X)
            e.tensor_scalar(out=col(Hs_ARG), in0=col(Hs_SUM),
                            scalar1=1.0 / IN, scalar2=1e-6, op0=ALU.mult,
                            op1=ALU.add)
            emit_rsqrt(nc, col(Hs_ARG), col(Hs_RR), col(Hs_X0), col(Hs_X1),
                       col(Hs_X2), e=e)
            e.tensor_tensor(out=col(Hs_AM), in0=col(Hs_AM), in1=col(Hs_RR),
                            op=ALU.mult)
            e.tensor_scalar(out=col(Hs_AM), in0=col(Hs_AM), scalar1=1e-5,
                            scalar2=None, op0=ALU.max)
            e.reciprocal(out=col(Hs_S), in_=col(Hs_AM))
            e.tensor_scalar(out=col(Hs_S), in0=col(Hs_S), scalar1=127.0,
                            scalar2=None, op0=ALU.mult)
            e.tensor_tensor(out=col(Hs_C), in0=col(Hs_RR), in1=col(Hs_S),
                            op=ALU.mult)
            e.tensor_tensor(out=col(Hs_SV), in0=col(Hs_S), in1=kb.swb[:, 1:2],
                            op=ALU.mult)
            e.reciprocal(out=col(Hs_INV), in_=col(Hs_SV))
            e.tensor_scalar(out=col(Hs_INV), in0=col(Hs_INV), scalar1=0.5,
                            scalar2=None, op0=ALU.mult)
            e.tensor_copy(out=kb.bfscr[:, 1:2], in_=col(Hs_SV))
            e.tensor_copy(out=col(Hs_SVH), in_=kb.bfscr[:, 1:2])
            e.tensor_tensor(out=col(Hs_SVL), in0=col(Hs_SV), in1=col(Hs_SVH),
                            op=ALU.subtract)
            e.tensor_copy(out=stq2[:, 0:1], in_=col(Hs_SVH))
            e.tensor_copy(out=stq2[:, 1:2], in_=col(Hs_SVH))
            return e.tensor_copy(out=stq2[:, 2:3], in_=col(Hs_SVL))
        s.task("dve", hstats, reads=(f"{tag}st2{p}", "swb"),
               writes=(f"{tag}st2{p}", f"{tag}stq2", "bfscr"),
               name=f"{t}.hstats", stage=4, chunk=c)
        def mint2(e):
            return e.transpose(out=pb[6][0:4, 128:256], in_=stq2[:],
                               identity=kb.identb[:])
        s.task("pe", mint2, reads=(f"{tag}stq2", "identb"), writes=("pb6",),
               name=f"{t}.minT2", stage=4, chunk=c)
        def evrow2(e, p=p):
            return e.tensor_copy(out=strow2[p][:], in_=pb[6][0:4, 128:256])
        s.task("dve", evrow2, reads=("pb6",), writes=(f"{tag}strow2{p}",),
               name=f"{t}.evrow2", stage=4, chunk=c)
        for sl in range(4):
            htb = ht if sl % 2 == 0 else xt
            htres = "stage" if sl % 2 == 0 else f"{tag}xt"
            def hqa(e, p=p, sl=sl, htb=htb):
                return e.tensor_scalar(out=htb[:],
                                       in0=h[:, sl * 1024:(sl + 1) * 1024],
                                       scalar1=st2[p][:, Hs_C:Hs_C + 1],
                                       scalar2=MAGIC, op0=ALU.mult,
                                       op1=ALU.add)
            s.task("pool", hqa, reads=(f"{tag}h", f"{tag}st2{p}"),
                   writes=(htres,), name=f"{t}.hqa{sl}", stage=4, chunk=c)
            def hqb(e, htb=htb):
                return e.tensor_scalar(out=hq[:], in0=htb[:], scalar1=-MAGIC,
                                       scalar2=127.0, op0=ALU.add,
                                       op1=ALU.min)
            s.task("dve", hqb, reads=(htres,), writes=(f"{tag}hq",),
                   name=f"{t}.hqb{sl}", stage=4, chunk=c)
            def thq(e):
                last = None
                for k in range(8):
                    last = e.transpose(out=pb[7][:, k * 128:(k + 1) * 128],
                                       in_=hq[:, k * 128:(k + 1) * 128],
                                       identity=kb.identb[:])
                return last
            s.task("pe", thq, reads=(f"{tag}hq", "identb"), writes=("pb7",),
                   name=f"{t}.thq{sl}", stage=4, chunk=c)
            def evhqT(e, sl=sl):
                e.copy(out=hqT[:, sl * 1024:sl * 1024 + 512],
                       in_=pb[7][:, 0:512])
                return e.copy(
                    out=hqT[:, sl * 1024 + 512:(sl + 1) * 1024],
                    in_=pb[7][:, 512:1024])
            s.task("act", evhqT, reads=("pb7", "csb", "csbb"),
                   writes=(f"{tag}hqT",),
                   name=f"{t}.evhqT{sl}", stage=4, chunk=c)
        def mm2(e, p=p):
            last = None
            for k in range(32):
                lt = hqT[:, k * 128:(k + 1) * 128]
                last = e.matmul(out=pb[4][:], lhsT=lt,
                                rhs=kb.wq2s[:, k, 0:512], start=(k == 0),
                                stop=False, skip_group_check=True)
                last = e.matmul(out=pb[5][:], lhsT=lt,
                                rhs=kb.wq2s[:, k, 512:1024], start=(k == 0),
                                stop=False, skip_group_check=True)
            last = e.matmul(out=pb[4][:], lhsT=strow2[p][0:3, :],
                            rhs=kb.b2x[0:3, 0:512], start=False, stop=True,
                            skip_group_check=True)
            last = e.matmul(out=pb[5][:], lhsT=strow2[p][0:3, :],
                            rhs=kb.b2x[0:3, 512:1024], start=False, stop=True,
                            skip_group_check=True)
            return last
        s.task("pe", mm2, reads=(f"{tag}hqT", f"{tag}strow2{p}", "WB",
                                 "bx_b2"),
               writes=("pb4", "pb5"), name=f"{t}.mm2", stage=5, chunk=c)
        s.task("qg", lambda e, c=c: e.dma_start(
            out=Xr[:], in_=xr[c * CH:(c + 1) * CH, :]),
            reads=(rsrc,), writes=(f"{tag}Xr",), name=f"{t}.ldXr",
            stage=5, chunk=c)
        def x1ev(e, p=p):
            e.scalar_tensor_tensor(out=X1o[:, 0:512], in0=pb[4][:],
                                   scalar=st2[p][:, Hs_INV:Hs_INV + 1],
                                   in1=Xr[:, 0:512], op0=ALU.mult,
                                   op1=ALU.add)
            return e.scalar_tensor_tensor(out=X1o[:, 512:1024], in0=pb[5][:],
                                          scalar=st2[p][:, Hs_INV:Hs_INV + 1],
                                          in1=Xr[:, 512:1024],
                                          op0=ALU.mult, op1=ALU.add)
        s.task("dve", x1ev, reads=("pb4", "pb5", f"{tag}st2{p}",
                                   f"{tag}Xr"),
               writes=(f"{tag}X1o",), name=f"{t}.x1ev", stage=6, chunk=c)
        s.task("qa", lambda e, c=c: e.dma_start(
            out=dstr[c * CH:(c + 1) * CH, :], in_=X1o[:]),
            reads=(f"{tag}X1o",), writes=(f"d{dst}.{c}",), name=f"{t}.st",
            stage=6, chunk=c)
    s.flush_stages()


def phase_a1(kb):
    nc, s, dr = kb.nc, kb.s, kb.dram
    SC = kb.SC
    SC.reset(kb.p1_sc_base)
    s.fence()
    pb = kb.pb

    kb.WB.reset()
    wqv = kb.WB.view([128, 8, D], BF16)
    wkv = kb.WB.view([128, 8, D], BF16)
    wvv = kb.WB.view([128, 8, D], BF16)
    wov = kb.WB.view([128, 8, D], BF16)
    kb.WA.reset()
    wstage = kb.WA.view([128, 8, D], F32)
    for wname, dstv, normcol in (("wq", wqv, kb.anw), ("wk", wkv, kb.anw),
                                 ("wv", wvv, kb.anw), ("wo", wov, kb.onw)):
        wr = dr[wname].rearrange("(c p) n -> p c n", p=128)
        s.task("qs", lambda e, wr=wr: e.dma_start(out=wstage[:], in_=wr),
               writes=("WA",), name=f"a1_ld_{wname}", nofence=True)
        def fold(e, dstv=dstv, ncol=normcol):
            last = None
            for k in range(8):
                last = e.tensor_scalar(out=dstv[:, k, :], in0=wstage[:, k, :],
                                       scalar1=ncol[:, k:k + 1], scalar2=None,
                                       op0=ALU.mult)
            return last
        s.task("dve", fold, reads=("WA", "anw", "onw"), writes=("WB",),
               name=f"a1_fold_{wname}", nofence=True)

    X1s = [SC.view([128, D], F32) for _ in range(2)]
    X1r = SC.view([128, D], F32)
    tb = SC.view([128, D], BF16)
    tT = [SC.view([128, D], BF16) for _ in range(2)]
    Kb = [SC.view([128, D], BF16) for _ in range(2)]
    Vb = [SC.view([128, D], BF16) for _ in range(3)]
    Qb = SC.view([128, D], BF16)
    qT = [SC.view([128, D], BF16) for _ in range(2)]
    kT = SC.view([128, D], BF16)
    Ssb = SC.view([128, D], BF16)
    atnT = SC.view([128, D], BF16)
    o2f = kb.stage_view
    o_sb = SC.view([128, D], BF16)
    dumm = o2f
    on = SC.view([128, D], BF16)
    onT = SC.view([128, D], BF16)
    y1 = SC.view([128, D], F32)
    s1 = SC.view([128, D], F32)
    st = [SC.view([128, 24], F32) for _ in range(2)]
    stq = [SC.view([128, 4], BF16) for _ in range(8)]
    late = [SC.view([128, 1], F32) for _ in range(8)]
    strow = [SC.view([4, 128], BF16) for _ in range(2)]
    st2 = [SC.view([128, 48], F32) for _ in range(2)]

    C_SUM, C_SQ, C_M, C_V, C_A, C_ARG, C_RLN, C_RRMS = 0, 1, 2, 3, 4, 5, 6, 7
    C_CC, C_MC, C_COR, C_CH, C_CL = 8, 9, 10, 11, 12
    C_X0, C_X1, C_X2 = 14, 16, 18
    O_SS, O_ARG, O_RS, O_X0, O_X1, O_X2 = 0, 8, 16, 24, 32, 40

    x1r, y1r, s1r = dr["X1"], dr["Y1"], dr["S1"]

    s.begin_stages()
    for c in range(NS):
        p = c % 2
        x3 = c % 2
        x8 = c % 8
        x4 = c % 3
        t = f"A1.{c}"
        active = c >= ACT0
        j = c - ACT0
        s.task("qs", lambda e, c=c, x3=x3: e.dma_start(
            out=X1s[x3][:], in_=x1r[c * CH:(c + 1) * CH, :]),
            reads=(f"dX1.{c}",), writes=(f"AXs{x3}",), name=f"{t}.ld",
            stage=0, chunk=c)
        def stats1(e, p=p, x3=x3):
            e.reduce_sum(out=st[p][:, C_SUM:C_SUM + 1], in_=X1s[x3][:],
                         axis=AX.X)
            return e.scalar_tensor_tensor(out=dumm[:], in0=X1s[x3][:],
                                          scalar=1.0, in1=X1s[x3][:],
                                          op0=ALU.mult, op1=ALU.mult,
                                          accum_out=st[p][:, C_SQ:C_SQ + 1])
        s.task("dve", stats1, reads=(f"AXs{x3}",), writes=(f"Ast{p}", "stage"),
               name=f"{t}.stats1", stage=1, chunk=c)
        def stats2(e, p=p, active=active, x8=x8):
            e = VG(nc)
            stp = st[p]
            col = lambda i, w=1: stp[:, i:i + w]
            e.tensor_scalar(out=col(C_M), in0=col(C_SUM), scalar1=1.0 / D,
                            scalar2=None, op0=ALU.mult)
            e.tensor_scalar(out=col(C_V), in0=col(C_SQ), scalar1=1.0 / D,
                            scalar2=None, op0=ALU.mult)
            e.tensor_tensor(out=col(C_A), in0=col(C_M), in1=col(C_M),
                            op=ALU.mult)
            e.tensor_tensor(out=col(C_V), in0=col(C_V), in1=col(C_A),
                            op=ALU.subtract)
            e.tensor_scalar(out=col(C_A), in0=col(C_V), scalar1=1e-5,
                            scalar2=None, op0=ALU.add)
            e.reciprocal(out=col(C_ARG), in_=col(C_A))
            e.tensor_tensor(out=col(C_ARG), in0=col(C_V), in1=col(C_ARG),
                            op=ALU.mult)
            e.tensor_scalar(out=col(C_ARG), in0=col(C_ARG), scalar1=1e-5,
                            scalar2=None, op0=ALU.add)
            emit_rsqrt(nc, stp[:, C_A:C_A + 2], stp[:, C_RLN:C_RLN + 2],
                       stp[:, C_X0:C_X0 + 2], stp[:, C_X1:C_X1 + 2],
                       stp[:, C_X2:C_X2 + 2], e=e)
            e.tensor_tensor(out=col(C_CC), in0=col(C_RLN), in1=col(C_RRMS),
                            op=ALU.mult)
            e.tensor_tensor(out=col(C_MC), in0=col(C_M), in1=col(C_CC),
                            op=ALU.mult)
            last = e.tensor_scalar(out=col(C_MC), in0=col(C_MC), scalar1=-1.0,
                                   scalar2=None, op0=ALU.mult)
            if active:
                e.tensor_copy(out=late[x8][:], in_=col(C_RLN))
                e.tensor_tensor(out=col(C_COR), in0=col(C_M), in1=col(C_RLN),
                                op=ALU.mult)
                e.tensor_scalar(out=col(C_COR), in0=col(C_COR), scalar1=-1.0,
                                scalar2=None, op0=ALU.mult)
                e.tensor_copy(out=kb.bfscr[:, 2:3], in_=col(C_COR))
                e.tensor_copy(out=col(C_CH), in_=kb.bfscr[:, 2:3])
                e.tensor_tensor(out=col(C_CL), in0=col(C_COR), in1=col(C_CH),
                                op=ALU.subtract)
                e.tensor_copy(out=stq[x8][:, 0:1], in_=col(C_CH))
                last = e.tensor_copy(out=stq[x8][:, 1:2], in_=col(C_CL))
            return last
        s.task("dve", stats2, reads=(f"Ast{p}",),
               writes=(f"Ast{p}", f"Astq{x8}", f"Alate{x8}", "bfscr"),
               name=f"{t}.stats2", stage=1, chunk=c)
        def tq(e, p=p, x3=x3):
            return e.tensor_scalar(out=tb[:], in0=X1s[x3][:],
                                   scalar1=st[p][:, C_CC:C_CC + 1],
                                   scalar2=st[p][:, C_MC:C_MC + 1],
                                   op0=ALU.mult, op1=ALU.add)
        s.task("pool", tq, reads=(f"AXs{x3}", f"Ast{p}"), writes=("Atb",),
               name=f"{t}.tq", stage=2, chunk=c)
        def ttq(e):
            last = None
            for k in range(8):
                last = e.transpose(out=pb[6][:, k * 128:(k + 1) * 128],
                                   in_=tb[:, k * 128:(k + 1) * 128],
                                   identity=kb.identb[:])
            return last
        s.task("pe", ttq, reads=("Atb", "identb"), writes=("pb6",),
               name=f"{t}.tT", stage=2, chunk=c)
        def evtT(e, p=p):
            e.copy(out=tT[p][:, 0:512], in_=pb[6][:, 0:512])
            return e.copy(out=tT[p][:, 512:1024],
                          in_=pb[6][:, 512:1024])
        s.task("act", evtT, reads=("pb6", "csb", "csbb"),
               writes=(f"AtT{p}",),
               name=f"{t}.evtT", stage=2, chunk=c)
        def kmm(e, p=p):
            last = None
            for k in range(8):
                lt = tT[p][:, k * 128:(k + 1) * 128]
                last = e.matmul(out=pb[0][:], lhsT=lt, rhs=wkv[:, k, 0:512],
                                start=(k == 0), stop=(k == 7),
                                skip_group_check=True)
                last = e.matmul(out=pb[1][:], lhsT=lt, rhs=wkv[:, k, 512:1024],
                                start=(k == 0), stop=(k == 7),
                                skip_group_check=True)
            return last
        s.task("pe", kmm, reads=(f"AtT{p}", "WB"), writes=("pb0", "pb1"),
               name=f"{t}.kmm", stage=3, chunk=c)
        def evK(e, p=p, c=c):
            e.tensor_scalar(out=Kb[p][:, 0:512], in0=pb[0][:],
                            scalar1=kb.maskc[:, c:c + 1], scalar2=None,
                            op0=ALU.mult)
            return e.tensor_scalar(out=Kb[p][:, 512:1024], in0=pb[1][:],
                                   scalar1=kb.maskc[:, c:c + 1], scalar2=None,
                                   op0=ALU.mult)
        s.task("dve", evK, reads=("pb0", "pb1", "maskc"), writes=(f"AKb{p}",),
               name=f"{t}.evK", stage=3, chunk=c)
        def vmm(e, p=p):
            last = None
            for k in range(8):
                lt = tT[p][:, k * 128:(k + 1) * 128]
                last = e.matmul(out=pb[2][:], lhsT=lt, rhs=wvv[:, k, 0:512],
                                start=(k == 0), stop=(k == 7),
                                skip_group_check=True)
                last = e.matmul(out=pb[3][:], lhsT=lt, rhs=wvv[:, k, 512:1024],
                                start=(k == 0), stop=(k == 7),
                                skip_group_check=True)
            return last
        s.task("pe", vmm, reads=(f"AtT{p}", "WB"), writes=("pb2", "pb3"),
               name=f"{t}.vmm", stage=3, chunk=c)
        def evV(e, x4=x4):
            e.copy(out=Vb[x4][:, 0:512], in_=pb[2][:])
            return e.copy(out=Vb[x4][:, 512:1024], in_=pb[3][:])
        s.task("act", evV, reads=("pb2", "pb3", "csb", "csbb"),
               writes=(f"AVb{x4}",), name=f"{t}.evV", stage=3, chunk=c)

        if active:
            def qmm(e, p=p):
                last = None
                for k in range(8):
                    lt = tT[p][:, k * 128:(k + 1) * 128]
                    last = e.matmul(out=pb[0][:], lhsT=lt,
                                    rhs=wqv[:, k, 0:512],
                                    start=(k == 0), stop=(k == 7),
                                    skip_group_check=True)
                    last = e.matmul(out=pb[1][:], lhsT=lt,
                                    rhs=wqv[:, k, 512:1024], start=(k == 0),
                                    stop=(k == 7), skip_group_check=True)
                return last
            s.task("pe", qmm, reads=(f"AtT{p}", "WB"), writes=("pb0", "pb1"),
                   name=f"{t}.qmm", stage=4, chunk=c)
            def evQ(e):
                e.copy(out=Qb[:, 0:512], in_=pb[0][:])
                return e.copy(out=Qb[:, 512:1024], in_=pb[1][:])
            s.task("act", evQ, reads=("pb0", "pb1", "csb", "csbb"),
                   writes=("AQb",), name=f"{t}.evQ", stage=4, chunk=c)
            def tq8(e):
                last = None
                for k in range(8):
                    last = e.transpose(out=pb[6][:, k * 128:(k + 1) * 128],
                                       in_=Qb[:, k * 128:(k + 1) * 128],
                                       identity=kb.identb[:])
                return last
            s.task("pe", tq8, reads=("AQb", "identb"), writes=("pb6",),
                   name=f"{t}.tq8", stage=4, chunk=c)
            def evqT(e, p=p):
                e.copy(out=qT[p][:, 0:512], in_=pb[6][:, 0:512])
                return e.copy(out=qT[p][:, 512:1024],
                              in_=pb[6][:, 512:1024])
            s.task("act", evqT, reads=("pb6", "csb", "csbb"),
                   writes=(f"AqT{p}",),
                   name=f"{t}.evqT", stage=4, chunk=c)
            def tk8(e, p=p):
                last = None
                for k in range(8):
                    last = e.transpose(out=pb[7][:, k * 128:(k + 1) * 128],
                                       in_=Kb[p][:, k * 128:(k + 1) * 128],
                                       identity=kb.identb[:])
                return last
            s.task("pe", tk8, reads=(f"AKb{p}", "identb"), writes=("pb7",),
                   name=f"{t}.tk8", stage=4, chunk=c)
            def evkT(e):
                e.copy(out=kT[:, 0:512], in_=pb[7][:, 0:512])
                return e.copy(out=kT[:, 512:1024],
                              in_=pb[7][:, 512:1024])
            s.task("act", evkT, reads=("pb7", "csb", "csbb"),
                   writes=("AkT",),
                   name=f"{t}.evkT", stage=4, chunk=c)
            def scopy(e):
                e.tensor_copy(out=Ssb[:, 0:512], in_=pb[4][:])
                return e.tensor_copy(out=Ssb[:, 512:1024], in_=pb[5][:])
            s.task("dve", scopy, reads=("pb4", "pb5"), writes=("ASsb",),
                   name=f"{t}.scopy", stage=5, chunk=c)
            def smm(e, p=p):
                last = None
                for hh in range(H):
                    bank = 0 if hh < 4 else 1
                    off = (hh % 4) * 128
                    last = e.matmul(out=pb[bank][:, off:off + 128],
                                    lhsT=kT[:, hh * 128:(hh + 1) * 128],
                                    rhs=qT[p][:, hh * 128:(hh + 1) * 128],
                                    start=True, stop=True,
                                    skip_group_check=True)
                return last
            s.task("pe", smm, reads=("AkT", f"AqT{p}"), writes=("pb0", "pb1"),
                   name=f"{t}.smm", stage=5, chunk=c)
            def mskev(e):
                last = None
                for hh in range(H):
                    bank = 0 if hh < 4 else 1
                    off = (hh % 4) * 128
                    last = e.tensor_tensor(
                        out=atnT[:, hh * 128:(hh + 1) * 128],
                        in0=pb[bank][:, off:off + 128], in1=kb.trimaskT[:],
                        op=ALU.mult)
                return last
            s.task("dve", mskev, reads=("pb0", "pb1", "trimaskT"),
                   writes=("AatnT",), name=f"{t}.mskev", stage=5, chunk=c)
            def omm(e, p=p, x4=x4):
                last = None
                for hh in range(H):
                    bank = 2 if hh < 4 else 3
                    off = (hh % 4) * 128
                    last = e.matmul(out=pb[bank][:, off:off + 128],
                                    lhsT=atnT[:, hh * 128:(hh + 1) * 128],
                                    rhs=Vb[x4][:, hh * 128:(hh + 1) * 128],
                                    start=True, stop=False,
                                    skip_group_check=True)
                    last = e.matmul(out=pb[bank][:, off:off + 128],
                                    lhsT=qT[p][:, hh * 128:(hh + 1) * 128],
                                    rhs=Ssb[:, hh * 128:(hh + 1) * 128],
                                    start=False, stop=True,
                                    skip_group_check=True)
                return last
            s.task("pe", omm, reads=("AatnT", f"AqT{p}", f"AVb{x4}", "ASsb"),
                   writes=("pb2", "pb3"), name=f"{t}.omm", stage=5, chunk=c)
            def oev(e):
                e.copy(out=o_sb[:, 0:512], in_=pb[2][:])
                return e.copy(out=o_sb[:, 512:1024], in_=pb[3][:])
            s.task("act", oev, reads=("pb2", "pb3", "csb", "csbb"),
                   writes=("Aosb",), name=f"{t}.oev", stage=5, chunk=c)
            def osq(e):
                return e.scalar_tensor_tensor(out=o2f[:], in0=o_sb[:],
                                              scalar=1.0, in1=o_sb[:],
                                              op0=ALU.mult, op1=ALU.mult)
            s.task("dve", osq, reads=("Aosb",), writes=("stage",),
                   name=f"{t}.osq", stage=6, chunk=c)
            def ored(e, p=p):
                return e.tensor_reduce(
                    out=st2[p][:, O_SS:O_SS + 8].rearrange(
                        "p (g o) -> p g o", g=8),
                    in_=o2f[:].rearrange("p (g f) -> p g f", g=8),
                    axis=AX.X, op=ALU.add)
            s.task("dve", ored, reads=("stage",), writes=(f"Ast2{p}",),
                   name=f"{t}.ored", stage=6, chunk=c)
            def orstat(e, p=p):
                e = VG(nc)
                e.tensor_scalar(out=st2[p][:, O_ARG:O_ARG + 8],
                                in0=st2[p][:, O_SS:O_SS + 8],
                                scalar1=1.0 / 128, scalar2=1e-5,
                                op0=ALU.mult, op1=ALU.add)
                return emit_rsqrt(nc, st2[p][:, O_ARG:O_ARG + 8],
                                  st2[p][:, O_RS:O_RS + 8],
                                  st2[p][:, O_X0:O_X0 + 8],
                                  st2[p][:, O_X1:O_X1 + 8],
                                  st2[p][:, O_X2:O_X2 + 8], e=e)
            s.task("dve", orstat, reads=(f"Ast2{p}",), writes=(f"Ast2{p}",),
                   name=f"{t}.orstat", stage=6, chunk=c)
            def oscale(e, p=p):
                last = None
                for hh in range(H):
                    last = e.tensor_scalar(
                        out=on[:, hh * 128:(hh + 1) * 128],
                        in0=o_sb[:, hh * 128:(hh + 1) * 128],
                        scalar1=st2[p][:, O_RS + hh:O_RS + hh + 1],
                        scalar2=None, op0=ALU.mult)
                return last
            s.task("dve", oscale, reads=("Aosb", f"Ast2{p}"),
                   writes=("Aon",), name=f"{t}.oscale", stage=6, chunk=c)
            def ton(e):
                last = None
                for k in range(8):
                    last = e.transpose(out=pb[6][:, k * 128:(k + 1) * 128],
                                       in_=on[:, k * 128:(k + 1) * 128],
                                       identity=kb.identb[:])
                return last
            s.task("pe", ton, reads=("Aon", "identb"), writes=("pb6",),
                   name=f"{t}.ton", stage=6, chunk=c)
            def evonT(e):
                e.copy(out=onT[:, 0:512], in_=pb[6][:, 0:512])
                return e.copy(out=onT[:, 512:1024],
                              in_=pb[6][:, 512:1024])
            s.task("act", evonT, reads=("pb6", "csb", "csbb"),
                   writes=("AonT",),
                   name=f"{t}.evonT", stage=6, chunk=c)
            def mint(e, x8=x8):
                return e.transpose(out=pb[7][0:4, 0:128], in_=stq[x8][:],
                                   identity=kb.identb[:])
            s.task("pe", mint, reads=(f"Astq{x8}", "identb"), writes=("pb7",),
                   name=f"{t}.minT", stage=6, chunk=c)
            def evrow(e, p=p):
                return e.tensor_copy(out=strow[p][:], in_=pb[7][0:4, 0:128])
            s.task("dve", evrow, reads=("pb7",), writes=(f"Astrow{p}",),
                   name=f"{t}.evrow", stage=6, chunk=c)
            def womm(e, p=p):
                last = None
                for k in range(8):
                    lt = onT[:, k * 128:(k + 1) * 128]
                    last = e.matmul(out=pb[0][:], lhsT=lt,
                                    rhs=wov[:, k, 0:512],
                                    start=(k == 0), stop=False,
                                    skip_group_check=True)
                    last = e.matmul(out=pb[1][:], lhsT=lt,
                                    rhs=wov[:, k, 512:1024], start=(k == 0),
                                    stop=False, skip_group_check=True)
                last = e.matmul(out=pb[0][:], lhsT=strow[p][0:2, :],
                                rhs=kb.ones2[0:2, 0:512], start=False,
                                stop=True, skip_group_check=True)
                last = e.matmul(out=pb[1][:], lhsT=strow[p][0:2, :],
                                rhs=kb.ones2[0:2, 512:1024], start=False,
                                stop=True, skip_group_check=True)
                return last
            s.task("pe", womm, reads=("AonT", f"Astrow{p}", "WB", "ones2"),
                   writes=("pb0", "pb1"), name=f"{t}.womm", stage=6, chunk=c)
            s.task("qg", lambda e, c=c: e.dma_start(
                out=X1r[:], in_=x1r[c * CH:(c + 1) * CH, :]),
                reads=(f"dX1.{c}",), writes=("AX1r",), name=f"{t}.ldX1r",
                stage=6, chunk=c)
            def y1ev(e, x8=x8):
                e.scalar_tensor_tensor(out=y1[:, 0:512],
                                       in0=X1r[:, 0:512],
                                       scalar=late[x8][:],
                                       in1=pb[0][:], op0=ALU.mult,
                                       op1=ALU.add)
                return e.scalar_tensor_tensor(out=y1[:, 512:1024],
                                              in0=X1r[:, 512:1024],
                                              scalar=late[x8][:],
                                              in1=pb[1][:], op0=ALU.mult,
                                              op1=ALU.add)
            s.task("dve", y1ev, reads=("pb0", "pb1", f"Alate{x8}", "AX1r"),
                   writes=("Ay1",), name=f"{t}.y1ev", stage=6, chunk=c)
            def s1ev(e):
                return e.tensor_tensor(out=s1[:], in0=X1r[:], in1=y1[:],
                                       op=ALU.add)
            s.task("dve", s1ev, reads=("Ay1", "AX1r"), writes=("As1",),
                   name=f"{t}.s1ev", stage=6, chunk=c)
            s.task("qa", lambda e, j=j: e.dma_start(
                out=y1r[j * CH:(j + 1) * CH, :], in_=y1[:]),
                reads=("Ay1",), writes=(f"dY1.{j}",), name=f"{t}.stY1",
                stage=6, chunk=c)
            s.task("qa", lambda e, j=j: e.dma_start(
                out=s1r[j * CH:(j + 1) * CH, :], in_=s1[:]),
                reads=("As1",), writes=(f"dS1.{j}",), name=f"{t}.stS1",
                stage=6, chunk=c)

        if kb.debug == "A1":
            dd = kb.dram
            if c == 0:
                s.task("qg", lambda e, p=p: e.dma_start(out=dd["dbg_K0"],
                                                        in_=Kb[p][:]),
                       reads=(f"AKb{p}",), writes=("ddbg",), name=f"{t}.dK0",
                       stage=4, chunk=c)
                s.task("qg", lambda e, p=p: e.dma_start(out=dd["dbg_V0"],
                                                        in_=Vb[p][:]),
                       reads=(f"AVb{p}",), writes=("ddbg",), name=f"{t}.dV0",
                       stage=4, chunk=c)
                s.task("qg", lambda e: e.dma_start(out=dd["dbg_t0"],
                                                   in_=tb[:]),
                       reads=("Atb",), writes=("ddbg",), name=f"{t}.dt0",
                       stage=2, chunk=c)
            if c == ACT0:
                s.task("qg", lambda e: e.dma_start(out=dd["dbg_S32"],
                                                   in_=Ssb[:]),
                       reads=("ASsb",), writes=("ddbg",), name=f"{t}.dS32",
                       stage=5, chunk=c)
                s.task("qg", lambda e, p=p: e.dma_start(out=dd["dbg_q32"],
                                                        in_=qT[p][:]),
                       reads=(f"AqT{p}",), writes=("ddbg",), name=f"{t}.dq32",
                       stage=5, chunk=c)
                s.task("qg", lambda e: e.dma_start(out=dd["dbg_at32"],
                                                   in_=atnT[:]),
                       reads=("AatnT",), writes=("ddbg",), name=f"{t}.dat32",
                       stage=5, chunk=c)
                s.task("qg", lambda e: e.dma_start(out=dd["dbg_on32"],
                                                   in_=on[:]),
                       reads=("Aon",), writes=("ddbg",), name=f"{t}.don32",
                       stage=6, chunk=c)
        def supd(e, p=p, c=c, x4=x4):
            last = None
            for hh in range(H):
                bank = 4 if hh < 4 else 5
                off = (hh % 4) * 128
                last = e.matmul(out=pb[bank][:, off:off + 128],
                                lhsT=Kb[p][:, hh * 128:(hh + 1) * 128],
                                rhs=Vb[x4][:, hh * 128:(hh + 1) * 128],
                                start=(c == 0 and hh % 4 == 0),
                                stop=(c == NS - 1),
                                skip_group_check=True)
            return last
        s.task("pe", supd, reads=(f"AKb{p}", f"AVb{x4}"),
               writes=("pb4", "pb5"), name=f"{t}.supd", stage=5, chunk=c)
    s.flush_stages()


def phase_a2a(kb):
    nc, s, dr = kb.nc, kb.s, kb.dram
    SC = kb.SC
    SC.reset(kb.p1_sc_base)
    s.fence()
    pb = kb.pb

    kb.WA.reset()
    wgv = kb.WA.view([128, 8, IN], BF16)
    kb.WB.reset()
    wuv = kb.WB.view([128, 8, IN], BF16)
    wgr = dr["wgate"].rearrange("(c p) n -> p c n", p=128)
    wur = dr["wup"].rearrange("(c p) n -> p c n", p=128)
    s.task("qg", lambda e: e.dma_start(out=wgv[:], in_=wgr),
           writes=("WA",), name="a2_ldwg", nofence=True)
    s.task("qg", lambda e: e.dma_start(out=wuv[:], in_=wur),
           writes=("WB",), name="a2_ldwu", nofence=True)

    Y1s = [SC.view([128, D], F32) for _ in range(2)]
    zb = SC.view([128, D], BF16)
    zT = [SC.view([128, D], BF16) for _ in range(2)]
    sg = [SC.view([128, 512], F32) for _ in range(2)]
    g = [SC.view([128, IN], BF16) for _ in range(2)]
    dumm = SC.view([128, D], F32)
    st = [SC.view([128, 8], F32) for _ in range(2)]
    Z_SQ, Z_ARG, Z_RR, Z_X0, Z_X1, Z_X2 = 0, 1, 2, 3, 4, 5

    y1r, gr = dr["Y1"], dr["G"]

    s.begin_stages()
    for j in range(NACT):
        p = j % 2
        t = f"A2a.{j}"
        s.task("qs", lambda e, j=j, p=p: e.dma_start(
            out=Y1s[p][:], in_=y1r[j * CH:(j + 1) * CH, :]),
            reads=(f"dY1.{j}",), writes=(f"BYs{p}",), name=f"{t}.ld",
            stage=0, chunk=j)
        def stats(e, p=p):
            e = VG(nc)
            e.scalar_tensor_tensor(out=dumm[:], in0=Y1s[p][:], scalar=1.0,
                                   in1=Y1s[p][:], op0=ALU.mult, op1=ALU.mult,
                                   accum_out=st[p][:, Z_SQ:Z_SQ + 1])
            e.tensor_scalar(out=st[p][:, Z_ARG:Z_ARG + 1],
                            in0=st[p][:, Z_SQ:Z_SQ + 1], scalar1=1.0 / D,
                            scalar2=1e-5, op0=ALU.mult, op1=ALU.add)
            return emit_rsqrt(nc, st[p][:, Z_ARG:Z_ARG + 1],
                              st[p][:, Z_RR:Z_RR + 1],
                              st[p][:, Z_X0:Z_X0 + 1],
                              st[p][:, Z_X1:Z_X1 + 1],
                              st[p][:, Z_X2:Z_X2 + 1], e=e)
        s.task("dve", stats, reads=(f"BYs{p}",), writes=(f"Bst{p}", "Bdumm"),
               name=f"{t}.stats", stage=1, chunk=j)
        def zq(e, p=p):
            return e.scalar_tensor_tensor(out=zb[:], in0=Y1s[p][:],
                                          scalar=st[p][:, Z_RR:Z_RR + 1],
                                          in1=kb.mnwb[:], op0=ALU.mult,
                                          op1=ALU.mult)
        s.task("dve", zq, reads=(f"BYs{p}", f"Bst{p}", "mnwb"),
               writes=("Bzb",), name=f"{t}.zq", stage=2, chunk=j)
        def tz(e):
            last = None
            for k in range(8):
                last = e.transpose(out=pb[6][:, k * 128:(k + 1) * 128],
                                   in_=zb[:, k * 128:(k + 1) * 128],
                                   identity=kb.identb[:])
            return last
        s.task("pe", tz, reads=("Bzb", "identb"), writes=("pb6",),
               name=f"{t}.tz", stage=2, chunk=j)
        def evzT(e, p=p):
            e.copy(out=zT[p][:, 0:512], in_=pb[6][:, 0:512])
            return e.copy(out=zT[p][:, 512:1024],
                          in_=pb[6][:, 512:1024])
        s.task("act", evzT, reads=("pb6", "csb", "csbb"),
               writes=(f"BzT{p}",),
               name=f"{t}.evzT", stage=2, chunk=j)
        for sl in range(8):
            bG = 0 if sl % 2 == 0 else 2
            bU = 1 if sl % 2 == 0 else 3
            def gu(e, p=p, sl=sl, bG=bG, bU=bU):
                last = None
                for k in range(8):
                    lt = zT[p][:, k * 128:(k + 1) * 128]
                    last = e.matmul(out=pb[bG][:], lhsT=lt,
                                    rhs=wgv[:, k, sl * 512:(sl + 1) * 512],
                                    start=(k == 0), stop=(k == 7),
                                    skip_group_check=True)
                    last = e.matmul(out=pb[bU][:], lhsT=lt,
                                    rhs=wuv[:, k, sl * 512:(sl + 1) * 512],
                                    start=(k == 0), stop=(k == 7),
                                    skip_group_check=True)
                return last
            s.task("pe", gu, reads=(f"BzT{p}", "WA", "WB"),
                   writes=(f"pb{bG}", f"pb{bU}"), name=f"{t}.gu{sl}",
                   stage=3, chunk=j)
            def sil(e, p=p, bG=bG):
                return e.activation(out=sg[p][:], in_=pb[bG][:], func=AF.Silu)
            s.task("act", sil, reads=(f"pb{bG}", "csb", "csbb"),
                   writes=(f"Bsg{p}",), name=f"{t}.sil{sl}", stage=3, chunk=j)
            def gm(e, p=p, sl=sl, bU=bU):
                return e.tensor_tensor(out=g[p][:, sl * 512:(sl + 1) * 512],
                                       in0=sg[p][:], in1=pb[bU][:],
                                       op=ALU.mult)
            s.task("dve", gm, reads=(f"Bsg{p}", f"pb{bU}"),
                   writes=(f"Bg{p}",), name=f"{t}.gm{sl}", stage=3, chunk=j)
        s.task("qg", lambda e, j=j, p=p: e.dma_start(
            out=gr[j * CH:(j + 1) * CH, :], in_=g[p][:]),
            reads=(f"Bg{p}",), writes=(f"dG.{j}",), name=f"{t}.stG",
            stage=4, chunk=j)
    s.flush_stages()


def phase_a2b(kb):
    nc, s, dr = kb.nc, kb.s, kb.dram
    SC = kb.SC
    SC.reset(kb.p1_sc_base)
    s.fence()
    pb = kb.pb

    kb.WB.reset()
    wdv = kb.WB.view([128, 32, D], BF16)
    wdr = dr["wdown"].rearrange("(c p) n -> p c n", p=128)
    s.task("qg", lambda e: e.dma_start(out=wdv[:], in_=wdr),
           writes=("WB",), name="a2b_ldwd", nofence=True)

    g2 = [SC.view([128, IN], BF16) for _ in range(2)]
    gT = [SC.view([128, IN], BF16) for _ in range(2)]
    S1s = [SC.view([128, D], F32) for _ in range(4)]
    X2o_ = SC.view([128, D], F32)
    X2o = [X2o_, X2o_]

    gr, s1r, x2r = dr["G"], dr["S1"], dr["X2"]

    s.begin_stages()
    for j in range(NACT):
        p = j % 2
        x4 = j % 4
        t = f"A2b.{j}"
        s.task("qs", lambda e, j=j, p=p: e.dma_start(
            out=g2[p][:], in_=gr[j * CH:(j + 1) * CH, :]),
            reads=(f"dG.{j}",), writes=(f"Cg{p}",), name=f"{t}.ldg",
            stage=0, chunk=j)
        s.task("qa", lambda e, j=j, x4=x4: e.dma_start(
            out=S1s[x4][:], in_=s1r[j * CH:(j + 1) * CH, :]),
            reads=(f"dS1.{j}",), writes=(f"CS1{x4}",), name=f"{t}.lds",
            stage=0, chunk=j)
        for r in range(4):
            bank = 6 if r % 2 == 0 else 7
            def tg(e, p=p, r=r, bank=bank):
                last = None
                for tt_ in range(8):
                    kk = r * 8 + tt_
                    last = e.transpose(
                        out=pb[bank][:, tt_ * 128:(tt_ + 1) * 128],
                        in_=g2[p][:, kk * 128:(kk + 1) * 128],
                        identity=kb.identb[:])
                return last
            s.task("pe", tg, reads=(f"Cg{p}", "identb"),
                   writes=(f"pb{bank}",), name=f"{t}.tg{r}", stage=1, chunk=j)
            def evgT(e, p=p, r=r, bank=bank):
                return e.copy(out=gT[p][:, r * 1024:(r + 1) * 1024],
                              in_=pb[bank][:])
            s.task("act", evgT, reads=(f"pb{bank}", "csb", "csbb"),
                   writes=(f"CgT{p}",),
                   name=f"{t}.evgT{r}", stage=1, chunk=j)
        def dmm(e, p=p):
            last = None
            for k in range(32):
                lt = gT[p][:, k * 128:(k + 1) * 128]
                last = e.matmul(out=pb[0][:], lhsT=lt, rhs=wdv[:, k, 0:512],
                                start=(k == 0), stop=(k == 31),
                                skip_group_check=True)
                last = e.matmul(out=pb[1][:], lhsT=lt,
                                rhs=wdv[:, k, 512:1024],
                                start=(k == 0), stop=(k == 31),
                                skip_group_check=True)
            return last
        s.task("pe", dmm, reads=(f"CgT{p}", "WB"), writes=("pb0", "pb1"),
               name=f"{t}.dmm", stage=2, chunk=j)
        def x2ev(e, p=p, x4=x4):
            e.tensor_tensor(out=X2o[p][:, 0:512], in0=S1s[x4][:, 0:512],
                            in1=pb[0][:], op=ALU.add)
            return e.tensor_tensor(out=X2o[p][:, 512:1024],
                                   in0=S1s[x4][:, 512:1024], in1=pb[1][:],
                                   op=ALU.add)
        s.task("dve", x2ev, reads=("pb0", "pb1", f"CS1{x4}"),
               writes=("CX2",), name=f"{t}.x2ev", stage=3, chunk=j)
        s.task("qs", lambda e, j=j, p=p: e.dma_start(
            out=x2r[j * CH:(j + 1) * CH, :], in_=X2o[p][:]),
            reads=("CX2",), writes=(f"dX2.{j}",), name=f"{t}.stX2",
            stage=3, chunk=j)
    s.flush_stages()



_CACHE = {}


def _build(debug=None):
    if debug not in _CACHE:
        _CACHE[debug] = build_program(debug)
    return _CACHE[debug]


def make_in_maps(inputs):
    X = np.ascontiguousarray(np.asarray(inputs["L0_attn_hidden_states"],
                                        dtype=np.float32))
    B, T, _ = X.shape
    w = {k: np.ascontiguousarray(np.asarray(v, dtype=np.float32))
         for k, v in inputs.items() if k != "L0_attn_hidden_states"}

    consts = np.zeros((128, 8), np.float32)
    consts[:, 1] = 1.0
    constsb = np.zeros((128, 4), ml_dtypes.bfloat16)
    constsb[:, 1] = 1.0
    base = {
        "consts": consts, "constsb": np.asarray(constsb),
        "identb": np.asarray(np.eye(128, dtype=ml_dtypes.bfloat16)),
        "trimaskT": np.asarray(np.triu(np.ones((128, 128), np.float32))
                               .astype(ml_dtypes.bfloat16)),
        "ones2": np.asarray(np.ones((2, D), ml_dtypes.bfloat16)),
        "mnwb": np.asarray(np.tile(
            np.asarray(w["L0_attn_mlp_norm_w"], np.float32)[None, :],
            (128, 1)).astype(ml_dtypes.bfloat16)),
        "anw": np.ascontiguousarray(
            np.asarray(w["L0_attn_attn_norm_w"], np.float32)
            .reshape(8, 128).T),
        "onw": np.ascontiguousarray(
            np.tile(np.asarray(w["L0_attn_attn_out_norm_w"], np.float32), 8)
            .reshape(8, 128).T),
        "wq": w["L0_attn_Wq"], "wk": w["L0_attn_Wk"], "wv": w["L0_attn_Wv"],
        "wo": w["L0_attn_Wo"], "wgate": w["L0_attn_Wgate"],
        "wup": w["L0_attn_Wup"], "wdown": w["L0_attn_Wdown"],
        "w1t": np.ascontiguousarray(w["L0_ffn_w1"].T),
        "w2t": np.ascontiguousarray(w["L0_ffn_w2"].T),
        "b1": np.ascontiguousarray(w["L0_ffn_b1"].reshape(1, IN)),
        "b2": np.ascontiguousarray(w["L0_ffn_b2"].reshape(1, D)),
    }

    half = T // 2
    in_maps = []
    for b in range(B):
        for role in range(2):
            m = dict(base)
            if role == 0:
                xin = np.concatenate(
                    [np.zeros((half, D), np.float32), X[b, :half]], axis=0)
                mk = np.concatenate([np.zeros(half, np.float32),
                                     np.ones(half, np.float32)])
            else:
                xin = X[b]
                mk = np.ones(T, np.float32)
            m["x"] = np.ascontiguousarray(xin)
            m["maskc"] = np.ascontiguousarray(mk.reshape(NS, CH).T)
            in_maps.append(m)
    return X, in_maps


def kernel(**inputs):
    X, in_maps = make_in_maps(inputs)
    B, T, _ = X.shape
    kb = _build(DEBUG_PHASE)
    res = run_bass_kernel_spmd(kb.nc, in_maps, core_ids=list(range(8)))
    if DEBUG_PHASE is not None:
        return [res.results[c] for c in range(8)]
    half = T // 2
    out = np.empty_like(X)
    for b in range(B):
        out[b, :half] = res.results[2 * b]["out"]
        out[b, half:] = res.results[2 * b + 1]["out"]
    return out


# version 05
# speedup vs baseline: 1.0347x; 1.0347x over previous
import numpy as np
import ml_dtypes
from contextlib import ExitStack

import concourse.bass as bass
import concourse.mybir as mybir
from concourse.bass_utils import run_bass_kernel_spmd

F32 = mybir.dt.float32
BF16 = mybir.dt.bfloat16
I16 = mybir.dt.int16
U8 = mybir.dt.uint8
AF = mybir.ActivationFunctionType
ALU = mybir.AluOpType
AX = mybir.AxisListType

D = 1024
IN = 4096
H = 8
CH = 128
NS = 64
ACT0 = 32
NACT = NS - ACT0
MAGIC = float(np.float32(3 << 22))
QK = float(0x5F37)

BODY_OF = {"pe": "tensor", "dve": "vector", "act": "scalar", "pool": "gpsimd",
           "qs": "sync", "qa": "scalar", "qg": "gpsimd"}
IS_DMA = {"pe": False, "dve": False, "act": False, "pool": False,
          "qs": True, "qa": True, "qg": True}

DEBUG_PHASE = None
SELF_DEPS = True
CHAIN_SYNC = True


class Task:
    __slots__ = ("ekey", "idx", "emit", "deps", "name", "reads", "writes")

    def __init__(self, ekey, idx, emit, name):
        self.ekey = ekey
        self.idx = idx
        self.emit = emit
        self.deps = {}
        self.name = name
        self.reads = set()
        self.writes = set()


class Sched:
    def __init__(self):
        self.counts = {k: 0 for k in BODY_OF}
        self.body_items = {b: [] for b in set(BODY_OF.values())}
        self.res = {}
        self.fence_snap = None

    def fence(self):
        self.fence_snap = dict(self.counts)

    def begin_stages(self):
        self._stq = []

    def flush_stages(self):
        q = self._stq
        self._stq = None
        if not q:
            return
        from collections import defaultdict
        bych = defaultdict(list)
        maxc = maxs = 0
        for c, st_, rec in q:
            bych[(c, st_)].append(rec)
            maxc = max(maxc, c)
            maxs = max(maxs, st_)
        for t in range(maxc + maxs + 1):
            for st_ in range(maxs, -1, -1):
                c = t - st_
                for rec in bych.get((c, st_), ()):
                    self.task(*rec)

    def task(self, ekey, emit, reads=(), writes=(), name="", nofence=False,
             stage=None, chunk=None):
        if stage is not None and getattr(self, "_stq", None) is not None:
            self._stq.append((chunk, stage,
                              (ekey, emit, reads, writes, name, nofence)))
            return None
        idx = self.counts[ekey]
        self.counts[ekey] += 1
        t = Task(ekey, idx, emit, name)

        def dep(tok):
            if tok is None or tok == mytok:
                return
            ek, ix = tok
            if ek == ekey and (IS_DMA[ek] or ek == "pe" or not SELF_DEPS):
                return
            need = ix + 1
            if t.deps.get(ek, 0) < need:
                t.deps[ek] = need

        mytok = (ekey, idx)
        if self.fence_snap is not None and not nofence:
            for ek, cnt in self.fence_snap.items():
                if cnt > 0 and ek != ekey and t.deps.get(ek, 0) < cnt:
                    t.deps[ek] = cnt

        t.reads = set(reads)
        t.writes = set(writes)
        for r in reads:
            e = self.res.get(r)
            if e is None:
                e = self.res[r] = [None, []]
            dep(e[0])
            e[1].append(mytok)
        for wname in writes:
            e = self.res.get(wname)
            if e is not None:
                dep(e[0])
                for rt in e[1]:
                    dep(rt)
            self.res[wname] = [mytok, []]
        self.body_items[BODY_OF[ekey]].append(t)
        return mytok

    def emit_all(self, nc, es, junk):
        sems = {k: es.enter_context(nc.semaphore(f"sem_{k}")) for k in BODY_OF}
        blk = es.enter_context(nc.Block())
        total = {k: self.counts[k] * (16 if IS_DMA[k] else 1) for k in BODY_OF}
        watermark = {b: {k: 0 for k in BODY_OF} for b in self.body_items}

        def run_body(body, engobj):
            wm = watermark[body]
            for t in self.body_items[body]:
                for ek in sorted(t.deps):
                    cnt = t.deps[ek]
                    val = cnt * 16 if IS_DMA[ek] else cnt
                    if wm[ek] >= val:
                        continue
                    engobj.wait_ge(sems[ek], val)
                    wm[ek] = val
                last = t.emit(engobj)
                assert last is not None, f"task {t.name} returned no instruction"
                last.then_inc(sems[t.ekey], 16 if IS_DMA[t.ekey] else 1)
            if body == "gpsimd":
                for k in ("qs", "qa", "qg", "pe", "dve", "act", "pool"):
                    if total[k] and wm[k] < total[k]:
                        engobj.wait_ge(sems[k], total[k])

        blk.sync(lambda e: run_body("sync", e))
        blk.vector(lambda e: run_body("vector", e))
        blk.scalar(lambda e: run_body("scalar", e))
        blk.tensor(lambda e: run_body("tensor", e))
        blk.gpsimd(lambda e: run_body("gpsimd", e))


class Arena:
    def __init__(self, kb, name, nbytes):
        self.name = name
        self.t = kb.es.enter_context(kb.nc.sbuf_tensor(name, [128, nbytes], U8))
        self.nbytes = nbytes
        self.off = 0

    def reset(self, off=0):
        self.off = off

    def view(self, shape, dt):
        esz = 2 if dt in (BF16, I16) else 4
        ncols0 = int(np.prod(shape[1:]))
        align = 64 if ncols0 * esz >= 512 else 8
        self.off = (self.off + align - 1) & ~(align - 1)
        ncols = ncols0
        need = ncols * esz
        assert self.off + need <= self.nbytes, \
            f"arena {self.name} overflow: {self.off}+{need}>{self.nbytes}"
        base = self.off // esz
        self.off += need
        ap = self.t.ap().bitcast(dt)
        v = ap[0:shape[0], base:base + ncols]
        if len(shape) == 3:
            v = v.rearrange("p (a b) -> p a b", a=shape[1])
        return v


class KB:
    def __init__(self, debug=None):
        self.nc = bass.Bass("TRN2", target_bir_lowering=False)
        self.s = Sched()
        self.es = ExitStack()
        self.debug = debug
        self.dram = {}
        self.pb = []

    def sb(self, name, shape, dt):
        return self.es.enter_context(self.nc.sbuf_tensor(name, shape, dt))

    def dram_t(self, name, shape, dt, kind):
        t = self.nc.dram_tensor(name, shape, dt, kind=kind)
        self.dram[name] = t.ap()
        return self.dram[name]


_FILL = {"t": None}


def set_filler(ap):
    _FILL["t"] = ap


def vsep(nc):
    nc.vector.memset(_FILL["t"], 0.0)


_CHAIN = {"sem": None, "cnt": 0}


def set_chain_sem(sem):
    _CHAIN["sem"] = sem
    _CHAIN["cnt"] = 0


class VG:

    def __init__(self, nc):
        self._nc = nc
        self._pending = None

    def __getattr__(self, name):
        f = getattr(self._nc.vector, name)
        nc = self._nc

        def g(*a, _indep=False, **k):
            if self._pending is not None and not _indep:
                if CHAIN_SYNC:
                    self._pending.then_inc(_CHAIN["sem"], 1)
                    _CHAIN["cnt"] += 1
                    nc.vector.wait_ge(_CHAIN["sem"], _CHAIN["cnt"])
                else:
                    vsep(nc)
            r = f(*a, **k)
            self._pending = r
            return r
        return g


def emit_rsqrt(nc, v_ap, y_ap, s0, s1, s2, e=None):
    if e is None:
        e = VG(nc)
    e.memset(s0, 0.0)
    hi = v_ap.bitcast(I16)[:, 1::2]
    e.tensor_copy(out=s1, in_=hi, _indep=True)
    e.tensor_scalar(out=s1, in0=s1, scalar1=-0.5, scalar2=QK,
                    op0=ALU.mult, op1=ALU.add)
    e.tensor_copy(out=s0.bitcast(I16)[:, 1::2], in_=s1)
    e.tensor_scalar(out=s2, in0=v_ap, scalar1=-0.5, scalar2=None,
                    op0=ALU.mult, _indep=True)
    cur = s0
    last = None
    for it in range(2):
        dst = y_ap if it == 1 else (s1 if cur is s0 else s0)
        e.tensor_tensor(out=dst, in0=cur, in1=cur, op=ALU.mult)
        e.tensor_tensor(out=dst, in0=dst, in1=s2, op=ALU.mult)
        e.tensor_scalar(out=dst, in0=dst, scalar1=1.5, scalar2=None,
                        op0=ALU.add)
        last = e.tensor_tensor(out=dst, in0=cur, in1=dst, op=ALU.mult)
        cur = dst
    return last



def build_program(debug=None):
    kb = KB(debug)
    nc = kb.nc

    kb.dram_t("x", [NS * CH, D], F32, "ExternalInput")
    kb.dram_t("maskc", [CH, NS], F32, "ExternalInput")
    kb.dram_t("consts", [128, 8], F32, "ExternalInput")
    kb.dram_t("constsb", [128, 4], BF16, "ExternalInput")
    kb.dram_t("identb", [128, 128], BF16, "ExternalInput")
    kb.dram_t("trimaskT", [128, 128], BF16, "ExternalInput")
    kb.dram_t("ones2", [2, D], BF16, "ExternalInput")
    kb.dram_t("mnwb", [128, D], BF16, "ExternalInput")
    kb.dram_t("anw", [128, 8], F32, "ExternalInput")
    kb.dram_t("onw", [128, 8], F32, "ExternalInput")
    for nm, shp in (("wq", [D, D]), ("wk", [D, D]), ("wv", [D, D]),
                    ("wo", [D, D]), ("wgate", [D, IN]), ("wup", [D, IN]),
                    ("wdown", [IN, D]), ("w1t", [D, IN]), ("w2t", [IN, D]),
                    ("b1", [1, IN]), ("b2", [1, D])):
        kb.dram_t(nm, shp, F32, "ExternalInput")

    dbg = debug
    kb.dram_t("X1", [NS * CH, D], F32,
              "ExternalOutput" if dbg == "P1" else "Internal")
    kb.dram_t("Y1", [NACT * CH, D], F32,
              "ExternalOutput" if dbg == "A1" else "Internal")
    kb.dram_t("S1", [NACT * CH, D], F32,
              "ExternalOutput" if dbg == "A1" else "Internal")
    kb.dram_t("G", [NACT * CH, IN], BF16, "Internal")
    kb.dram_t("X2", [NACT * CH, D], F32,
              "ExternalOutput" if dbg == "A2" else "Internal")
    if dbg is None:
        kb.dram_t("out", [NACT * CH, D], F32, "ExternalOutput")
    kb.dram_t("WQ1d", [D, IN], BF16, "Internal")
    kb.dram_t("WQ2d", [IN, D], BF16, "Internal")
    if dbg == "A1":
        kb.dram_t("dbg_K0", [128, D], BF16, "ExternalOutput")
        kb.dram_t("dbg_V0", [128, D], BF16, "ExternalOutput")
        kb.dram_t("dbg_t0", [128, D], BF16, "ExternalOutput")
        kb.dram_t("dbg_S32", [128, D], BF16, "ExternalOutput")
        kb.dram_t("dbg_q32", [128, D], BF16, "ExternalOutput")
        kb.dram_t("dbg_at32", [128, D], BF16, "ExternalOutput")
        kb.dram_t("dbg_on32", [128, D], BF16, "ExternalOutput")
    if dbg == "P1":
        kb.dram_t("dbg_xq", [2 * 128, D], BF16, "ExternalOutput")
        kb.dram_t("dbg_xqT", [2 * 128, D], BF16, "ExternalOutput")
        kb.dram_t("dbg_h", [2 * 128, IN], F32, "ExternalOutput")
        kb.dram_t("dbg_st", [2 * 128, 32], F32, "ExternalOutput")
        kb.dram_t("dbg_st2", [2 * 128, 32], F32, "ExternalOutput")
        kb.dram_t("dbg_strow", [8, 128], BF16, "ExternalOutput")
        kb.dram_t("dbg_wq1", [128, 1024], BF16, "ExternalOutput")
        kb.dram_t("dbg_swb", [128, 2], F32, "ExternalOutput")

    with kb.es:
        kb.csb = kb.sb("s_csb", [128, 8], F32)
        kb.csbb = kb.sb("s_csbb", [128, 4], BF16)
        nc.const_aps.register(F32, 0.0, kb.csb[:, 0:1])
        nc.const_aps.register(F32, 1.0, kb.csb[:, 1:2])
        nc.const_aps.register(BF16, 0.0, kb.csbb[:, 0:1])
        nc.const_aps.register(BF16, 1.0, kb.csbb[:, 1:2])
        kb.identb = kb.sb("s_identb", [128, 128], BF16)
        kb.trimaskT = kb.sb("s_trimaskT", [128, 128], BF16)
        kb.maskc = kb.sb("s_maskc", [128, NS], F32)
        kb.ones2 = kb.sb("s_ones2", [2, D], BF16)
        kb.mnwb = kb.sb("s_mnwb", [128, D], BF16)
        kb.anw = kb.sb("s_anw", [128, 8], F32)
        kb.onw = kb.sb("s_onw", [128, 8], F32)
        kb.swb = kb.sb("swb", [128, 2], F32)
        kb.bfscr = kb.sb("bfscr", [128, 4], BF16)
        kb.b1x = kb.sb("b1x", [3, IN], BF16)
        kb.b2x = kb.sb("b2x", [3, D], BF16)
        kb.brow = kb.sb("brow", [1, 512], F32)
        kb.brow2 = kb.sb("brow2", [1, 512], F32)
        kb.browb = kb.sb("browb", [1, 512], BF16)
        kb.macc = kb.sb("macc", [128, 4], F32)
        kb.onescol = kb.sb("onescol", [128, 1], F32)
        kb.cell = kb.sb("cell", [1, 4], F32)
        kb.onesrow_f = kb.sb("onesrow_f", [1, 128], F32)
        kb.jdve = kb.sb("jdve", [128, 1], F32)
        kb.jact = kb.sb("jact", [128, 1], F32)
        kb.jpool = kb.sb("jpool", [128, 1], F32)
        set_filler(kb.jdve[:])
        kb.vch = kb.es.enter_context(nc.semaphore("vch"))
        set_chain_sem(kb.vch)

        for i in range(6):
            kb.pb.append(kb.es.enter_context(
                nc.psum_tensor(f"pb{i}", [128, 512], F32)))
        for i in range(6, 8):
            kb.pb.append(kb.es.enter_context(
                nc.psum_tensor(f"pb{i}", [128, 1024], BF16)))

        kb.WA = Arena(kb, "WA", 65536)
        kb.WB = Arena(kb, "WB", 65536)
        kb.SC = Arena(kb, "SC", 60608)

        startup(kb)
        prep_ffn_weights(kb)
        phase_ffn(kb, src="x", dst="X1", nslots=NS, tag="P1", first=True)
        if dbg != "P1":
            phase_a1(kb)
            if dbg != "A1":
                phase_a2a(kb)
                phase_a2b(kb)
                if dbg != "A2":
                    phase_ffn(kb, src="X2", dst="out", nslots=NACT, tag="P2",
                              first=False)
        kb.s.emit_all(nc, kb.es, (kb.jdve[:], kb.jact[:], kb.jpool[:]))
    return kb


def startup(kb):
    nc, s, dr = kb.nc, kb.s, kb.dram

    def ld(dst, src_name, res):
        s.task("qs", lambda e, d=dst, sn=src_name: e.dma_start(out=d,
                                                               in_=dr[sn]),
               writes=(res,), name=f"ld_{src_name}")

    ld(kb.csb[:], "consts", "csb")
    ld(kb.csbb[:], "constsb", "csbb")
    ld(kb.identb[:], "identb", "identb")
    ld(kb.trimaskT[:], "trimaskT", "trimaskT")
    ld(kb.maskc[:], "maskc", "maskc")
    ld(kb.ones2[:], "ones2", "ones2")
    ld(kb.mnwb[:], "mnwb", "mnwb")
    ld(kb.anw[:], "anw", "anw")
    ld(kb.onw[:], "onw", "onw")
    s.task("dve", lambda e: e.memset(kb.onescol[:], 1.0), writes=("onescol",),
           name="ones_col")
    s.task("dve", lambda e: e.memset(kb.onesrow_f[:], 1.0), writes=("onesrow",),
           name="ones_row")

    for bname, btile, wdt in (("b1", kb.b1x, IN), ("b2", kb.b2x, D)):
        for sl in range(wdt // 512):
            c0 = sl * 512
            s.task("qs", lambda e, bn=bname, c0=c0: e.dma_start(
                out=kb.brow[:], in_=dr[bn][0:1, c0:c0 + 512]),
                writes=("brow",), name=f"ldb_{bname}_{sl}")
            def hl(e):
                e = VG(nc)
                e.tensor_copy(out=kb.browb[:], in_=kb.brow[:])
                e.tensor_copy(out=kb.brow2[:], in_=kb.browb[:])
                return e.tensor_tensor(out=kb.brow2[:], in0=kb.brow[:],
                                       in1=kb.brow2[:], op=ALU.subtract)
            s.task("dve", hl, reads=("brow",), writes=("browb", "brow2"),
                   name=f"bhl_{bname}_{sl}")
            def p_hi(e, bt=btile, c0=c0):
                return e.dma_start(out=bt[0:1, c0:c0 + 512],
                                   in_=kb.browb[0:1, :])
            s.task("qg", p_hi, reads=("browb",), writes=(f"bx_{bname}",),
                   name=f"bhi_{bname}_{sl}")
            def p_hi2(e, bt=btile, c0=c0):
                return e.dma_start(out=bt[2:3, c0:c0 + 512],
                                   in_=kb.browb[0:1, :])
            s.task("qg", p_hi2, reads=("browb",), writes=(f"bx_{bname}",),
                   name=f"bhi2_{bname}_{sl}")
            def lo_conv(e):
                return e.tensor_copy(out=kb.browb[:], in_=kb.brow2[:])
            s.task("dve", lo_conv, reads=("brow2",), writes=("browb",),
                   name=f"blc_{bname}_{sl}")
            def p_lo(e, bt=btile, c0=c0):
                return e.dma_start(out=bt[1:2, c0:c0 + 512],
                                   in_=kb.browb[0:1, :])
            s.task("qg", p_lo, reads=("browb",), writes=(f"bx_{bname}",),
                   name=f"blo_{bname}_{sl}")


def prep_ffn_weights(kb):
    nc, s, dr = kb.nc, kb.s, kb.dram
    kb.wq1s = kb.WA.view([128, 8, IN], BF16)
    kb.wq1f = kb.wq1s.rearrange("p a b -> p (a b)")
    kb.wq2s = kb.WB.view([128, 32, D], BF16)
    kb.wq2f = kb.wq2s.rearrange("p a b -> p (a b)")
    stage = kb.SC.view([128, 1024], F32)
    kb.stage_view = stage
    kb.p1_sc_base = kb.SC.off

    w1f = dr["w1t"].rearrange("(c p) n -> p c n", p=128)
    w2f = dr["w2t"].rearrange("(c p) n -> p c n", p=128)
    sp1 = dr["WQ1d"].rearrange("(c p) n -> p c n", p=128)
    sp2 = dr["WQ2d"].rearrange("(c p) n -> p c n", p=128)

    def dtile(w3, wdt, hh):
        if wdt >= 1024:
            ch, off = (hh * 1024) // wdt, (hh * 1024) % wdt
            return w3[:, ch, off:off + 1024]
        raise AssertionError

    s.task("dve", lambda e: e.memset(kb.macc[:], 0.0), writes=("macc",),
           name="macc0")
    NT = (8 * IN) // 1024
    for wi, (wf, wdt) in ((0, (w1f, IN)), (1, (w2f, D))):
        for hh in range(NT):
            srcv = dtile(wf, wdt, hh)
            s.task("qs", lambda e, sc=srcv: e.dma_start(out=stage[:], in_=sc),
                   writes=("stage",), name=f"mw_ld{wi}_{hh}", nofence=True)
            def red(e, wi=wi):
                e = VG(nc)
                e.tensor_reduce(out=kb.macc[:, 2 + wi:3 + wi], in_=stage[:],
                                axis=AX.X, op=ALU.add, apply_absolute_value=True)
                return e.tensor_tensor(out=kb.macc[:, wi:wi + 1],
                                       in0=kb.macc[:, wi:wi + 1],
                                       in1=kb.macc[:, 2 + wi:3 + wi],
                                       op=ALU.add)
            s.task("dve", red, reads=("stage",), writes=("macc",),
                   name=f"mw_red{wi}_{hh}")
    def totmm(e):
        return e.matmul(out=kb.pb[0][0:1, 0:2], lhsT=kb.onescol[:],
                        rhs=kb.macc[:, 0:2], start=True, stop=True)
    s.task("pe", totmm, reads=("macc", "onescol"), writes=("pb0",),
           name="mw_tot")
    def swcalc(e):
        e = VG(nc)
        e.tensor_copy(out=kb.cell[0:1, 0:2], in_=kb.pb[0][0:1, 0:2])
        e.tensor_scalar(out=kb.cell[0:1, 0:2], in0=kb.cell[0:1, 0:2],
                        scalar1=1.0 / (D * IN), scalar2=1e-5,
                        op0=ALU.mult, op1=ALU.max)
        return e.reciprocal(out=kb.cell[0:1, 0:2], in_=kb.cell[0:1, 0:2])
    s.task("dve", swcalc, reads=("pb0",), writes=("cell",), name="sw_calc")
    def swbmm(e):
        return e.matmul(out=kb.pb[1][:, 0:2], lhsT=kb.onesrow_f[:],
                        rhs=kb.cell[0:1, 0:2], start=True, stop=True)
    s.task("pe", swbmm, reads=("cell", "onesrow"), writes=("pb1",),
           name="swb_mm")
    s.task("dve", lambda e: e.tensor_copy(out=kb.swb[:], in_=kb.pb[1][:, 0:2]),
           reads=("pb1",), writes=("swb",), name="swb_ev")

    for wi, (wf, wdt, dstf, spf, arena, spn) in enumerate((
            (w1f, IN, kb.wq1f, sp1, "WA", "dWQ1d"),
            (w2f, D, kb.wq2f, sp2, "WB", "dWQ2d"))):
        for hh in range(NT):
            sl = slice(hh * 1024, (hh + 1) * 1024)
            srcv = dtile(wf, wdt, hh)
            s.task("qs", lambda e, sc=srcv: e.dma_start(out=stage[:], in_=sc),
                   writes=("stage",), name=f"qw_ld{wi}_{hh}", nofence=True)
            def qa(e, wi=wi):
                return e.tensor_scalar(out=stage[:], in0=stage[:],
                                       scalar1=kb.swb[:, wi:wi + 1],
                                       scalar2=MAGIC, op0=ALU.mult,
                                       op1=ALU.add)
            s.task("pool", qa, reads=("stage", "swb"), writes=("stage",),
                   name=f"qw_a{wi}_{hh}")
            def qb(e):
                return e.tensor_scalar(out=stage[:], in0=stage[:],
                                       scalar1=-MAGIC, scalar2=-1.0,
                                       op0=ALU.add, op1=ALU.max)
            s.task("dve", qb, reads=("stage",), writes=("stage",),
                   name=f"qw_b{wi}_{hh}")
            def qc(e, dsl=dstf[:, sl]):
                return e.tensor_scalar(out=dsl, in0=stage[:], scalar1=1.0,
                                       scalar2=None, op0=ALU.min)
            s.task("dve", qc, reads=("stage",), writes=(arena,),
                   name=f"qw_c{wi}_{hh}")
            s.task("qg", lambda e, dsl=dstf[:, sl], ssl=dtile(spf, wdt, hh):
                   e.dma_start(out=ssl, in_=dsl), reads=(arena,),
                   writes=(spn,), name=f"qw_sp{wi}_{hh}", nofence=True)


def phase_ffn(kb, src, dst, nslots, tag, first):
    nc, s, dr = kb.nc, kb.s, kb.dram
    SC = kb.SC
    pb = kb.pb
    SC.reset(kb.p1_sc_base)
    if not first:
        s.fence()
        w1r = dr["WQ1d"].rearrange("(c p) n -> p c n", p=128)
        w2r = dr["WQ2d"].rearrange("(c p) n -> p c n", p=128)
        s.task("qs", lambda e: e.dma_start(out=kb.wq1s[:], in_=w1r),
               reads=("dWQ1d",), writes=("WA",), name="p2_ldw1", nofence=True)
        s.task("qs", lambda e: e.dma_start(out=kb.wq2s[:], in_=w2r),
               reads=("dWQ2d",), writes=("WB",), name="p2_ldw2", nofence=True)

    Xs = [SC.view([128, D], F32) for _ in range(2)]
    Xr = SC.view([128, D], F32)
    X1o = SC.view([128, D], F32)
    h = SC.view([128, IN], F32)
    xt = SC.view([128, D], F32)
    xq = SC.view([128, D], BF16)
    xqT = SC.view([128, D], BF16)
    ht = kb.stage_view
    hq = SC.view([128, 1024], BF16)
    hqT = SC.view([128, IN], BF16)
    st = [SC.view([128, 32], F32) for _ in range(2)]
    st2 = [SC.view([128, 32], F32) for _ in range(2)]
    stq1 = SC.view([128, 4], BF16)
    stq2 = SC.view([128, 4], BF16)
    strow1 = [SC.view([4, 128], BF16) for _ in range(2)]
    strow2 = [SC.view([4, 128], BF16) for _ in range(2)]

    if first:
        s.fence()

    xr = dr[src]
    dstr = dr[dst]

    C_SUM, C_SQ0, C_SQ1, C_MAX, C_MIN = 0, 1, 2, 3, 4
    C_M, C_V, C_A, C_ARG, C_RLN, C_RRMS = 5, 6, 7, 8, 9, 10
    C_RA, C_AM, C_S, C_C, C_MC, C_SV, C_SVH, C_SVL, C_INV = \
        11, 12, 13, 14, 15, 16, 17, 18, 19
    C_X0, C_X1, C_X2 = 20, 22, 24
    Hs_SUM, Hs_AM, Hs_ARG, Hs_RR, Hs_S, Hs_C, Hs_SV, Hs_SVH, Hs_SVL, Hs_INV = \
        8, 9, 10, 11, 12, 13, 14, 15, 16, 17
    Hs_X0, Hs_X1, Hs_X2 = 18, 19, 20

    s.begin_stages()
    for c in range(nslots):
        p = c % 2
        x3 = c % 2
        t = f"{tag}.{c}"
        rsrc = f"d{src}.{c}" if src != "x" else f"dx.{c}"
        s.task("qs", lambda e, c=c, x3=x3: e.dma_start(
            out=Xs[x3][:], in_=xr[c * CH:(c + 1) * CH, :]),
            reads=(rsrc,), writes=(f"{tag}Xs{x3}",), name=f"{t}.ldX",
            stage=0, chunk=c)
        def stats1(e, p=p, x3=x3):
            e.reduce_sum(out=st[p][:, C_SUM:C_SUM + 1], in_=Xs[x3][:],
                         axis=AX.X)
            e.reduce_max(out=st[p][:, C_MAX:C_MAX + 1], in_=Xs[x3][:],
                         axis=AX.X)
            return e.tensor_reduce(out=st[p][:, C_MIN:C_MIN + 1],
                                   in_=Xs[x3][:], axis=AX.X, op=ALU.min)
        s.task("dve", stats1, reads=(f"{tag}Xs{x3}",),
               writes=(f"{tag}st{p}",), name=f"{t}.stats1", stage=1, chunk=c)
        def sq(e, p=p, x3=x3):
            e.activation(out=ht[:, 0:512], in_=Xs[x3][:, 0:512],
                         func=AF.Square,
                         accum_out=st[p][:, C_SQ0:C_SQ0 + 1])
            return e.activation(out=ht[:, 512:1024], in_=Xs[x3][:, 512:1024],
                                func=AF.Square,
                                accum_out=st[p][:, C_SQ1:C_SQ1 + 1])
        s.task("act", sq, reads=(f"{tag}Xs{x3}", "csb"),
               writes=("stage", f"{tag}st{p}"), name=f"{t}.sq",
               stage=1, chunk=c)
        def stats2(e, p=p):
            e = VG(nc)
            stp = st[p]
            col = lambda i, w=1: stp[:, i:i + w]
            e.tensor_scalar(out=col(C_M), in0=col(C_SUM), scalar1=1.0 / D,
                            scalar2=None, op0=ALU.mult)
            e.tensor_tensor(out=col(C_V), in0=col(C_SQ0), in1=col(C_SQ1),
                            op=ALU.add)
            e.tensor_scalar(out=col(C_V), in0=col(C_V), scalar1=1.0 / D,
                            scalar2=None, op0=ALU.mult)
            e.tensor_tensor(out=col(C_A), in0=col(C_M), in1=col(C_M),
                            op=ALU.mult)
            e.tensor_tensor(out=col(C_V), in0=col(C_V), in1=col(C_A),
                            op=ALU.subtract)
            e.tensor_scalar(out=col(C_A), in0=col(C_V), scalar1=1e-5,
                            scalar2=None, op0=ALU.add)
            e.reciprocal(out=col(C_RA), in_=col(C_A))
            e.tensor_tensor(out=col(C_ARG), in0=col(C_V), in1=col(C_RA),
                            op=ALU.mult)
            e.tensor_scalar(out=col(C_ARG), in0=col(C_ARG), scalar1=1e-6,
                            scalar2=None, op0=ALU.add)
            emit_rsqrt(nc, stp[:, C_A:C_A + 2], stp[:, C_RLN:C_RLN + 2],
                       stp[:, C_X0:C_X0 + 2], stp[:, C_X1:C_X1 + 2],
                       stp[:, C_X2:C_X2 + 2], e=e)
            e.tensor_tensor(out=col(C_AM), in0=col(C_MAX), in1=col(C_M),
                            op=ALU.subtract)
            e.tensor_tensor(out=col(C_MIN), in0=col(C_M), in1=col(C_MIN),
                            op=ALU.subtract)
            e.tensor_tensor(out=col(C_AM), in0=col(C_AM), in1=col(C_MIN),
                            op=ALU.max)
            e.tensor_tensor(out=col(C_AM), in0=col(C_AM), in1=col(C_RLN),
                            op=ALU.mult)
            e.tensor_tensor(out=col(C_AM), in0=col(C_AM), in1=col(C_RRMS),
                            op=ALU.mult)
            e.tensor_scalar(out=col(C_AM), in0=col(C_AM), scalar1=1e-5,
                            scalar2=None, op0=ALU.max)
            e.reciprocal(out=col(C_S), in_=col(C_AM))
            e.tensor_scalar(out=col(C_S), in0=col(C_S), scalar1=127.0,
                            scalar2=None, op0=ALU.mult)
            e.tensor_tensor(out=col(C_C), in0=col(C_RLN), in1=col(C_RRMS),
                            op=ALU.mult)
            e.tensor_tensor(out=col(C_C), in0=col(C_C), in1=col(C_S),
                            op=ALU.mult)
            e.tensor_tensor(out=col(C_MC), in0=col(C_M), in1=col(C_C),
                            op=ALU.mult)
            e.tensor_scalar(out=col(C_MC), in0=col(C_MC), scalar1=-1.0,
                            scalar2=None, op0=ALU.mult)
            e.tensor_tensor(out=col(C_SV), in0=col(C_S), in1=kb.swb[:, 0:1],
                            op=ALU.mult)
            e.reciprocal(out=col(C_INV), in_=col(C_SV))
            e.tensor_copy(out=kb.bfscr[:, 0:1], in_=col(C_SV))
            e.tensor_copy(out=col(C_SVH), in_=kb.bfscr[:, 0:1])
            e.tensor_tensor(out=col(C_SVL), in0=col(C_SV), in1=col(C_SVH),
                            op=ALU.subtract)
            e.tensor_copy(out=stq1[:, 0:1], in_=col(C_SVH))
            e.tensor_copy(out=stq1[:, 1:2], in_=col(C_SVH))
            return e.tensor_copy(out=stq1[:, 2:3], in_=col(C_SVL))
        s.task("dve", stats2, reads=(f"{tag}st{p}", "swb"),
               writes=(f"{tag}st{p}", f"{tag}stq1", "bfscr"),
               name=f"{t}.stats2", stage=1, chunk=c)
        def qa(e, p=p, x3=x3):
            return e.tensor_scalar(out=xt[:], in0=Xs[x3][:],
                                   scalar1=st[p][:, C_C:C_C + 1],
                                   scalar2=st[p][:, C_MC:C_MC + 1],
                                   op0=ALU.mult, op1=ALU.add)
        s.task("pool", qa, reads=(f"{tag}Xs{x3}", f"{tag}st{p}"),
               writes=(f"{tag}xt",), name=f"{t}.qa", stage=2, chunk=c)
        def qb(e):
            return e.tensor_scalar(out=xq[:], in0=xt[:], scalar1=MAGIC,
                                   scalar2=-MAGIC, op0=ALU.add, op1=ALU.add)
        s.task("dve", qb, reads=(f"{tag}xt",), writes=(f"{tag}xq",),
               name=f"{t}.qb", stage=2, chunk=c)
        def txq(e):
            last = None
            for k in range(8):
                last = e.transpose(out=pb[6][:, k * 128:(k + 1) * 128],
                                   in_=xq[:, k * 128:(k + 1) * 128],
                                   identity=kb.identb[:])
            return last
        s.task("pe", txq, reads=(f"{tag}xq", "identb"), writes=("pb6",),
               name=f"{t}.txq", stage=2, chunk=c)
        def evxqT(e):
            e.copy(out=xqT[:, 0:512], in_=pb[6][:, 0:512])
            return e.copy(out=xqT[:, 512:1024], in_=pb[6][:, 512:1024])
        s.task("act", evxqT, reads=("pb6", "csb", "csbb"),
               writes=(f"{tag}xqT",),
               name=f"{t}.evxqT", stage=2, chunk=c)
        def mint(e):
            return e.transpose(out=pb[6][0:4, 0:128], in_=stq1[:],
                               identity=kb.identb[:])
        s.task("pe", mint, reads=(f"{tag}stq1", "identb"), writes=("pb6",),
               name=f"{t}.minT", stage=2, chunk=c)
        def evrow(e, p=p):
            return e.tensor_copy(out=strow1[p][:], in_=pb[6][0:4, 0:128])
        s.task("dve", evrow, reads=("pb6",), writes=(f"{tag}strow1{p}",),
               name=f"{t}.evrow", stage=2, chunk=c)
        for q in range(4):
            bA, bB = (0, 1) if q % 2 == 0 else (2, 3)
            def mm1(e, p=p, q=q, bA=bA, bB=bB):
                last = None
                for k in range(8):
                    lt = xqT[:, k * 128:(k + 1) * 128]
                    last = e.matmul(out=pb[bA][:], lhsT=lt,
                                    rhs=kb.wq1s[:, k, q * 1024:q * 1024 + 512],
                                    start=(k == 0), stop=False,
                                    skip_group_check=True)
                    last = e.matmul(
                        out=pb[bB][:], lhsT=lt,
                        rhs=kb.wq1s[:, k, q * 1024 + 512:(q + 1) * 1024],
                        start=(k == 0), stop=False, skip_group_check=True)
                last = e.matmul(out=pb[bA][:], lhsT=strow1[p][0:3, :],
                                rhs=kb.b1x[0:3, q * 1024:q * 1024 + 512],
                                start=False, stop=True, skip_group_check=True)
                last = e.matmul(out=pb[bB][:], lhsT=strow1[p][0:3, :],
                                rhs=kb.b1x[0:3, q * 1024 + 512:(q + 1) * 1024],
                                start=False, stop=True, skip_group_check=True)
                return last
            s.task("pe", mm1, reads=(f"{tag}xqT", f"{tag}strow1{p}", "WA",
                                     "bx_b1"),
                   writes=(f"pb{bA}", f"pb{bB}"), name=f"{t}.mm1q{q}",
                   stage=3, chunk=c)
            def gel(e, p=p, q=q, bA=bA, bB=bB):
                e.activation(out=h[:, q * 1024:q * 1024 + 512], in_=pb[bA][:],
                             func=AF.Gelu, scale=st[p][:, C_INV:C_INV + 1])
                return e.activation(out=h[:, q * 1024 + 512:(q + 1) * 1024],
                                    in_=pb[bB][:], func=AF.Gelu,
                                    scale=st[p][:, C_INV:C_INV + 1])
            s.task("act", gel, reads=(f"pb{bA}", f"pb{bB}", f"{tag}st{p}",
                                      "csb"),
                   writes=(f"{tag}h",), name=f"{t}.gelu{q}", stage=3, chunk=c)
            def hsq_q(e, p=p, q=q):
                e.activation(out=ht[:, 0:512],
                             in_=h[:, q * 1024:q * 1024 + 512],
                             func=AF.Square,
                             accum_out=st2[p][:, 2 * q:2 * q + 1])
                return e.activation(out=ht[:, 512:1024],
                                    in_=h[:, q * 1024 + 512:(q + 1) * 1024],
                                    func=AF.Square,
                                    accum_out=st2[p][:, 2 * q + 1:2 * q + 2])
            s.task("act", hsq_q, reads=(f"{tag}h", "csb"),
                   writes=("stage", f"{tag}st2{p}"), name=f"{t}.hsq{q}",
                   stage=3, chunk=c)
            def ham_q(e, p=p, q=q):
                return e.reduce_max(out=st2[p][:, 21 + q:22 + q],
                                    in_=h[:, q * 1024:(q + 1) * 1024],
                                    axis=AX.X, apply_absolute_value=True)
            s.task("dve", ham_q, reads=(f"{tag}h",), writes=(f"{tag}st2{p}",),
                   name=f"{t}.ham{q}", stage=3, chunk=c)
        def hstats(e, p=p):
            e = VG(nc)
            stp = st2[p]
            col = lambda i, w=1: stp[:, i:i + w]
            e.reduce_sum(out=col(Hs_SUM), in_=stp[:, 0:8], axis=AX.X)
            e.reduce_max(out=col(Hs_AM), in_=stp[:, 21:25], axis=AX.X)
            e.tensor_scalar(out=col(Hs_ARG), in0=col(Hs_SUM),
                            scalar1=1.0 / IN, scalar2=1e-6, op0=ALU.mult,
                            op1=ALU.add)
            emit_rsqrt(nc, col(Hs_ARG), col(Hs_RR), col(Hs_X0), col(Hs_X1),
                       col(Hs_X2), e=e)
            e.tensor_tensor(out=col(Hs_AM), in0=col(Hs_AM), in1=col(Hs_RR),
                            op=ALU.mult)
            e.tensor_scalar(out=col(Hs_AM), in0=col(Hs_AM), scalar1=1e-5,
                            scalar2=None, op0=ALU.max)
            e.reciprocal(out=col(Hs_S), in_=col(Hs_AM))
            e.tensor_scalar(out=col(Hs_S), in0=col(Hs_S), scalar1=127.0,
                            scalar2=None, op0=ALU.mult)
            e.tensor_tensor(out=col(Hs_C), in0=col(Hs_RR), in1=col(Hs_S),
                            op=ALU.mult)
            e.tensor_tensor(out=col(Hs_SV), in0=col(Hs_S), in1=kb.swb[:, 1:2],
                            op=ALU.mult)
            e.reciprocal(out=col(Hs_INV), in_=col(Hs_SV))
            e.tensor_scalar(out=col(Hs_INV), in0=col(Hs_INV), scalar1=0.5,
                            scalar2=None, op0=ALU.mult)
            e.tensor_copy(out=kb.bfscr[:, 1:2], in_=col(Hs_SV))
            e.tensor_copy(out=col(Hs_SVH), in_=kb.bfscr[:, 1:2])
            e.tensor_tensor(out=col(Hs_SVL), in0=col(Hs_SV), in1=col(Hs_SVH),
                            op=ALU.subtract)
            e.tensor_copy(out=stq2[:, 0:1], in_=col(Hs_SVH))
            e.tensor_copy(out=stq2[:, 1:2], in_=col(Hs_SVH))
            return e.tensor_copy(out=stq2[:, 2:3], in_=col(Hs_SVL))
        s.task("dve", hstats, reads=(f"{tag}st2{p}", "swb"),
               writes=(f"{tag}st2{p}", f"{tag}stq2", "bfscr"),
               name=f"{t}.hstats", stage=4, chunk=c)
        def mint2(e):
            return e.transpose(out=pb[6][0:4, 128:256], in_=stq2[:],
                               identity=kb.identb[:])
        s.task("pe", mint2, reads=(f"{tag}stq2", "identb"), writes=("pb6",),
               name=f"{t}.minT2", stage=4, chunk=c)
        def evrow2(e, p=p):
            return e.tensor_copy(out=strow2[p][:], in_=pb[6][0:4, 128:256])
        s.task("dve", evrow2, reads=("pb6",), writes=(f"{tag}strow2{p}",),
               name=f"{t}.evrow2", stage=4, chunk=c)
        for sl in range(4):
            htb = ht if sl % 2 == 0 else xt
            htres = "stage" if sl % 2 == 0 else f"{tag}xt"
            def hqa(e, p=p, sl=sl, htb=htb):
                return e.tensor_scalar(out=htb[:],
                                       in0=h[:, sl * 1024:(sl + 1) * 1024],
                                       scalar1=st2[p][:, Hs_C:Hs_C + 1],
                                       scalar2=MAGIC, op0=ALU.mult,
                                       op1=ALU.add)
            s.task("pool", hqa, reads=(f"{tag}h", f"{tag}st2{p}"),
                   writes=(htres,), name=f"{t}.hqa{sl}", stage=4, chunk=c)
            def hqb(e, htb=htb):
                return e.tensor_scalar(out=hq[:], in0=htb[:], scalar1=-MAGIC,
                                       scalar2=127.0, op0=ALU.add,
                                       op1=ALU.min)
            s.task("dve", hqb, reads=(htres,), writes=(f"{tag}hq",),
                   name=f"{t}.hqb{sl}", stage=4, chunk=c)
            def thq(e):
                last = None
                for k in range(8):
                    last = e.transpose(out=pb[7][:, k * 128:(k + 1) * 128],
                                       in_=hq[:, k * 128:(k + 1) * 128],
                                       identity=kb.identb[:])
                return last
            s.task("pe", thq, reads=(f"{tag}hq", "identb"), writes=("pb7",),
                   name=f"{t}.thq{sl}", stage=4, chunk=c)
            def evhqT(e, sl=sl):
                e.copy(out=hqT[:, sl * 1024:sl * 1024 + 512],
                       in_=pb[7][:, 0:512])
                return e.copy(
                    out=hqT[:, sl * 1024 + 512:(sl + 1) * 1024],
                    in_=pb[7][:, 512:1024])
            s.task("act", evhqT, reads=("pb7", "csb", "csbb"),
                   writes=(f"{tag}hqT",),
                   name=f"{t}.evhqT{sl}", stage=4, chunk=c)
        def mm2(e, p=p):
            last = None
            for k in range(32):
                lt = hqT[:, k * 128:(k + 1) * 128]
                last = e.matmul(out=pb[4][:], lhsT=lt,
                                rhs=kb.wq2s[:, k, 0:512], start=(k == 0),
                                stop=False, skip_group_check=True)
                last = e.matmul(out=pb[5][:], lhsT=lt,
                                rhs=kb.wq2s[:, k, 512:1024], start=(k == 0),
                                stop=False, skip_group_check=True)
            last = e.matmul(out=pb[4][:], lhsT=strow2[p][0:3, :],
                            rhs=kb.b2x[0:3, 0:512], start=False, stop=True,
                            skip_group_check=True)
            last = e.matmul(out=pb[5][:], lhsT=strow2[p][0:3, :],
                            rhs=kb.b2x[0:3, 512:1024], start=False, stop=True,
                            skip_group_check=True)
            return last
        s.task("pe", mm2, reads=(f"{tag}hqT", f"{tag}strow2{p}", "WB",
                                 "bx_b2"),
               writes=("pb4", "pb5"), name=f"{t}.mm2", stage=5, chunk=c)
        s.task("qg", lambda e, c=c: e.dma_start(
            out=Xr[:], in_=xr[c * CH:(c + 1) * CH, :]),
            reads=(rsrc,), writes=(f"{tag}Xr",), name=f"{t}.ldXr",
            stage=5, chunk=c)
        def x1ev(e, p=p):
            e.scalar_tensor_tensor(out=X1o[:, 0:512], in0=pb[4][:],
                                   scalar=st2[p][:, Hs_INV:Hs_INV + 1],
                                   in1=Xr[:, 0:512], op0=ALU.mult,
                                   op1=ALU.add)
            return e.scalar_tensor_tensor(out=X1o[:, 512:1024], in0=pb[5][:],
                                          scalar=st2[p][:, Hs_INV:Hs_INV + 1],
                                          in1=Xr[:, 512:1024],
                                          op0=ALU.mult, op1=ALU.add)
        s.task("dve", x1ev, reads=("pb4", "pb5", f"{tag}st2{p}",
                                   f"{tag}Xr"),
               writes=(f"{tag}X1o",), name=f"{t}.x1ev", stage=6, chunk=c)
        s.task("qa", lambda e, c=c: e.dma_start(
            out=dstr[c * CH:(c + 1) * CH, :], in_=X1o[:]),
            reads=(f"{tag}X1o",), writes=(f"d{dst}.{c}",), name=f"{t}.st",
            stage=6, chunk=c)
    s.flush_stages()


def phase_a1(kb):
    nc, s, dr = kb.nc, kb.s, kb.dram
    SC = kb.SC
    SC.reset(kb.p1_sc_base)
    s.fence()
    pb = kb.pb

    kb.WB.reset()
    wqv = kb.WB.view([128, 8, D], BF16)
    wkv = kb.WB.view([128, 8, D], BF16)
    wvv = kb.WB.view([128, 8, D], BF16)
    wov = kb.WB.view([128, 8, D], BF16)
    kb.WA.reset()
    wstage = kb.WA.view([128, 8, D], F32)
    for wname, dstv, normcol in (("wq", wqv, kb.anw), ("wk", wkv, kb.anw),
                                 ("wv", wvv, kb.anw), ("wo", wov, kb.onw)):
        wr = dr[wname].rearrange("(c p) n -> p c n", p=128)
        s.task("qs", lambda e, wr=wr: e.dma_start(out=wstage[:], in_=wr),
               writes=("WA",), name=f"a1_ld_{wname}", nofence=True)
        def fold(e, dstv=dstv, ncol=normcol):
            last = None
            for k in range(8):
                last = e.tensor_scalar(out=dstv[:, k, :], in0=wstage[:, k, :],
                                       scalar1=ncol[:, k:k + 1], scalar2=None,
                                       op0=ALU.mult)
            return last
        s.task("dve", fold, reads=("WA", "anw", "onw"), writes=("WB",),
               name=f"a1_fold_{wname}", nofence=True)

    X1s = [SC.view([128, D], F32) for _ in range(2)]
    X1r = SC.view([128, D], F32)
    tb = SC.view([128, D], BF16)
    tT = [SC.view([128, D], BF16) for _ in range(2)]
    Kb = [SC.view([128, D], BF16) for _ in range(2)]
    Vb = [SC.view([128, D], BF16) for _ in range(3)]
    Qb = SC.view([128, D], BF16)
    qT = [SC.view([128, D], BF16) for _ in range(2)]
    kT = SC.view([128, D], BF16)
    Ssb = SC.view([128, D], BF16)
    atnT = SC.view([128, D], BF16)
    o2f = kb.stage_view
    o_sb = SC.view([128, D], BF16)
    dumm = o2f
    on = SC.view([128, D], BF16)
    onT = SC.view([128, D], BF16)
    y1 = SC.view([128, D], F32)
    s1 = SC.view([128, D], F32)
    st = [SC.view([128, 24], F32) for _ in range(2)]
    stq = [SC.view([128, 4], BF16) for _ in range(8)]
    late = [SC.view([128, 1], F32) for _ in range(8)]
    strow = [SC.view([4, 128], BF16) for _ in range(2)]
    st2 = [SC.view([128, 48], F32) for _ in range(2)]

    C_SUM, C_SQ, C_M, C_V, C_A, C_ARG, C_RLN, C_RRMS = 0, 1, 2, 3, 4, 5, 6, 7
    C_CC, C_MC, C_COR, C_CH, C_CL = 8, 9, 10, 11, 12
    C_X0, C_X1, C_X2 = 14, 16, 18
    O_SS, O_ARG, O_RS, O_X0, O_X1, O_X2 = 0, 8, 16, 24, 32, 40

    x1r, y1r, s1r = dr["X1"], dr["Y1"], dr["S1"]

    s.begin_stages()
    for c in range(NS):
        p = c % 2
        x3 = c % 2
        x8 = c % 8
        x4 = c % 3
        t = f"A1.{c}"
        active = c >= ACT0
        j = c - ACT0
        s.task("qs", lambda e, c=c, x3=x3: e.dma_start(
            out=X1s[x3][:], in_=x1r[c * CH:(c + 1) * CH, :]),
            reads=(f"dX1.{c}",), writes=(f"AXs{x3}",), name=f"{t}.ld",
            stage=0, chunk=c)
        def stats1(e, p=p, x3=x3):
            e.reduce_sum(out=st[p][:, C_SUM:C_SUM + 1], in_=X1s[x3][:],
                         axis=AX.X)
            return e.scalar_tensor_tensor(out=dumm[:], in0=X1s[x3][:],
                                          scalar=1.0, in1=X1s[x3][:],
                                          op0=ALU.mult, op1=ALU.mult,
                                          accum_out=st[p][:, C_SQ:C_SQ + 1])
        s.task("dve", stats1, reads=(f"AXs{x3}",), writes=(f"Ast{p}", "stage"),
               name=f"{t}.stats1", stage=1, chunk=c)
        def stats2(e, p=p, active=active, x8=x8):
            e = VG(nc)
            stp = st[p]
            col = lambda i, w=1: stp[:, i:i + w]
            e.tensor_scalar(out=col(C_M), in0=col(C_SUM), scalar1=1.0 / D,
                            scalar2=None, op0=ALU.mult)
            e.tensor_scalar(out=col(C_V), in0=col(C_SQ), scalar1=1.0 / D,
                            scalar2=None, op0=ALU.mult)
            e.tensor_tensor(out=col(C_A), in0=col(C_M), in1=col(C_M),
                            op=ALU.mult)
            e.tensor_tensor(out=col(C_V), in0=col(C_V), in1=col(C_A),
                            op=ALU.subtract)
            e.tensor_scalar(out=col(C_A), in0=col(C_V), scalar1=1e-5,
                            scalar2=None, op0=ALU.add)
            e.reciprocal(out=col(C_ARG), in_=col(C_A))
            e.tensor_tensor(out=col(C_ARG), in0=col(C_V), in1=col(C_ARG),
                            op=ALU.mult)
            e.tensor_scalar(out=col(C_ARG), in0=col(C_ARG), scalar1=1e-5,
                            scalar2=None, op0=ALU.add)
            emit_rsqrt(nc, stp[:, C_A:C_A + 2], stp[:, C_RLN:C_RLN + 2],
                       stp[:, C_X0:C_X0 + 2], stp[:, C_X1:C_X1 + 2],
                       stp[:, C_X2:C_X2 + 2], e=e)
            e.tensor_tensor(out=col(C_CC), in0=col(C_RLN), in1=col(C_RRMS),
                            op=ALU.mult)
            e.tensor_tensor(out=col(C_MC), in0=col(C_M), in1=col(C_CC),
                            op=ALU.mult)
            last = e.tensor_scalar(out=col(C_MC), in0=col(C_MC), scalar1=-1.0,
                                   scalar2=None, op0=ALU.mult)
            if active:
                e.tensor_copy(out=late[x8][:], in_=col(C_RLN))
                e.tensor_tensor(out=col(C_COR), in0=col(C_M), in1=col(C_RLN),
                                op=ALU.mult)
                e.tensor_scalar(out=col(C_COR), in0=col(C_COR), scalar1=-1.0,
                                scalar2=None, op0=ALU.mult)
                e.tensor_copy(out=kb.bfscr[:, 2:3], in_=col(C_COR))
                e.tensor_copy(out=col(C_CH), in_=kb.bfscr[:, 2:3])
                e.tensor_tensor(out=col(C_CL), in0=col(C_COR), in1=col(C_CH),
                                op=ALU.subtract)
                e.tensor_copy(out=stq[x8][:, 0:1], in_=col(C_CH))
                last = e.tensor_copy(out=stq[x8][:, 1:2], in_=col(C_CL))
            return last
        s.task("dve", stats2, reads=(f"Ast{p}",),
               writes=(f"Ast{p}", f"Astq{x8}", f"Alate{x8}", "bfscr"),
               name=f"{t}.stats2", stage=1, chunk=c)
        def tq(e, p=p, x3=x3):
            return e.tensor_scalar(out=tb[:], in0=X1s[x3][:],
                                   scalar1=st[p][:, C_CC:C_CC + 1],
                                   scalar2=st[p][:, C_MC:C_MC + 1],
                                   op0=ALU.mult, op1=ALU.add)
        s.task("pool", tq, reads=(f"AXs{x3}", f"Ast{p}"), writes=("Atb",),
               name=f"{t}.tq", stage=2, chunk=c)
        def ttq(e):
            last = None
            for k in range(8):
                last = e.transpose(out=pb[6][:, k * 128:(k + 1) * 128],
                                   in_=tb[:, k * 128:(k + 1) * 128],
                                   identity=kb.identb[:])
            return last
        s.task("pe", ttq, reads=("Atb", "identb"), writes=("pb6",),
               name=f"{t}.tT", stage=2, chunk=c)
        def evtT(e, p=p):
            e.copy(out=tT[p][:, 0:512], in_=pb[6][:, 0:512])
            return e.copy(out=tT[p][:, 512:1024],
                          in_=pb[6][:, 512:1024])
        s.task("act", evtT, reads=("pb6", "csb", "csbb"),
               writes=(f"AtT{p}",),
               name=f"{t}.evtT", stage=2, chunk=c)
        def kmm(e, p=p):
            last = None
            for k in range(8):
                lt = tT[p][:, k * 128:(k + 1) * 128]
                last = e.matmul(out=pb[0][:], lhsT=lt, rhs=wkv[:, k, 0:512],
                                start=(k == 0), stop=(k == 7),
                                skip_group_check=True)
                last = e.matmul(out=pb[1][:], lhsT=lt, rhs=wkv[:, k, 512:1024],
                                start=(k == 0), stop=(k == 7),
                                skip_group_check=True)
            return last
        s.task("pe", kmm, reads=(f"AtT{p}", "WB"), writes=("pb0", "pb1"),
               name=f"{t}.kmm", stage=3, chunk=c)
        def evK(e, p=p, c=c):
            e.tensor_scalar(out=Kb[p][:, 0:512], in0=pb[0][:],
                            scalar1=kb.maskc[:, c:c + 1], scalar2=None,
                            op0=ALU.mult)
            return e.tensor_scalar(out=Kb[p][:, 512:1024], in0=pb[1][:],
                                   scalar1=kb.maskc[:, c:c + 1], scalar2=None,
                                   op0=ALU.mult)
        s.task("dve", evK, reads=("pb0", "pb1", "maskc"), writes=(f"AKb{p}",),
               name=f"{t}.evK", stage=3, chunk=c)
        def vmm(e, p=p):
            last = None
            for k in range(8):
                lt = tT[p][:, k * 128:(k + 1) * 128]
                last = e.matmul(out=pb[2][:], lhsT=lt, rhs=wvv[:, k, 0:512],
                                start=(k == 0), stop=(k == 7),
                                skip_group_check=True)
                last = e.matmul(out=pb[3][:], lhsT=lt, rhs=wvv[:, k, 512:1024],
                                start=(k == 0), stop=(k == 7),
                                skip_group_check=True)
            return last
        s.task("pe", vmm, reads=(f"AtT{p}", "WB"), writes=("pb2", "pb3"),
               name=f"{t}.vmm", stage=3, chunk=c)
        def evV(e, x4=x4):
            e.copy(out=Vb[x4][:, 0:512], in_=pb[2][:])
            return e.copy(out=Vb[x4][:, 512:1024], in_=pb[3][:])
        s.task("act", evV, reads=("pb2", "pb3", "csb", "csbb"),
               writes=(f"AVb{x4}",), name=f"{t}.evV", stage=3, chunk=c)

        if active:
            def qmm(e, p=p):
                last = None
                for k in range(8):
                    lt = tT[p][:, k * 128:(k + 1) * 128]
                    last = e.matmul(out=pb[0][:], lhsT=lt,
                                    rhs=wqv[:, k, 0:512],
                                    start=(k == 0), stop=(k == 7),
                                    skip_group_check=True)
                    last = e.matmul(out=pb[1][:], lhsT=lt,
                                    rhs=wqv[:, k, 512:1024], start=(k == 0),
                                    stop=(k == 7), skip_group_check=True)
                return last
            s.task("pe", qmm, reads=(f"AtT{p}", "WB"), writes=("pb0", "pb1"),
                   name=f"{t}.qmm", stage=4, chunk=c)
            def evQ(e):
                e.copy(out=Qb[:, 0:512], in_=pb[0][:])
                return e.copy(out=Qb[:, 512:1024], in_=pb[1][:])
            s.task("act", evQ, reads=("pb0", "pb1", "csb", "csbb"),
                   writes=("AQb",), name=f"{t}.evQ", stage=4, chunk=c)
            def tq8(e):
                last = None
                for k in range(8):
                    last = e.transpose(out=pb[6][:, k * 128:(k + 1) * 128],
                                       in_=Qb[:, k * 128:(k + 1) * 128],
                                       identity=kb.identb[:])
                return last
            s.task("pe", tq8, reads=("AQb", "identb"), writes=("pb6",),
                   name=f"{t}.tq8", stage=4, chunk=c)
            def evqT(e, p=p):
                e.copy(out=qT[p][:, 0:512], in_=pb[6][:, 0:512])
                return e.copy(out=qT[p][:, 512:1024],
                              in_=pb[6][:, 512:1024])
            s.task("act", evqT, reads=("pb6", "csb", "csbb"),
                   writes=(f"AqT{p}",),
                   name=f"{t}.evqT", stage=4, chunk=c)
            def tk8(e, p=p):
                last = None
                for k in range(8):
                    last = e.transpose(out=pb[7][:, k * 128:(k + 1) * 128],
                                       in_=Kb[p][:, k * 128:(k + 1) * 128],
                                       identity=kb.identb[:])
                return last
            s.task("pe", tk8, reads=(f"AKb{p}", "identb"), writes=("pb7",),
                   name=f"{t}.tk8", stage=4, chunk=c)
            def evkT(e):
                e.copy(out=kT[:, 0:512], in_=pb[7][:, 0:512])
                return e.copy(out=kT[:, 512:1024],
                              in_=pb[7][:, 512:1024])
            s.task("act", evkT, reads=("pb7", "csb", "csbb"),
                   writes=("AkT",),
                   name=f"{t}.evkT", stage=4, chunk=c)
            def scopy(e):
                e.tensor_copy(out=Ssb[:, 0:512], in_=pb[4][:])
                return e.tensor_copy(out=Ssb[:, 512:1024], in_=pb[5][:])
            s.task("dve", scopy, reads=("pb4", "pb5"), writes=("ASsb",),
                   name=f"{t}.scopy", stage=5, chunk=c)
            def smm(e, p=p):
                last = None
                for hh in range(H):
                    bank = 0 if hh < 4 else 1
                    off = (hh % 4) * 128
                    last = e.matmul(out=pb[bank][:, off:off + 128],
                                    lhsT=kT[:, hh * 128:(hh + 1) * 128],
                                    rhs=qT[p][:, hh * 128:(hh + 1) * 128],
                                    start=True, stop=True,
                                    skip_group_check=True)
                return last
            s.task("pe", smm, reads=("AkT", f"AqT{p}"), writes=("pb0", "pb1"),
                   name=f"{t}.smm", stage=5, chunk=c)
            def mskev(e):
                last = None
                for hh in range(H):
                    bank = 0 if hh < 4 else 1
                    off = (hh % 4) * 128
                    last = e.tensor_tensor(
                        out=atnT[:, hh * 128:(hh + 1) * 128],
                        in0=pb[bank][:, off:off + 128], in1=kb.trimaskT[:],
                        op=ALU.mult)
                return last
            s.task("dve", mskev, reads=("pb0", "pb1", "trimaskT"),
                   writes=("AatnT",), name=f"{t}.mskev", stage=5, chunk=c)
            def omm(e, p=p, x4=x4):
                last = None
                for hh in range(H):
                    bank = 2 if hh < 4 else 3
                    off = (hh % 4) * 128
                    last = e.matmul(out=pb[bank][:, off:off + 128],
                                    lhsT=atnT[:, hh * 128:(hh + 1) * 128],
                                    rhs=Vb[x4][:, hh * 128:(hh + 1) * 128],
                                    start=True, stop=False,
                                    skip_group_check=True)
                    last = e.matmul(out=pb[bank][:, off:off + 128],
                                    lhsT=qT[p][:, hh * 128:(hh + 1) * 128],
                                    rhs=Ssb[:, hh * 128:(hh + 1) * 128],
                                    start=False, stop=True,
                                    skip_group_check=True)
                return last
            s.task("pe", omm, reads=("AatnT", f"AqT{p}", f"AVb{x4}", "ASsb"),
                   writes=("pb2", "pb3"), name=f"{t}.omm", stage=5, chunk=c)
            def oev(e):
                e.copy(out=o_sb[:, 0:512], in_=pb[2][:])
                return e.copy(out=o_sb[:, 512:1024], in_=pb[3][:])
            s.task("act", oev, reads=("pb2", "pb3", "csb", "csbb"),
                   writes=("Aosb",), name=f"{t}.oev", stage=5, chunk=c)
            def osq(e):
                return e.scalar_tensor_tensor(out=o2f[:], in0=o_sb[:],
                                              scalar=1.0, in1=o_sb[:],
                                              op0=ALU.mult, op1=ALU.mult)
            s.task("dve", osq, reads=("Aosb",), writes=("stage",),
                   name=f"{t}.osq", stage=6, chunk=c)
            def ored(e, p=p):
                return e.tensor_reduce(
                    out=st2[p][:, O_SS:O_SS + 8].rearrange(
                        "p (g o) -> p g o", g=8),
                    in_=o2f[:].rearrange("p (g f) -> p g f", g=8),
                    axis=AX.X, op=ALU.add)
            s.task("dve", ored, reads=("stage",), writes=(f"Ast2{p}",),
                   name=f"{t}.ored", stage=6, chunk=c)
            def orstat(e, p=p):
                e = VG(nc)
                e.tensor_scalar(out=st2[p][:, O_ARG:O_ARG + 8],
                                in0=st2[p][:, O_SS:O_SS + 8],
                                scalar1=1.0 / 128, scalar2=1e-5,
                                op0=ALU.mult, op1=ALU.add)
                return emit_rsqrt(nc, st2[p][:, O_ARG:O_ARG + 8],
                                  st2[p][:, O_RS:O_RS + 8],
                                  st2[p][:, O_X0:O_X0 + 8],
                                  st2[p][:, O_X1:O_X1 + 8],
                                  st2[p][:, O_X2:O_X2 + 8], e=e)
            s.task("dve", orstat, reads=(f"Ast2{p}",), writes=(f"Ast2{p}",),
                   name=f"{t}.orstat", stage=6, chunk=c)
            def oscale(e, p=p):
                last = None
                for hh in range(H):
                    last = e.tensor_scalar(
                        out=on[:, hh * 128:(hh + 1) * 128],
                        in0=o_sb[:, hh * 128:(hh + 1) * 128],
                        scalar1=st2[p][:, O_RS + hh:O_RS + hh + 1],
                        scalar2=None, op0=ALU.mult)
                return last
            s.task("dve", oscale, reads=("Aosb", f"Ast2{p}"),
                   writes=("Aon",), name=f"{t}.oscale", stage=6, chunk=c)
            def ton(e):
                last = None
                for k in range(8):
                    last = e.transpose(out=pb[6][:, k * 128:(k + 1) * 128],
                                       in_=on[:, k * 128:(k + 1) * 128],
                                       identity=kb.identb[:])
                return last
            s.task("pe", ton, reads=("Aon", "identb"), writes=("pb6",),
                   name=f"{t}.ton", stage=6, chunk=c)
            def evonT(e):
                e.copy(out=onT[:, 0:512], in_=pb[6][:, 0:512])
                return e.copy(out=onT[:, 512:1024],
                              in_=pb[6][:, 512:1024])
            s.task("act", evonT, reads=("pb6", "csb", "csbb"),
                   writes=("AonT",),
                   name=f"{t}.evonT", stage=6, chunk=c)
            def mint(e, x8=x8):
                return e.transpose(out=pb[7][0:4, 0:128], in_=stq[x8][:],
                                   identity=kb.identb[:])
            s.task("pe", mint, reads=(f"Astq{x8}", "identb"), writes=("pb7",),
                   name=f"{t}.minT", stage=6, chunk=c)
            def evrow(e, p=p):
                return e.tensor_copy(out=strow[p][:], in_=pb[7][0:4, 0:128])
            s.task("dve", evrow, reads=("pb7",), writes=(f"Astrow{p}",),
                   name=f"{t}.evrow", stage=6, chunk=c)
            def womm(e, p=p):
                last = None
                for k in range(8):
                    lt = onT[:, k * 128:(k + 1) * 128]
                    last = e.matmul(out=pb[0][:], lhsT=lt,
                                    rhs=wov[:, k, 0:512],
                                    start=(k == 0), stop=False,
                                    skip_group_check=True)
                    last = e.matmul(out=pb[1][:], lhsT=lt,
                                    rhs=wov[:, k, 512:1024], start=(k == 0),
                                    stop=False, skip_group_check=True)
                last = e.matmul(out=pb[0][:], lhsT=strow[p][0:2, :],
                                rhs=kb.ones2[0:2, 0:512], start=False,
                                stop=True, skip_group_check=True)
                last = e.matmul(out=pb[1][:], lhsT=strow[p][0:2, :],
                                rhs=kb.ones2[0:2, 512:1024], start=False,
                                stop=True, skip_group_check=True)
                return last
            s.task("pe", womm, reads=("AonT", f"Astrow{p}", "WB", "ones2"),
                   writes=("pb0", "pb1"), name=f"{t}.womm", stage=6, chunk=c)
            s.task("qg", lambda e, c=c: e.dma_start(
                out=X1r[:], in_=x1r[c * CH:(c + 1) * CH, :]),
                reads=(f"dX1.{c}",), writes=("AX1r",), name=f"{t}.ldX1r",
                stage=6, chunk=c)
            def y1ev(e, x8=x8):
                e.scalar_tensor_tensor(out=y1[:, 0:512],
                                       in0=X1r[:, 0:512],
                                       scalar=late[x8][:],
                                       in1=pb[0][:], op0=ALU.mult,
                                       op1=ALU.add)
                return e.scalar_tensor_tensor(out=y1[:, 512:1024],
                                              in0=X1r[:, 512:1024],
                                              scalar=late[x8][:],
                                              in1=pb[1][:], op0=ALU.mult,
                                              op1=ALU.add)
            s.task("dve", y1ev, reads=("pb0", "pb1", f"Alate{x8}", "AX1r"),
                   writes=("Ay1",), name=f"{t}.y1ev", stage=6, chunk=c)
            def s1ev(e):
                return e.tensor_tensor(out=s1[:], in0=X1r[:], in1=y1[:],
                                       op=ALU.add)
            s.task("dve", s1ev, reads=("Ay1", "AX1r"), writes=("As1",),
                   name=f"{t}.s1ev", stage=6, chunk=c)
            s.task("qa", lambda e, j=j: e.dma_start(
                out=y1r[j * CH:(j + 1) * CH, :], in_=y1[:]),
                reads=("Ay1",), writes=(f"dY1.{j}",), name=f"{t}.stY1",
                stage=6, chunk=c)
            s.task("qa", lambda e, j=j: e.dma_start(
                out=s1r[j * CH:(j + 1) * CH, :], in_=s1[:]),
                reads=("As1",), writes=(f"dS1.{j}",), name=f"{t}.stS1",
                stage=6, chunk=c)

        if kb.debug == "A1":
            dd = kb.dram
            if c == 0:
                s.task("qg", lambda e, p=p: e.dma_start(out=dd["dbg_K0"],
                                                        in_=Kb[p][:]),
                       reads=(f"AKb{p}",), writes=("ddbg",), name=f"{t}.dK0",
                       stage=4, chunk=c)
                s.task("qg", lambda e, p=p: e.dma_start(out=dd["dbg_V0"],
                                                        in_=Vb[p][:]),
                       reads=(f"AVb{p}",), writes=("ddbg",), name=f"{t}.dV0",
                       stage=4, chunk=c)
                s.task("qg", lambda e: e.dma_start(out=dd["dbg_t0"],
                                                   in_=tb[:]),
                       reads=("Atb",), writes=("ddbg",), name=f"{t}.dt0",
                       stage=2, chunk=c)
            if c == ACT0:
                s.task("qg", lambda e: e.dma_start(out=dd["dbg_S32"],
                                                   in_=Ssb[:]),
                       reads=("ASsb",), writes=("ddbg",), name=f"{t}.dS32",
                       stage=5, chunk=c)
                s.task("qg", lambda e, p=p: e.dma_start(out=dd["dbg_q32"],
                                                        in_=qT[p][:]),
                       reads=(f"AqT{p}",), writes=("ddbg",), name=f"{t}.dq32",
                       stage=5, chunk=c)
                s.task("qg", lambda e: e.dma_start(out=dd["dbg_at32"],
                                                   in_=atnT[:]),
                       reads=("AatnT",), writes=("ddbg",), name=f"{t}.dat32",
                       stage=5, chunk=c)
                s.task("qg", lambda e: e.dma_start(out=dd["dbg_on32"],
                                                   in_=on[:]),
                       reads=("Aon",), writes=("ddbg",), name=f"{t}.don32",
                       stage=6, chunk=c)
        def supd(e, p=p, c=c, x4=x4):
            last = None
            for hh in range(H):
                bank = 4 if hh < 4 else 5
                off = (hh % 4) * 128
                last = e.matmul(out=pb[bank][:, off:off + 128],
                                lhsT=Kb[p][:, hh * 128:(hh + 1) * 128],
                                rhs=Vb[x4][:, hh * 128:(hh + 1) * 128],
                                start=(c == 0 and hh % 4 == 0),
                                stop=(c == NS - 1),
                                skip_group_check=True)
            return last
        s.task("pe", supd, reads=(f"AKb{p}", f"AVb{x4}"),
               writes=("pb4", "pb5"), name=f"{t}.supd", stage=5, chunk=c)
    s.flush_stages()


def phase_a2a(kb):
    nc, s, dr = kb.nc, kb.s, kb.dram
    SC = kb.SC
    SC.reset(kb.p1_sc_base)
    s.fence()
    pb = kb.pb

    kb.WA.reset()
    wgv = kb.WA.view([128, 8, IN], BF16)
    kb.WB.reset()
    wuv = kb.WB.view([128, 8, IN], BF16)
    wgr = dr["wgate"].rearrange("(c p) n -> p c n", p=128)
    wur = dr["wup"].rearrange("(c p) n -> p c n", p=128)
    s.task("qg", lambda e: e.dma_start(out=wgv[:], in_=wgr),
           writes=("WA",), name="a2_ldwg", nofence=True)
    s.task("qg", lambda e: e.dma_start(out=wuv[:], in_=wur),
           writes=("WB",), name="a2_ldwu", nofence=True)

    Y1s = [SC.view([128, D], F32) for _ in range(2)]
    zb = SC.view([128, D], BF16)
    zT = [SC.view([128, D], BF16) for _ in range(2)]
    sg = [SC.view([128, 512], F32) for _ in range(2)]
    g = [SC.view([128, IN], BF16) for _ in range(2)]
    dumm = SC.view([128, D], F32)
    st = [SC.view([128, 8], F32) for _ in range(2)]
    Z_SQ, Z_ARG, Z_RR, Z_X0, Z_X1, Z_X2 = 0, 1, 2, 3, 4, 5

    y1r, gr = dr["Y1"], dr["G"]

    s.begin_stages()
    for j in range(NACT):
        p = j % 2
        t = f"A2a.{j}"
        s.task("qs", lambda e, j=j, p=p: e.dma_start(
            out=Y1s[p][:], in_=y1r[j * CH:(j + 1) * CH, :]),
            reads=(f"dY1.{j}",), writes=(f"BYs{p}",), name=f"{t}.ld",
            stage=0, chunk=j)
        def stats(e, p=p):
            e = VG(nc)
            e.scalar_tensor_tensor(out=dumm[:], in0=Y1s[p][:], scalar=1.0,
                                   in1=Y1s[p][:], op0=ALU.mult, op1=ALU.mult,
                                   accum_out=st[p][:, Z_SQ:Z_SQ + 1])
            e.tensor_scalar(out=st[p][:, Z_ARG:Z_ARG + 1],
                            in0=st[p][:, Z_SQ:Z_SQ + 1], scalar1=1.0 / D,
                            scalar2=1e-5, op0=ALU.mult, op1=ALU.add)
            return emit_rsqrt(nc, st[p][:, Z_ARG:Z_ARG + 1],
                              st[p][:, Z_RR:Z_RR + 1],
                              st[p][:, Z_X0:Z_X0 + 1],
                              st[p][:, Z_X1:Z_X1 + 1],
                              st[p][:, Z_X2:Z_X2 + 1], e=e)
        s.task("dve", stats, reads=(f"BYs{p}",), writes=(f"Bst{p}", "Bdumm"),
               name=f"{t}.stats", stage=1, chunk=j)
        def zq(e, p=p):
            return e.scalar_tensor_tensor(out=zb[:], in0=Y1s[p][:],
                                          scalar=st[p][:, Z_RR:Z_RR + 1],
                                          in1=kb.mnwb[:], op0=ALU.mult,
                                          op1=ALU.mult)
        s.task("dve", zq, reads=(f"BYs{p}", f"Bst{p}", "mnwb"),
               writes=("Bzb",), name=f"{t}.zq", stage=2, chunk=j)
        def tz(e):
            last = None
            for k in range(8):
                last = e.transpose(out=pb[6][:, k * 128:(k + 1) * 128],
                                   in_=zb[:, k * 128:(k + 1) * 128],
                                   identity=kb.identb[:])
            return last
        s.task("pe", tz, reads=("Bzb", "identb"), writes=("pb6",),
               name=f"{t}.tz", stage=2, chunk=j)
        def evzT(e, p=p):
            e.copy(out=zT[p][:, 0:512], in_=pb[6][:, 0:512])
            return e.copy(out=zT[p][:, 512:1024],
                          in_=pb[6][:, 512:1024])
        s.task("act", evzT, reads=("pb6", "csb", "csbb"),
               writes=(f"BzT{p}",),
               name=f"{t}.evzT", stage=2, chunk=j)
        for sl in range(8):
            bG = 0 if sl % 2 == 0 else 2
            bU = 1 if sl % 2 == 0 else 3
            def gu(e, p=p, sl=sl, bG=bG, bU=bU):
                last = None
                for k in range(8):
                    lt = zT[p][:, k * 128:(k + 1) * 128]
                    last = e.matmul(out=pb[bG][:], lhsT=lt,
                                    rhs=wgv[:, k, sl * 512:(sl + 1) * 512],
                                    start=(k == 0), stop=(k == 7),
                                    skip_group_check=True)
                    last = e.matmul(out=pb[bU][:], lhsT=lt,
                                    rhs=wuv[:, k, sl * 512:(sl + 1) * 512],
                                    start=(k == 0), stop=(k == 7),
                                    skip_group_check=True)
                return last
            s.task("pe", gu, reads=(f"BzT{p}", "WA", "WB"),
                   writes=(f"pb{bG}", f"pb{bU}"), name=f"{t}.gu{sl}",
                   stage=3, chunk=j)
            def sil(e, p=p, bG=bG):
                return e.activation(out=sg[p][:], in_=pb[bG][:], func=AF.Silu)
            s.task("act", sil, reads=(f"pb{bG}", "csb", "csbb"),
                   writes=(f"Bsg{p}",), name=f"{t}.sil{sl}", stage=3, chunk=j)
            def gm(e, p=p, sl=sl, bU=bU):
                return e.tensor_tensor(out=g[p][:, sl * 512:(sl + 1) * 512],
                                       in0=sg[p][:], in1=pb[bU][:],
                                       op=ALU.mult)
            s.task("dve", gm, reads=(f"Bsg{p}", f"pb{bU}"),
                   writes=(f"Bg{p}",), name=f"{t}.gm{sl}", stage=3, chunk=j)
        s.task("qg", lambda e, j=j, p=p: e.dma_start(
            out=gr[j * CH:(j + 1) * CH, :], in_=g[p][:]),
            reads=(f"Bg{p}",), writes=(f"dG.{j}",), name=f"{t}.stG",
            stage=4, chunk=j)
    s.flush_stages()


def phase_a2b(kb):
    nc, s, dr = kb.nc, kb.s, kb.dram
    SC = kb.SC
    SC.reset(kb.p1_sc_base)
    s.fence()
    pb = kb.pb

    kb.WB.reset()
    wdv = kb.WB.view([128, 32, D], BF16)
    wdr = dr["wdown"].rearrange("(c p) n -> p c n", p=128)
    s.task("qg", lambda e: e.dma_start(out=wdv[:], in_=wdr),
           writes=("WB",), name="a2b_ldwd", nofence=True)

    g2 = [SC.view([128, IN], BF16) for _ in range(2)]
    gT = [SC.view([128, IN], BF16) for _ in range(2)]
    S1s = [SC.view([128, D], F32) for _ in range(4)]
    X2o_ = SC.view([128, D], F32)
    X2o = [X2o_, X2o_]

    gr, s1r, x2r = dr["G"], dr["S1"], dr["X2"]

    s.begin_stages()
    for j in range(NACT):
        p = j % 2
        x4 = j % 4
        t = f"A2b.{j}"
        s.task("qs", lambda e, j=j, p=p: e.dma_start(
            out=g2[p][:], in_=gr[j * CH:(j + 1) * CH, :]),
            reads=(f"dG.{j}",), writes=(f"Cg{p}",), name=f"{t}.ldg",
            stage=0, chunk=j)
        s.task("qa", lambda e, j=j, x4=x4: e.dma_start(
            out=S1s[x4][:], in_=s1r[j * CH:(j + 1) * CH, :]),
            reads=(f"dS1.{j}",), writes=(f"CS1{x4}",), name=f"{t}.lds",
            stage=0, chunk=j)
        for r in range(4):
            bank = 6 if r % 2 == 0 else 7
            def tg(e, p=p, r=r, bank=bank):
                last = None
                for tt_ in range(8):
                    kk = r * 8 + tt_
                    last = e.transpose(
                        out=pb[bank][:, tt_ * 128:(tt_ + 1) * 128],
                        in_=g2[p][:, kk * 128:(kk + 1) * 128],
                        identity=kb.identb[:])
                return last
            s.task("pe", tg, reads=(f"Cg{p}", "identb"),
                   writes=(f"pb{bank}",), name=f"{t}.tg{r}", stage=1, chunk=j)
            def evgT(e, p=p, r=r, bank=bank):
                return e.copy(out=gT[p][:, r * 1024:(r + 1) * 1024],
                              in_=pb[bank][:])
            s.task("act", evgT, reads=(f"pb{bank}", "csb", "csbb"),
                   writes=(f"CgT{p}",),
                   name=f"{t}.evgT{r}", stage=1, chunk=j)
        def dmm(e, p=p):
            last = None
            for k in range(32):
                lt = gT[p][:, k * 128:(k + 1) * 128]
                last = e.matmul(out=pb[0][:], lhsT=lt, rhs=wdv[:, k, 0:512],
                                start=(k == 0), stop=(k == 31),
                                skip_group_check=True)
                last = e.matmul(out=pb[1][:], lhsT=lt,
                                rhs=wdv[:, k, 512:1024],
                                start=(k == 0), stop=(k == 31),
                                skip_group_check=True)
            return last
        s.task("pe", dmm, reads=(f"CgT{p}", "WB"), writes=("pb0", "pb1"),
               name=f"{t}.dmm", stage=2, chunk=j)
        def x2ev(e, p=p, x4=x4):
            e.tensor_tensor(out=X2o[p][:, 0:512], in0=S1s[x4][:, 0:512],
                            in1=pb[0][:], op=ALU.add)
            return e.tensor_tensor(out=X2o[p][:, 512:1024],
                                   in0=S1s[x4][:, 512:1024], in1=pb[1][:],
                                   op=ALU.add)
        s.task("dve", x2ev, reads=("pb0", "pb1", f"CS1{x4}"),
               writes=("CX2",), name=f"{t}.x2ev", stage=3, chunk=j)
        s.task("qs", lambda e, j=j, p=p: e.dma_start(
            out=x2r[j * CH:(j + 1) * CH, :], in_=X2o[p][:]),
            reads=("CX2",), writes=(f"dX2.{j}",), name=f"{t}.stX2",
            stage=3, chunk=j)
    s.flush_stages()



_CACHE = {}


def _build(debug=None):
    if debug not in _CACHE:
        _CACHE[debug] = build_program(debug)
    return _CACHE[debug]


def make_in_maps(inputs):
    X = np.ascontiguousarray(np.asarray(inputs["L0_attn_hidden_states"],
                                        dtype=np.float32))
    B, T, _ = X.shape
    w = {k: np.ascontiguousarray(np.asarray(v, dtype=np.float32))
         for k, v in inputs.items() if k != "L0_attn_hidden_states"}

    consts = np.zeros((128, 8), np.float32)
    consts[:, 1] = 1.0
    constsb = np.zeros((128, 4), ml_dtypes.bfloat16)
    constsb[:, 1] = 1.0
    base = {
        "consts": consts, "constsb": np.asarray(constsb),
        "identb": np.asarray(np.eye(128, dtype=ml_dtypes.bfloat16)),
        "trimaskT": np.asarray(np.triu(np.ones((128, 128), np.float32))
                               .astype(ml_dtypes.bfloat16)),
        "ones2": np.asarray(np.ones((2, D), ml_dtypes.bfloat16)),
        "mnwb": np.asarray(np.tile(
            np.asarray(w["L0_attn_mlp_norm_w"], np.float32)[None, :],
            (128, 1)).astype(ml_dtypes.bfloat16)),
        "anw": np.ascontiguousarray(
            np.asarray(w["L0_attn_attn_norm_w"], np.float32)
            .reshape(8, 128).T),
        "onw": np.ascontiguousarray(
            np.tile(np.asarray(w["L0_attn_attn_out_norm_w"], np.float32), 8)
            .reshape(8, 128).T),
        "wq": w["L0_attn_Wq"], "wk": w["L0_attn_Wk"], "wv": w["L0_attn_Wv"],
        "wo": w["L0_attn_Wo"], "wgate": w["L0_attn_Wgate"],
        "wup": w["L0_attn_Wup"], "wdown": w["L0_attn_Wdown"],
        "w1t": np.ascontiguousarray(w["L0_ffn_w1"].T),
        "w2t": np.ascontiguousarray(w["L0_ffn_w2"].T),
        "b1": np.ascontiguousarray(w["L0_ffn_b1"].reshape(1, IN)),
        "b2": np.ascontiguousarray(w["L0_ffn_b2"].reshape(1, D)),
    }

    half = T // 2
    in_maps = []
    for b in range(B):
        for role in range(2):
            m = dict(base)
            if role == 0:
                xin = np.concatenate(
                    [np.zeros((half, D), np.float32), X[b, :half]], axis=0)
                mk = np.concatenate([np.zeros(half, np.float32),
                                     np.ones(half, np.float32)])
            else:
                xin = X[b]
                mk = np.ones(T, np.float32)
            m["x"] = np.ascontiguousarray(xin)
            m["maskc"] = np.ascontiguousarray(mk.reshape(NS, CH).T)
            in_maps.append(m)
    return X, in_maps


def kernel(**inputs):
    X, in_maps = make_in_maps(inputs)
    B, T, _ = X.shape
    kb = _build(DEBUG_PHASE)
    res = run_bass_kernel_spmd(kb.nc, in_maps, core_ids=list(range(8)))
    if DEBUG_PHASE is not None:
        return [res.results[c] for c in range(8)]
    half = T // 2
    out = np.empty_like(X)
    for b in range(B):
        out[b, :half] = res.results[2 * b]["out"]
        out[b, half:] = res.results[2 * b + 1]["out"]
    return out


# version 06 (final)
# speedup vs baseline: 1.0497x; 1.0145x over previous
import numpy as np
import ml_dtypes
from contextlib import ExitStack

import concourse.bass as bass
import concourse.mybir as mybir
from concourse.bass_utils import run_bass_kernel_spmd

F32 = mybir.dt.float32
BF16 = mybir.dt.bfloat16
I16 = mybir.dt.int16
U8 = mybir.dt.uint8
AF = mybir.ActivationFunctionType
ALU = mybir.AluOpType
AX = mybir.AxisListType

D = 1024
IN = 4096
H = 8
CH = 128
NS = 64
ACT0 = 32
NACT = NS - ACT0
MAGIC = float(np.float32(3 << 22))
QK = float(0x5F37)

BODY_OF = {"pe": "tensor", "dve": "vector", "act": "scalar", "pool": "gpsimd",
           "qs": "sync", "qa": "scalar", "qg": "gpsimd"}
IS_DMA = {"pe": False, "dve": False, "act": False, "pool": False,
          "qs": True, "qa": True, "qg": True}

DEBUG_PHASE = None
SELF_DEPS = True
CHAIN_SYNC = True


class Task:
    __slots__ = ("ekey", "idx", "emit", "deps", "name", "reads", "writes")

    def __init__(self, ekey, idx, emit, name):
        self.ekey = ekey
        self.idx = idx
        self.emit = emit
        self.deps = {}
        self.name = name
        self.reads = set()
        self.writes = set()


class Sched:
    def __init__(self):
        self.counts = {k: 0 for k in BODY_OF}
        self.body_items = {b: [] for b in set(BODY_OF.values())}
        self.res = {}
        self.fence_snap = None

    def fence(self):
        self.fence_snap = dict(self.counts)

    def begin_stages(self):
        self._stq = []

    def flush_stages(self):
        q = self._stq
        self._stq = None
        if not q:
            return
        from collections import defaultdict
        bych = defaultdict(list)
        maxc = maxs = 0
        for c, st_, rec in q:
            bych[(c, st_)].append(rec)
            maxc = max(maxc, c)
            maxs = max(maxs, st_)
        for t in range(maxc + maxs + 1):
            for st_ in range(maxs, -1, -1):
                c = t - st_
                for rec in bych.get((c, st_), ()):
                    self.task(*rec)

    def task(self, ekey, emit, reads=(), writes=(), name="", nofence=False,
             stage=None, chunk=None):
        if stage is not None and getattr(self, "_stq", None) is not None:
            self._stq.append((chunk, stage,
                              (ekey, emit, reads, writes, name, nofence)))
            return None
        idx = self.counts[ekey]
        self.counts[ekey] += 1
        t = Task(ekey, idx, emit, name)

        def dep(tok):
            if tok is None or tok == mytok:
                return
            ek, ix = tok
            if ek == ekey and (IS_DMA[ek] or ek == "pe" or not SELF_DEPS):
                return
            need = ix + 1
            if t.deps.get(ek, 0) < need:
                t.deps[ek] = need

        mytok = (ekey, idx)
        if self.fence_snap is not None and not nofence:
            for ek, cnt in self.fence_snap.items():
                if cnt > 0 and ek != ekey and t.deps.get(ek, 0) < cnt:
                    t.deps[ek] = cnt

        t.reads = set(reads)
        t.writes = set(writes)
        for r in reads:
            e = self.res.get(r)
            if e is None:
                e = self.res[r] = [None, []]
            dep(e[0])
            e[1].append(mytok)
        for wname in writes:
            e = self.res.get(wname)
            if e is not None:
                dep(e[0])
                for rt in e[1]:
                    dep(rt)
            self.res[wname] = [mytok, []]
        self.body_items[BODY_OF[ekey]].append(t)
        return mytok

    def emit_all(self, nc, es, junk):
        sems = {k: es.enter_context(nc.semaphore(f"sem_{k}")) for k in BODY_OF}
        blk = es.enter_context(nc.Block())
        total = {k: self.counts[k] * (16 if IS_DMA[k] else 1) for k in BODY_OF}
        watermark = {b: {k: 0 for k in BODY_OF} for b in self.body_items}

        def run_body(body, engobj):
            wm = watermark[body]
            for t in self.body_items[body]:
                for ek in sorted(t.deps):
                    cnt = t.deps[ek]
                    val = cnt * 16 if IS_DMA[ek] else cnt
                    if wm[ek] >= val:
                        continue
                    engobj.wait_ge(sems[ek], val)
                    wm[ek] = val
                last = t.emit(engobj)
                assert last is not None, f"task {t.name} returned no instruction"
                last.then_inc(sems[t.ekey], 16 if IS_DMA[t.ekey] else 1)
            if body == "gpsimd":
                for k in ("qs", "qa", "qg", "pe", "dve", "act", "pool"):
                    if total[k] and wm[k] < total[k]:
                        engobj.wait_ge(sems[k], total[k])

        blk.sync(lambda e: run_body("sync", e))
        blk.vector(lambda e: run_body("vector", e))
        blk.scalar(lambda e: run_body("scalar", e))
        blk.tensor(lambda e: run_body("tensor", e))
        blk.gpsimd(lambda e: run_body("gpsimd", e))


class Arena:
    def __init__(self, kb, name, nbytes):
        self.name = name
        self.t = kb.es.enter_context(kb.nc.sbuf_tensor(name, [128, nbytes], U8))
        self.nbytes = nbytes
        self.off = 0

    def reset(self, off=0):
        self.off = off

    def view(self, shape, dt):
        esz = 2 if dt in (BF16, I16) else 4
        ncols0 = int(np.prod(shape[1:]))
        align = 64 if ncols0 * esz >= 512 else 8
        self.off = (self.off + align - 1) & ~(align - 1)
        ncols = ncols0
        need = ncols * esz
        assert self.off + need <= self.nbytes, \
            f"arena {self.name} overflow: {self.off}+{need}>{self.nbytes}"
        base = self.off // esz
        self.off += need
        ap = self.t.ap().bitcast(dt)
        v = ap[0:shape[0], base:base + ncols]
        if len(shape) == 3:
            v = v.rearrange("p (a b) -> p a b", a=shape[1])
        return v


class KB:
    def __init__(self, debug=None):
        self.nc = bass.Bass("TRN2", target_bir_lowering=False)
        self.s = Sched()
        self.es = ExitStack()
        self.debug = debug
        self.dram = {}
        self.pb = []

    def sb(self, name, shape, dt):
        return self.es.enter_context(self.nc.sbuf_tensor(name, shape, dt))

    def dram_t(self, name, shape, dt, kind):
        t = self.nc.dram_tensor(name, shape, dt, kind=kind)
        self.dram[name] = t.ap()
        return self.dram[name]


_FILL = {"t": None}


def set_filler(ap):
    _FILL["t"] = ap


def vsep(nc):
    nc.vector.memset(_FILL["t"], 0.0)


_CHAIN = {"sem": None, "cnt": 0}


def set_chain_sem(sem):
    _CHAIN["sem"] = sem
    _CHAIN["cnt"] = 0


class VG:

    def __init__(self, nc):
        self._nc = nc
        self._pending = None

    def __getattr__(self, name):
        f = getattr(self._nc.vector, name)
        nc = self._nc

        def g(*a, _indep=False, **k):
            if self._pending is not None and not _indep:
                if CHAIN_SYNC:
                    self._pending.then_inc(_CHAIN["sem"], 1)
                    _CHAIN["cnt"] += 1
                    nc.vector.wait_ge(_CHAIN["sem"], _CHAIN["cnt"])
                else:
                    vsep(nc)
            r = f(*a, **k)
            self._pending = r
            return r
        return g


def emit_rsqrt(nc, v_ap, y_ap, s0, s1, s2, e=None):
    if e is None:
        e = VG(nc)
    e.memset(s0, 0.0)
    hi = v_ap.bitcast(I16)[:, 1::2]
    e.tensor_copy(out=s1, in_=hi, _indep=True)
    e.tensor_scalar(out=s1, in0=s1, scalar1=-0.5, scalar2=QK,
                    op0=ALU.mult, op1=ALU.add)
    e.tensor_copy(out=s0.bitcast(I16)[:, 1::2], in_=s1)
    e.tensor_scalar(out=s2, in0=v_ap, scalar1=-0.5, scalar2=None,
                    op0=ALU.mult, _indep=True)
    cur = s0
    last = None
    for it in range(2):
        dst = y_ap if it == 1 else (s1 if cur is s0 else s0)
        e.tensor_tensor(out=dst, in0=cur, in1=cur, op=ALU.mult)
        e.tensor_tensor(out=dst, in0=dst, in1=s2, op=ALU.mult)
        e.tensor_scalar(out=dst, in0=dst, scalar1=1.5, scalar2=None,
                        op0=ALU.add)
        last = e.tensor_tensor(out=dst, in0=cur, in1=dst, op=ALU.mult)
        cur = dst
    return last



def build_program(debug=None):
    kb = KB(debug)
    nc = kb.nc

    kb.dram_t("x", [NS * CH, D], F32, "ExternalInput")
    kb.dram_t("maskc", [CH, NS], F32, "ExternalInput")
    kb.dram_t("consts", [128, 8], F32, "ExternalInput")
    kb.dram_t("constsb", [128, 4], BF16, "ExternalInput")
    kb.dram_t("identb", [128, 128], BF16, "ExternalInput")
    kb.dram_t("trimaskT", [128, 128], BF16, "ExternalInput")
    kb.dram_t("ones2", [2, D], BF16, "ExternalInput")
    kb.dram_t("mnwb", [128, D], BF16, "ExternalInput")
    kb.dram_t("anw", [128, 8], F32, "ExternalInput")
    kb.dram_t("onw", [128, 8], F32, "ExternalInput")
    for nm, shp in (("wq", [D, D]), ("wk", [D, D]), ("wv", [D, D]),
                    ("wo", [D, D]), ("wgate", [D, IN]), ("wup", [D, IN]),
                    ("wdown", [IN, D]), ("w1t", [D, IN]), ("w2t", [IN, D]),
                    ("b1", [1, IN]), ("b2", [1, D])):
        kb.dram_t(nm, shp, F32, "ExternalInput")

    dbg = debug
    kb.dram_t("X1", [NS * CH, D], F32,
              "ExternalOutput" if dbg == "P1" else "Internal")
    kb.dram_t("Y1", [NACT * CH, D], F32,
              "ExternalOutput" if dbg == "A1" else "Internal")
    kb.dram_t("S1", [NACT * CH, D], F32,
              "ExternalOutput" if dbg == "A1" else "Internal")
    kb.dram_t("G", [NACT * CH, IN], BF16, "Internal")
    kb.dram_t("X2", [NACT * CH, D], F32,
              "ExternalOutput" if dbg == "A2" else "Internal")
    if dbg is None:
        kb.dram_t("out", [NACT * CH, D], F32, "ExternalOutput")
    kb.dram_t("WQ1d", [D, IN], BF16, "Internal")
    kb.dram_t("WQ2d", [IN, D], BF16, "Internal")
    if dbg == "A1":
        kb.dram_t("dbg_K0", [128, D], BF16, "ExternalOutput")
        kb.dram_t("dbg_V0", [128, D], BF16, "ExternalOutput")
        kb.dram_t("dbg_t0", [128, D], BF16, "ExternalOutput")
        kb.dram_t("dbg_S32", [128, D], BF16, "ExternalOutput")
        kb.dram_t("dbg_q32", [128, D], BF16, "ExternalOutput")
        kb.dram_t("dbg_at32", [128, D], BF16, "ExternalOutput")
        kb.dram_t("dbg_on32", [128, D], BF16, "ExternalOutput")
    if dbg == "P1":
        kb.dram_t("dbg_xq", [2 * 128, D], BF16, "ExternalOutput")
        kb.dram_t("dbg_xqT", [2 * 128, D], BF16, "ExternalOutput")
        kb.dram_t("dbg_h", [2 * 128, IN], F32, "ExternalOutput")
        kb.dram_t("dbg_st", [2 * 128, 32], F32, "ExternalOutput")
        kb.dram_t("dbg_st2", [2 * 128, 32], F32, "ExternalOutput")
        kb.dram_t("dbg_strow", [8, 128], BF16, "ExternalOutput")
        kb.dram_t("dbg_wq1", [128, 1024], BF16, "ExternalOutput")
        kb.dram_t("dbg_swb", [128, 2], F32, "ExternalOutput")

    with kb.es:
        kb.csb = kb.sb("s_csb", [128, 8], F32)
        kb.csbb = kb.sb("s_csbb", [128, 4], BF16)
        nc.const_aps.register(F32, 0.0, kb.csb[:, 0:1])
        nc.const_aps.register(F32, 1.0, kb.csb[:, 1:2])
        nc.const_aps.register(BF16, 0.0, kb.csbb[:, 0:1])
        nc.const_aps.register(BF16, 1.0, kb.csbb[:, 1:2])
        kb.identb = kb.sb("s_identb", [128, 128], BF16)
        kb.trimaskT = kb.sb("s_trimaskT", [128, 128], BF16)
        kb.maskc = kb.sb("s_maskc", [128, NS], F32)
        kb.ones2 = kb.sb("s_ones2", [2, D], BF16)
        kb.mnwb = kb.sb("s_mnwb", [128, D], BF16)
        kb.anw = kb.sb("s_anw", [128, 8], F32)
        kb.onw = kb.sb("s_onw", [128, 8], F32)
        kb.swb = kb.sb("swb", [128, 2], F32)
        kb.bfscr = kb.sb("bfscr", [128, 4], BF16)
        kb.b1x = kb.sb("b1x", [3, IN], BF16)
        kb.b2x = kb.sb("b2x", [3, D], BF16)
        kb.brow = kb.sb("brow", [1, 512], F32)
        kb.brow2 = kb.sb("brow2", [1, 512], F32)
        kb.browb = kb.sb("browb", [1, 512], BF16)
        kb.macc = kb.sb("macc", [128, 4], F32)
        kb.onescol = kb.sb("onescol", [128, 1], F32)
        kb.cell = kb.sb("cell", [1, 4], F32)
        kb.onesrow_f = kb.sb("onesrow_f", [1, 128], F32)
        kb.jdve = kb.sb("jdve", [128, 1], F32)
        kb.jact = kb.sb("jact", [128, 1], F32)
        kb.jpool = kb.sb("jpool", [128, 1], F32)
        set_filler(kb.jdve[:])
        kb.vch = kb.es.enter_context(nc.semaphore("vch"))
        set_chain_sem(kb.vch)

        for i in range(6):
            kb.pb.append(kb.es.enter_context(
                nc.psum_tensor(f"pb{i}", [128, 512], F32)))
        for i in range(6, 8):
            kb.pb.append(kb.es.enter_context(
                nc.psum_tensor(f"pb{i}", [128, 1024], BF16)))

        kb.WA = Arena(kb, "WA", 65536)
        kb.WB = Arena(kb, "WB", 65536)
        kb.SC = Arena(kb, "SC", 60608)

        startup(kb)
        prep_ffn_weights(kb)
        phase_ffn(kb, src="x", dst="X1", nslots=NS, tag="P1", first=True)
        if dbg != "P1":
            phase_a1(kb)
            if dbg != "A1":
                phase_a2a(kb)
                phase_a2b(kb)
                if dbg != "A2":
                    phase_ffn(kb, src="X2", dst="out", nslots=NACT, tag="P2",
                              first=False)
        kb.s.emit_all(nc, kb.es, (kb.jdve[:], kb.jact[:], kb.jpool[:]))
    return kb


def startup(kb):
    nc, s, dr = kb.nc, kb.s, kb.dram

    def ld(dst, src_name, res):
        s.task("qs", lambda e, d=dst, sn=src_name: e.dma_start(out=d,
                                                               in_=dr[sn]),
               writes=(res,), name=f"ld_{src_name}")

    ld(kb.csb[:], "consts", "csb")
    ld(kb.csbb[:], "constsb", "csbb")
    ld(kb.identb[:], "identb", "identb")
    ld(kb.trimaskT[:], "trimaskT", "trimaskT")
    ld(kb.maskc[:], "maskc", "maskc")
    ld(kb.ones2[:], "ones2", "ones2")
    ld(kb.mnwb[:], "mnwb", "mnwb")
    ld(kb.anw[:], "anw", "anw")
    ld(kb.onw[:], "onw", "onw")
    s.task("dve", lambda e: e.memset(kb.onescol[:], 1.0), writes=("onescol",),
           name="ones_col")
    s.task("dve", lambda e: e.memset(kb.onesrow_f[:], 1.0), writes=("onesrow",),
           name="ones_row")

    for bname, btile, wdt in (("b1", kb.b1x, IN), ("b2", kb.b2x, D)):
        for sl in range(wdt // 512):
            c0 = sl * 512
            s.task("qs", lambda e, bn=bname, c0=c0: e.dma_start(
                out=kb.brow[:], in_=dr[bn][0:1, c0:c0 + 512]),
                writes=("brow",), name=f"ldb_{bname}_{sl}")
            def hl(e):
                e = VG(nc)
                e.tensor_copy(out=kb.browb[:], in_=kb.brow[:])
                e.tensor_copy(out=kb.brow2[:], in_=kb.browb[:])
                return e.tensor_tensor(out=kb.brow2[:], in0=kb.brow[:],
                                       in1=kb.brow2[:], op=ALU.subtract)
            s.task("dve", hl, reads=("brow",), writes=("browb", "brow2"),
                   name=f"bhl_{bname}_{sl}")
            def p_hi(e, bt=btile, c0=c0):
                return e.dma_start(out=bt[0:1, c0:c0 + 512],
                                   in_=kb.browb[0:1, :])
            s.task("qg", p_hi, reads=("browb",), writes=(f"bx_{bname}",),
                   name=f"bhi_{bname}_{sl}")
            def p_hi2(e, bt=btile, c0=c0):
                return e.dma_start(out=bt[2:3, c0:c0 + 512],
                                   in_=kb.browb[0:1, :])
            s.task("qg", p_hi2, reads=("browb",), writes=(f"bx_{bname}",),
                   name=f"bhi2_{bname}_{sl}")
            def lo_conv(e):
                return e.tensor_copy(out=kb.browb[:], in_=kb.brow2[:])
            s.task("dve", lo_conv, reads=("brow2",), writes=("browb",),
                   name=f"blc_{bname}_{sl}")
            def p_lo(e, bt=btile, c0=c0):
                return e.dma_start(out=bt[1:2, c0:c0 + 512],
                                   in_=kb.browb[0:1, :])
            s.task("qg", p_lo, reads=("browb",), writes=(f"bx_{bname}",),
                   name=f"blo_{bname}_{sl}")


def prep_ffn_weights(kb):
    nc, s, dr = kb.nc, kb.s, kb.dram
    kb.wq1s = kb.WA.view([128, 8, IN], BF16)
    kb.wq1f = kb.wq1s.rearrange("p a b -> p (a b)")
    kb.wq2s = kb.WB.view([128, 32, D], BF16)
    kb.wq2f = kb.wq2s.rearrange("p a b -> p (a b)")
    stage = kb.SC.view([128, 1024], F32)
    kb.stage_view = stage
    kb.p1_sc_base = kb.SC.off

    w1f = dr["w1t"].rearrange("(c p) n -> p c n", p=128)
    w2f = dr["w2t"].rearrange("(c p) n -> p c n", p=128)
    sp1 = dr["WQ1d"].rearrange("(c p) n -> p c n", p=128)
    sp2 = dr["WQ2d"].rearrange("(c p) n -> p c n", p=128)

    def dtile(w3, wdt, hh):
        if wdt >= 1024:
            ch, off = (hh * 1024) // wdt, (hh * 1024) % wdt
            return w3[:, ch, off:off + 1024]
        raise AssertionError

    s.task("dve", lambda e: e.memset(kb.macc[:], 0.0), writes=("macc",),
           name="macc0")
    NT = (8 * IN) // 1024
    for wi, (wf, wdt) in ((0, (w1f, IN)), (1, (w2f, D))):
        for hh in range(NT):
            srcv = dtile(wf, wdt, hh)
            s.task("qs", lambda e, sc=srcv: e.dma_start(out=stage[:], in_=sc),
                   writes=("stage",), name=f"mw_ld{wi}_{hh}", nofence=True)
            def red(e, wi=wi):
                e = VG(nc)
                e.tensor_reduce(out=kb.macc[:, 2 + wi:3 + wi], in_=stage[:],
                                axis=AX.X, op=ALU.add, apply_absolute_value=True)
                return e.tensor_tensor(out=kb.macc[:, wi:wi + 1],
                                       in0=kb.macc[:, wi:wi + 1],
                                       in1=kb.macc[:, 2 + wi:3 + wi],
                                       op=ALU.add)
            s.task("dve", red, reads=("stage",), writes=("macc",),
                   name=f"mw_red{wi}_{hh}")
    def totmm(e):
        return e.matmul(out=kb.pb[0][0:1, 0:2], lhsT=kb.onescol[:],
                        rhs=kb.macc[:, 0:2], start=True, stop=True)
    s.task("pe", totmm, reads=("macc", "onescol"), writes=("pb0",),
           name="mw_tot")
    def swcalc(e):
        e = VG(nc)
        e.tensor_copy(out=kb.cell[0:1, 0:2], in_=kb.pb[0][0:1, 0:2])
        e.tensor_scalar(out=kb.cell[0:1, 0:2], in0=kb.cell[0:1, 0:2],
                        scalar1=1.0 / (D * IN), scalar2=1e-5,
                        op0=ALU.mult, op1=ALU.max)
        return e.reciprocal(out=kb.cell[0:1, 0:2], in_=kb.cell[0:1, 0:2])
    s.task("dve", swcalc, reads=("pb0",), writes=("cell",), name="sw_calc")
    def swbmm(e):
        return e.matmul(out=kb.pb[1][:, 0:2], lhsT=kb.onesrow_f[:],
                        rhs=kb.cell[0:1, 0:2], start=True, stop=True)
    s.task("pe", swbmm, reads=("cell", "onesrow"), writes=("pb1",),
           name="swb_mm")
    s.task("dve", lambda e: e.tensor_copy(out=kb.swb[:], in_=kb.pb[1][:, 0:2]),
           reads=("pb1",), writes=("swb",), name="swb_ev")

    for wi, (wf, wdt, dstf, spf, arena, spn) in enumerate((
            (w1f, IN, kb.wq1f, sp1, "WA", "dWQ1d"),
            (w2f, D, kb.wq2f, sp2, "WB", "dWQ2d"))):
        for hh in range(NT):
            sl = slice(hh * 1024, (hh + 1) * 1024)
            srcv = dtile(wf, wdt, hh)
            s.task("qs", lambda e, sc=srcv: e.dma_start(out=stage[:], in_=sc),
                   writes=("stage",), name=f"qw_ld{wi}_{hh}", nofence=True)
            def qa(e, wi=wi):
                return e.tensor_scalar(out=stage[:], in0=stage[:],
                                       scalar1=kb.swb[:, wi:wi + 1],
                                       scalar2=MAGIC, op0=ALU.mult,
                                       op1=ALU.add)
            s.task("pool", qa, reads=("stage", "swb"), writes=("stage",),
                   name=f"qw_a{wi}_{hh}")
            def qb(e):
                return e.tensor_scalar(out=stage[:], in0=stage[:],
                                       scalar1=-MAGIC, scalar2=-1.0,
                                       op0=ALU.add, op1=ALU.max)
            s.task("dve", qb, reads=("stage",), writes=("stage",),
                   name=f"qw_b{wi}_{hh}")
            def qc(e, dsl=dstf[:, sl]):
                return e.tensor_scalar(out=dsl, in0=stage[:], scalar1=1.0,
                                       scalar2=None, op0=ALU.min)
            s.task("dve", qc, reads=("stage",), writes=(arena,),
                   name=f"qw_c{wi}_{hh}")
            s.task("qg", lambda e, dsl=dstf[:, sl], ssl=dtile(spf, wdt, hh):
                   e.dma_start(out=ssl, in_=dsl), reads=(arena,),
                   writes=(spn,), name=f"qw_sp{wi}_{hh}", nofence=True)


def phase_ffn(kb, src, dst, nslots, tag, first):
    nc, s, dr = kb.nc, kb.s, kb.dram
    SC = kb.SC
    pb = kb.pb
    SC.reset(kb.p1_sc_base)
    if not first:
        s.fence()
        w1r = dr["WQ1d"].rearrange("(c p) n -> p c n", p=128)
        w2r = dr["WQ2d"].rearrange("(c p) n -> p c n", p=128)
        s.task("qs", lambda e: e.dma_start(out=kb.wq1s[:], in_=w1r),
               reads=("dWQ1d",), writes=("WA",), name="p2_ldw1", nofence=True)
        s.task("qs", lambda e: e.dma_start(out=kb.wq2s[:], in_=w2r),
               reads=("dWQ2d",), writes=("WB",), name="p2_ldw2", nofence=True)

    Xs = [SC.view([128, D], F32) for _ in range(2)]
    Xr = SC.view([128, D], F32)
    X1o = SC.view([128, D], F32)
    h = SC.view([128, IN], F32)
    xt = SC.view([128, D], F32)
    xq = SC.view([128, D], BF16)
    xqT = SC.view([128, D], BF16)
    ht = kb.stage_view
    hq = SC.view([128, 1024], BF16)
    hqT = SC.view([128, IN], BF16)
    st = [SC.view([128, 32], F32) for _ in range(2)]
    st2 = [SC.view([128, 32], F32) for _ in range(2)]
    stq1 = SC.view([128, 4], BF16)
    stq2 = SC.view([128, 4], BF16)
    strow1 = [SC.view([4, 128], BF16) for _ in range(2)]
    strow2 = [SC.view([4, 128], BF16) for _ in range(2)]

    if first:
        s.fence()

    xr = dr[src]
    dstr = dr[dst]

    C_SUM, C_SQ0, C_SQ1, C_MAX, C_MIN = 0, 1, 2, 3, 4
    C_M, C_V, C_A, C_ARG, C_RLN, C_RRMS = 5, 6, 7, 8, 9, 10
    C_RA, C_AM, C_S, C_C, C_MC, C_SV, C_SVH, C_SVL, C_INV = \
        11, 12, 13, 14, 15, 16, 17, 18, 19
    C_X0, C_X1, C_X2 = 20, 22, 24
    Hs_SUM, Hs_AM, Hs_ARG, Hs_RR, Hs_S, Hs_C, Hs_SV, Hs_SVH, Hs_SVL, Hs_INV = \
        8, 9, 10, 11, 12, 13, 14, 15, 16, 17
    Hs_X0, Hs_X1, Hs_X2 = 18, 19, 20

    s.begin_stages()
    for c in range(nslots):
        p = c % 2
        x3 = c % 2
        t = f"{tag}.{c}"
        rsrc = f"d{src}.{c}" if src != "x" else f"dx.{c}"
        s.task("qs", lambda e, c=c, x3=x3: e.dma_start(
            out=Xs[x3][:], in_=xr[c * CH:(c + 1) * CH, :]),
            reads=(rsrc,), writes=(f"{tag}Xs{x3}",), name=f"{t}.ldX",
            stage=0, chunk=c)
        def stats1(e, p=p, x3=x3):
            e.reduce_sum(out=st[p][:, C_SUM:C_SUM + 1], in_=Xs[x3][:],
                         axis=AX.X)
            e.reduce_max(out=st[p][:, C_MAX:C_MAX + 1], in_=Xs[x3][:],
                         axis=AX.X)
            return e.tensor_reduce(out=st[p][:, C_MIN:C_MIN + 1],
                                   in_=Xs[x3][:], axis=AX.X, op=ALU.min)
        s.task("dve", stats1, reads=(f"{tag}Xs{x3}",),
               writes=(f"{tag}st{p}",), name=f"{t}.stats1", stage=1, chunk=c)
        def sq(e, p=p, x3=x3):
            e.activation(out=ht[:, 0:512], in_=Xs[x3][:, 0:512],
                         func=AF.Square,
                         accum_out=st[p][:, C_SQ0:C_SQ0 + 1])
            return e.activation(out=ht[:, 512:1024], in_=Xs[x3][:, 512:1024],
                                func=AF.Square,
                                accum_out=st[p][:, C_SQ1:C_SQ1 + 1])
        s.task("act", sq, reads=(f"{tag}Xs{x3}", "csb"),
               writes=("stage", f"{tag}st{p}"), name=f"{t}.sq",
               stage=1, chunk=c)
        def stats2(e, p=p):
            e = VG(nc)
            stp = st[p]
            col = lambda i, w=1: stp[:, i:i + w]
            e.tensor_scalar(out=col(C_M), in0=col(C_SUM), scalar1=1.0 / D,
                            scalar2=None, op0=ALU.mult)
            e.tensor_tensor(out=col(C_V), in0=col(C_SQ0), in1=col(C_SQ1),
                            op=ALU.add)
            e.tensor_scalar(out=col(C_V), in0=col(C_V), scalar1=1.0 / D,
                            scalar2=None, op0=ALU.mult)
            e.tensor_tensor(out=col(C_A), in0=col(C_M), in1=col(C_M),
                            op=ALU.mult)
            e.tensor_tensor(out=col(C_V), in0=col(C_V), in1=col(C_A),
                            op=ALU.subtract)
            e.tensor_scalar(out=col(C_A), in0=col(C_V), scalar1=1e-5,
                            scalar2=None, op0=ALU.add)
            e.reciprocal(out=col(C_RA), in_=col(C_A))
            e.tensor_tensor(out=col(C_ARG), in0=col(C_V), in1=col(C_RA),
                            op=ALU.mult)
            e.tensor_scalar(out=col(C_ARG), in0=col(C_ARG), scalar1=1e-6,
                            scalar2=None, op0=ALU.add)
            emit_rsqrt(nc, stp[:, C_A:C_A + 2], stp[:, C_RLN:C_RLN + 2],
                       stp[:, C_X0:C_X0 + 2], stp[:, C_X1:C_X1 + 2],
                       stp[:, C_X2:C_X2 + 2], e=e)
            e.tensor_tensor(out=col(C_AM), in0=col(C_MAX), in1=col(C_M),
                            op=ALU.subtract)
            e.tensor_tensor(out=col(C_MIN), in0=col(C_M), in1=col(C_MIN),
                            op=ALU.subtract)
            e.tensor_tensor(out=col(C_AM), in0=col(C_AM), in1=col(C_MIN),
                            op=ALU.max)
            e.tensor_tensor(out=col(C_AM), in0=col(C_AM), in1=col(C_RLN),
                            op=ALU.mult)
            e.tensor_tensor(out=col(C_AM), in0=col(C_AM), in1=col(C_RRMS),
                            op=ALU.mult)
            e.tensor_scalar(out=col(C_AM), in0=col(C_AM), scalar1=1e-5,
                            scalar2=None, op0=ALU.max)
            e.reciprocal(out=col(C_S), in_=col(C_AM))
            e.tensor_scalar(out=col(C_S), in0=col(C_S), scalar1=127.0,
                            scalar2=None, op0=ALU.mult)
            e.tensor_tensor(out=col(C_C), in0=col(C_RLN), in1=col(C_RRMS),
                            op=ALU.mult)
            e.tensor_tensor(out=col(C_C), in0=col(C_C), in1=col(C_S),
                            op=ALU.mult)
            e.tensor_tensor(out=col(C_MC), in0=col(C_M), in1=col(C_C),
                            op=ALU.mult)
            e.tensor_scalar(out=col(C_MC), in0=col(C_MC), scalar1=-1.0,
                            scalar2=None, op0=ALU.mult)
            e.tensor_tensor(out=col(C_SV), in0=col(C_S), in1=kb.swb[:, 0:1],
                            op=ALU.mult)
            e.reciprocal(out=col(C_INV), in_=col(C_SV))
            e.tensor_copy(out=kb.bfscr[:, 0:1], in_=col(C_SV))
            e.tensor_copy(out=col(C_SVH), in_=kb.bfscr[:, 0:1])
            e.tensor_tensor(out=col(C_SVL), in0=col(C_SV), in1=col(C_SVH),
                            op=ALU.subtract)
            e.tensor_copy(out=stq1[:, 0:1], in_=col(C_SVH))
            e.tensor_copy(out=stq1[:, 1:2], in_=col(C_SVH))
            return e.tensor_copy(out=stq1[:, 2:3], in_=col(C_SVL))
        s.task("dve", stats2, reads=(f"{tag}st{p}", "swb"),
               writes=(f"{tag}st{p}", f"{tag}stq1", "bfscr"),
               name=f"{t}.stats2", stage=1, chunk=c)
        def qa(e, p=p, x3=x3):
            return e.tensor_scalar(out=xt[:], in0=Xs[x3][:],
                                   scalar1=st[p][:, C_C:C_C + 1],
                                   scalar2=st[p][:, C_MC:C_MC + 1],
                                   op0=ALU.mult, op1=ALU.add)
        s.task("pool", qa, reads=(f"{tag}Xs{x3}", f"{tag}st{p}"),
               writes=(f"{tag}xt",), name=f"{t}.qa", stage=2, chunk=c)
        def qb(e):
            return e.tensor_scalar(out=xq[:], in0=xt[:], scalar1=MAGIC,
                                   scalar2=-MAGIC, op0=ALU.add, op1=ALU.add)
        s.task("dve", qb, reads=(f"{tag}xt",), writes=(f"{tag}xq",),
               name=f"{t}.qb", stage=2, chunk=c)
        def txq(e):
            last = None
            for k in range(8):
                last = e.transpose(out=pb[6][:, k * 128:(k + 1) * 128],
                                   in_=xq[:, k * 128:(k + 1) * 128],
                                   identity=kb.identb[:])
            return last
        s.task("pe", txq, reads=(f"{tag}xq", "identb"), writes=("pb6",),
               name=f"{t}.txq", stage=2, chunk=c)
        def evxqT(e):
            e.copy(out=xqT[:, 0:512], in_=pb[6][:, 0:512])
            return e.copy(out=xqT[:, 512:1024], in_=pb[6][:, 512:1024])
        s.task("act", evxqT, reads=("pb6", "csb", "csbb"),
               writes=(f"{tag}xqT",),
               name=f"{t}.evxqT", stage=2, chunk=c)
        def mint(e):
            return e.transpose(out=pb[6][0:4, 0:128], in_=stq1[:],
                               identity=kb.identb[:])
        s.task("pe", mint, reads=(f"{tag}stq1", "identb"), writes=("pb6",),
               name=f"{t}.minT", stage=2, chunk=c)
        def evrow(e, p=p):
            return e.tensor_copy(out=strow1[p][:], in_=pb[6][0:4, 0:128])
        s.task("dve", evrow, reads=("pb6",), writes=(f"{tag}strow1{p}",),
               name=f"{t}.evrow", stage=2, chunk=c)
        for q in range(4):
            bA, bB = (0, 1) if q % 2 == 0 else (2, 3)
            def mm1(e, p=p, q=q, bA=bA, bB=bB):
                last = None
                for k in range(8):
                    lt = xqT[:, k * 128:(k + 1) * 128]
                    last = e.matmul(out=pb[bA][:], lhsT=lt,
                                    rhs=kb.wq1s[:, k, q * 1024:q * 1024 + 512],
                                    start=(k == 0), stop=False,
                                    skip_group_check=True)
                    last = e.matmul(
                        out=pb[bB][:], lhsT=lt,
                        rhs=kb.wq1s[:, k, q * 1024 + 512:(q + 1) * 1024],
                        start=(k == 0), stop=False, skip_group_check=True)
                last = e.matmul(out=pb[bA][:], lhsT=strow1[p][0:3, :],
                                rhs=kb.b1x[0:3, q * 1024:q * 1024 + 512],
                                start=False, stop=True, skip_group_check=True)
                last = e.matmul(out=pb[bB][:], lhsT=strow1[p][0:3, :],
                                rhs=kb.b1x[0:3, q * 1024 + 512:(q + 1) * 1024],
                                start=False, stop=True, skip_group_check=True)
                return last
            s.task("pe", mm1, reads=(f"{tag}xqT", f"{tag}strow1{p}", "WA",
                                     "bx_b1"),
                   writes=(f"pb{bA}", f"pb{bB}"), name=f"{t}.mm1q{q}",
                   stage=3, chunk=c)
            def gel(e, p=p, q=q, bA=bA, bB=bB):
                e.activation(out=h[:, q * 1024:q * 1024 + 512], in_=pb[bA][:],
                             func=AF.Gelu, scale=st[p][:, C_INV:C_INV + 1])
                return e.activation(out=h[:, q * 1024 + 512:(q + 1) * 1024],
                                    in_=pb[bB][:], func=AF.Gelu,
                                    scale=st[p][:, C_INV:C_INV + 1])
            s.task("act", gel, reads=(f"pb{bA}", f"pb{bB}", f"{tag}st{p}",
                                      "csb"),
                   writes=(f"{tag}h",), name=f"{t}.gelu{q}", stage=3, chunk=c)
            def hsq_q(e, p=p, q=q):
                e.activation(out=ht[:, 0:512],
                             in_=h[:, q * 1024:q * 1024 + 512],
                             func=AF.Square,
                             accum_out=st2[p][:, 2 * q:2 * q + 1])
                return e.activation(out=ht[:, 512:1024],
                                    in_=h[:, q * 1024 + 512:(q + 1) * 1024],
                                    func=AF.Square,
                                    accum_out=st2[p][:, 2 * q + 1:2 * q + 2])
            s.task("act", hsq_q, reads=(f"{tag}h", "csb"),
                   writes=("stage", f"{tag}st2{p}"), name=f"{t}.hsq{q}",
                   stage=3, chunk=c)
            def ham_q(e, p=p, q=q):
                return e.reduce_max(out=st2[p][:, 21 + q:22 + q],
                                    in_=h[:, q * 1024:(q + 1) * 1024],
                                    axis=AX.X, apply_absolute_value=True)
            s.task("dve", ham_q, reads=(f"{tag}h",), writes=(f"{tag}st2{p}",),
                   name=f"{t}.ham{q}", stage=3, chunk=c)
        def hstats(e, p=p):
            e = VG(nc)
            stp = st2[p]
            col = lambda i, w=1: stp[:, i:i + w]
            e.reduce_sum(out=col(Hs_SUM), in_=stp[:, 0:8], axis=AX.X)
            e.reduce_max(out=col(Hs_AM), in_=stp[:, 21:25], axis=AX.X)
            e.tensor_scalar(out=col(Hs_ARG), in0=col(Hs_SUM),
                            scalar1=1.0 / IN, scalar2=1e-6, op0=ALU.mult,
                            op1=ALU.add)
            emit_rsqrt(nc, col(Hs_ARG), col(Hs_RR), col(Hs_X0), col(Hs_X1),
                       col(Hs_X2), e=e)
            e.tensor_tensor(out=col(Hs_AM), in0=col(Hs_AM), in1=col(Hs_RR),
                            op=ALU.mult)
            e.tensor_scalar(out=col(Hs_AM), in0=col(Hs_AM), scalar1=1e-5,
                            scalar2=None, op0=ALU.max)
            e.reciprocal(out=col(Hs_S), in_=col(Hs_AM))
            e.tensor_scalar(out=col(Hs_S), in0=col(Hs_S), scalar1=127.0,
                            scalar2=None, op0=ALU.mult)
            e.tensor_tensor(out=col(Hs_C), in0=col(Hs_RR), in1=col(Hs_S),
                            op=ALU.mult)
            e.tensor_tensor(out=col(Hs_SV), in0=col(Hs_S), in1=kb.swb[:, 1:2],
                            op=ALU.mult)
            e.reciprocal(out=col(Hs_INV), in_=col(Hs_SV))
            e.tensor_scalar(out=col(Hs_INV), in0=col(Hs_INV), scalar1=0.5,
                            scalar2=None, op0=ALU.mult)
            e.tensor_copy(out=kb.bfscr[:, 1:2], in_=col(Hs_SV))
            e.tensor_copy(out=col(Hs_SVH), in_=kb.bfscr[:, 1:2])
            e.tensor_tensor(out=col(Hs_SVL), in0=col(Hs_SV), in1=col(Hs_SVH),
                            op=ALU.subtract)
            e.tensor_copy(out=stq2[:, 0:1], in_=col(Hs_SVH))
            e.tensor_copy(out=stq2[:, 1:2], in_=col(Hs_SVH))
            return e.tensor_copy(out=stq2[:, 2:3], in_=col(Hs_SVL))
        s.task("dve", hstats, reads=(f"{tag}st2{p}", "swb"),
               writes=(f"{tag}st2{p}", f"{tag}stq2", "bfscr"),
               name=f"{t}.hstats", stage=4, chunk=c)
        def mint2(e):
            return e.transpose(out=pb[6][0:4, 128:256], in_=stq2[:],
                               identity=kb.identb[:])
        s.task("pe", mint2, reads=(f"{tag}stq2", "identb"), writes=("pb6",),
               name=f"{t}.minT2", stage=4, chunk=c)
        def evrow2(e, p=p):
            return e.tensor_copy(out=strow2[p][:], in_=pb[6][0:4, 128:256])
        s.task("dve", evrow2, reads=("pb6",), writes=(f"{tag}strow2{p}",),
               name=f"{t}.evrow2", stage=4, chunk=c)
        for sl in range(4):
            htb = ht if sl % 2 == 0 else xt
            htres = "stage" if sl % 2 == 0 else f"{tag}xt"
            def hqa(e, p=p, sl=sl, htb=htb):
                return e.tensor_scalar(out=htb[:],
                                       in0=h[:, sl * 1024:(sl + 1) * 1024],
                                       scalar1=st2[p][:, Hs_C:Hs_C + 1],
                                       scalar2=MAGIC, op0=ALU.mult,
                                       op1=ALU.add)
            s.task("pool", hqa, reads=(f"{tag}h", f"{tag}st2{p}"),
                   writes=(htres,), name=f"{t}.hqa{sl}", stage=4, chunk=c)
            def hqb(e, htb=htb):
                return e.tensor_scalar(out=hq[:], in0=htb[:], scalar1=-MAGIC,
                                       scalar2=127.0, op0=ALU.add,
                                       op1=ALU.min)
            s.task("dve", hqb, reads=(htres,), writes=(f"{tag}hq",),
                   name=f"{t}.hqb{sl}", stage=4, chunk=c)
            def thq(e):
                last = None
                for k in range(8):
                    last = e.transpose(out=pb[7][:, k * 128:(k + 1) * 128],
                                       in_=hq[:, k * 128:(k + 1) * 128],
                                       identity=kb.identb[:])
                return last
            s.task("pe", thq, reads=(f"{tag}hq", "identb"), writes=("pb7",),
                   name=f"{t}.thq{sl}", stage=4, chunk=c)
            def evhqT(e, sl=sl):
                e.copy(out=hqT[:, sl * 1024:sl * 1024 + 512],
                       in_=pb[7][:, 0:512])
                return e.copy(
                    out=hqT[:, sl * 1024 + 512:(sl + 1) * 1024],
                    in_=pb[7][:, 512:1024])
            s.task("act", evhqT, reads=("pb7", "csb", "csbb"),
                   writes=(f"{tag}hqT",),
                   name=f"{t}.evhqT{sl}", stage=4, chunk=c)
        def mm2(e, p=p):
            last = None
            for k in range(32):
                lt = hqT[:, k * 128:(k + 1) * 128]
                last = e.matmul(out=pb[4][:], lhsT=lt,
                                rhs=kb.wq2s[:, k, 0:512], start=(k == 0),
                                stop=False, skip_group_check=True)
                last = e.matmul(out=pb[5][:], lhsT=lt,
                                rhs=kb.wq2s[:, k, 512:1024], start=(k == 0),
                                stop=False, skip_group_check=True)
            last = e.matmul(out=pb[4][:], lhsT=strow2[p][0:3, :],
                            rhs=kb.b2x[0:3, 0:512], start=False, stop=True,
                            skip_group_check=True)
            last = e.matmul(out=pb[5][:], lhsT=strow2[p][0:3, :],
                            rhs=kb.b2x[0:3, 512:1024], start=False, stop=True,
                            skip_group_check=True)
            return last
        s.task("pe", mm2, reads=(f"{tag}hqT", f"{tag}strow2{p}", "WB",
                                 "bx_b2"),
               writes=("pb4", "pb5"), name=f"{t}.mm2", stage=5, chunk=c)
        s.task("qg", lambda e, c=c: e.dma_start(
            out=Xr[:], in_=xr[c * CH:(c + 1) * CH, :]),
            reads=(rsrc,), writes=(f"{tag}Xr",), name=f"{t}.ldXr",
            stage=5, chunk=c)
        def x1ev(e, p=p):
            e.scalar_tensor_tensor(out=X1o[:, 0:512], in0=pb[4][:],
                                   scalar=st2[p][:, Hs_INV:Hs_INV + 1],
                                   in1=Xr[:, 0:512], op0=ALU.mult,
                                   op1=ALU.add)
            return e.scalar_tensor_tensor(out=X1o[:, 512:1024], in0=pb[5][:],
                                          scalar=st2[p][:, Hs_INV:Hs_INV + 1],
                                          in1=Xr[:, 512:1024],
                                          op0=ALU.mult, op1=ALU.add)
        s.task("dve", x1ev, reads=("pb4", "pb5", f"{tag}st2{p}",
                                   f"{tag}Xr"),
               writes=(f"{tag}X1o",), name=f"{t}.x1ev", stage=6, chunk=c)
        s.task("qa", lambda e, c=c: e.dma_start(
            out=dstr[c * CH:(c + 1) * CH, :], in_=X1o[:]),
            reads=(f"{tag}X1o",), writes=(f"d{dst}.{c}",), name=f"{t}.st",
            stage=6, chunk=c)
    s.flush_stages()


def phase_a1(kb):
    nc, s, dr = kb.nc, kb.s, kb.dram
    SC = kb.SC
    SC.reset(kb.p1_sc_base)
    s.fence()
    pb = kb.pb

    kb.WB.reset()
    wqv = kb.WB.view([128, 8, D], BF16)
    wkv = kb.WB.view([128, 8, D], BF16)
    wvv = kb.WB.view([128, 8, D], BF16)
    wov = kb.WB.view([128, 8, D], BF16)
    kb.WA.reset()
    wstage = kb.WA.view([128, 8, D], F32)
    for wname, dstv, normcol in (("wk", wkv, kb.anw), ("wv", wvv, kb.anw),
                                 ("wq", wqv, kb.anw), ("wo", wov, kb.onw)):
        wr = dr[wname].rearrange("(c p) n -> p c n", p=128)
        s.task("qs", lambda e, wr=wr: e.dma_start(out=wstage[:], in_=wr),
               writes=("WA",), name=f"a1_ld_{wname}", nofence=True)
        def fold(e, dstv=dstv, ncol=normcol):
            last = None
            for k in range(8):
                last = e.tensor_scalar(out=dstv[:, k, :], in0=wstage[:, k, :],
                                       scalar1=ncol[:, k:k + 1], scalar2=None,
                                       op0=ALU.mult)
            return last
        s.task("dve", fold, reads=("WA", "anw", "onw"), writes=("WB",),
               name=f"a1_fold_{wname}", nofence=True)

    X1s = [SC.view([128, D], F32) for _ in range(2)]
    X1r = SC.view([128, D], F32)
    tb = SC.view([128, D], BF16)
    tT = [SC.view([128, D], BF16) for _ in range(2)]
    Kb = [SC.view([128, D], BF16) for _ in range(2)]
    Vb = [SC.view([128, D], BF16) for _ in range(3)]
    Qb = SC.view([128, D], BF16)
    qT = [SC.view([128, D], BF16) for _ in range(2)]
    kT = SC.view([128, D], BF16)
    Ssb = SC.view([128, D], BF16)
    atnT = SC.view([128, D], BF16)
    o2f = kb.stage_view
    o_sb = SC.view([128, D], BF16)
    dumm = o2f
    on = SC.view([128, D], BF16)
    onT = SC.view([128, D], BF16)
    y1 = SC.view([128, D], F32)
    s1 = SC.view([128, D], F32)
    st = [SC.view([128, 24], F32) for _ in range(2)]
    stq = [SC.view([128, 4], BF16) for _ in range(8)]
    late = [SC.view([128, 1], F32) for _ in range(8)]
    strow = [SC.view([4, 128], BF16) for _ in range(2)]
    st2 = [SC.view([128, 48], F32) for _ in range(2)]

    C_SUM, C_SQ, C_M, C_V, C_A, C_ARG, C_RLN, C_RRMS = 0, 1, 2, 3, 4, 5, 6, 7
    C_CC, C_MC, C_COR, C_CH, C_CL = 8, 9, 10, 11, 12
    C_X0, C_X1, C_X2 = 14, 16, 18
    O_SS, O_ARG, O_RS, O_X0, O_X1, O_X2 = 0, 8, 16, 24, 32, 40

    x1r, y1r, s1r = dr["X1"], dr["Y1"], dr["S1"]

    s.begin_stages()
    for c in range(NS):
        p = c % 2
        x3 = c % 2
        x8 = c % 8
        x4 = c % 3
        t = f"A1.{c}"
        active = c >= ACT0
        j = c - ACT0
        s.task("qs", lambda e, c=c, x3=x3: e.dma_start(
            out=X1s[x3][:], in_=x1r[c * CH:(c + 1) * CH, :]),
            reads=(f"dX1.{c}",), writes=(f"AXs{x3}",), name=f"{t}.ld",
            stage=0, chunk=c)
        def stats1(e, p=p, x3=x3):
            e.reduce_sum(out=st[p][:, C_SUM:C_SUM + 1], in_=X1s[x3][:],
                         axis=AX.X)
            return e.scalar_tensor_tensor(out=dumm[:], in0=X1s[x3][:],
                                          scalar=1.0, in1=X1s[x3][:],
                                          op0=ALU.mult, op1=ALU.mult,
                                          accum_out=st[p][:, C_SQ:C_SQ + 1])
        s.task("dve", stats1, reads=(f"AXs{x3}",), writes=(f"Ast{p}", "stage"),
               name=f"{t}.stats1", stage=1, chunk=c)
        def stats2(e, p=p, active=active, x8=x8):
            e = VG(nc)
            stp = st[p]
            col = lambda i, w=1: stp[:, i:i + w]
            e.tensor_scalar(out=col(C_M), in0=col(C_SUM), scalar1=1.0 / D,
                            scalar2=None, op0=ALU.mult)
            e.tensor_scalar(out=col(C_V), in0=col(C_SQ), scalar1=1.0 / D,
                            scalar2=None, op0=ALU.mult)
            e.tensor_tensor(out=col(C_A), in0=col(C_M), in1=col(C_M),
                            op=ALU.mult)
            e.tensor_tensor(out=col(C_V), in0=col(C_V), in1=col(C_A),
                            op=ALU.subtract)
            e.tensor_scalar(out=col(C_A), in0=col(C_V), scalar1=1e-5,
                            scalar2=None, op0=ALU.add)
            e.reciprocal(out=col(C_ARG), in_=col(C_A))
            e.tensor_tensor(out=col(C_ARG), in0=col(C_V), in1=col(C_ARG),
                            op=ALU.mult)
            e.tensor_scalar(out=col(C_ARG), in0=col(C_ARG), scalar1=1e-5,
                            scalar2=None, op0=ALU.add)
            emit_rsqrt(nc, stp[:, C_A:C_A + 2], stp[:, C_RLN:C_RLN + 2],
                       stp[:, C_X0:C_X0 + 2], stp[:, C_X1:C_X1 + 2],
                       stp[:, C_X2:C_X2 + 2], e=e)
            e.tensor_tensor(out=col(C_CC), in0=col(C_RLN), in1=col(C_RRMS),
                            op=ALU.mult)
            e.tensor_tensor(out=col(C_MC), in0=col(C_M), in1=col(C_CC),
                            op=ALU.mult)
            last = e.tensor_scalar(out=col(C_MC), in0=col(C_MC), scalar1=-1.0,
                                   scalar2=None, op0=ALU.mult)
            if active:
                e.tensor_copy(out=late[x8][:], in_=col(C_RLN))
                e.tensor_tensor(out=col(C_COR), in0=col(C_M), in1=col(C_RLN),
                                op=ALU.mult)
                e.tensor_scalar(out=col(C_COR), in0=col(C_COR), scalar1=-1.0,
                                scalar2=None, op0=ALU.mult)
                e.tensor_copy(out=kb.bfscr[:, 2:3], in_=col(C_COR))
                e.tensor_copy(out=col(C_CH), in_=kb.bfscr[:, 2:3])
                e.tensor_tensor(out=col(C_CL), in0=col(C_COR), in1=col(C_CH),
                                op=ALU.subtract)
                e.tensor_copy(out=stq[x8][:, 0:1], in_=col(C_CH))
                last = e.tensor_copy(out=stq[x8][:, 1:2], in_=col(C_CL))
            return last
        s.task("dve", stats2, reads=(f"Ast{p}",),
               writes=(f"Ast{p}", f"Astq{x8}", f"Alate{x8}", "bfscr"),
               name=f"{t}.stats2", stage=1, chunk=c)
        def tq(e, p=p, x3=x3):
            return e.tensor_scalar(out=tb[:], in0=X1s[x3][:],
                                   scalar1=st[p][:, C_CC:C_CC + 1],
                                   scalar2=st[p][:, C_MC:C_MC + 1],
                                   op0=ALU.mult, op1=ALU.add)
        s.task("pool", tq, reads=(f"AXs{x3}", f"Ast{p}"), writes=("Atb",),
               name=f"{t}.tq", stage=2, chunk=c)
        def ttq(e):
            last = None
            for k in range(8):
                last = e.transpose(out=pb[6][:, k * 128:(k + 1) * 128],
                                   in_=tb[:, k * 128:(k + 1) * 128],
                                   identity=kb.identb[:])
            return last
        s.task("pe", ttq, reads=("Atb", "identb"), writes=("pb6",),
               name=f"{t}.tT", stage=2, chunk=c)
        def evtT(e, p=p):
            e.copy(out=tT[p][:, 0:512], in_=pb[6][:, 0:512])
            return e.copy(out=tT[p][:, 512:1024],
                          in_=pb[6][:, 512:1024])
        s.task("act", evtT, reads=("pb6", "csb", "csbb"),
               writes=(f"AtT{p}",),
               name=f"{t}.evtT", stage=2, chunk=c)
        def kmm(e, p=p):
            last = None
            for k in range(8):
                lt = tT[p][:, k * 128:(k + 1) * 128]
                last = e.matmul(out=pb[0][:], lhsT=lt, rhs=wkv[:, k, 0:512],
                                start=(k == 0), stop=(k == 7),
                                skip_group_check=True)
                last = e.matmul(out=pb[1][:], lhsT=lt, rhs=wkv[:, k, 512:1024],
                                start=(k == 0), stop=(k == 7),
                                skip_group_check=True)
            return last
        s.task("pe", kmm, reads=(f"AtT{p}", "WB"), writes=("pb0", "pb1"),
               name=f"{t}.kmm", stage=3, chunk=c)
        def evK(e, p=p, c=c):
            e.tensor_scalar(out=Kb[p][:, 0:512], in0=pb[0][:],
                            scalar1=kb.maskc[:, c:c + 1], scalar2=None,
                            op0=ALU.mult)
            return e.tensor_scalar(out=Kb[p][:, 512:1024], in0=pb[1][:],
                                   scalar1=kb.maskc[:, c:c + 1], scalar2=None,
                                   op0=ALU.mult)
        s.task("dve", evK, reads=("pb0", "pb1", "maskc"), writes=(f"AKb{p}",),
               name=f"{t}.evK", stage=3, chunk=c)
        def vmm(e, p=p):
            last = None
            for k in range(8):
                lt = tT[p][:, k * 128:(k + 1) * 128]
                last = e.matmul(out=pb[2][:], lhsT=lt, rhs=wvv[:, k, 0:512],
                                start=(k == 0), stop=(k == 7),
                                skip_group_check=True)
                last = e.matmul(out=pb[3][:], lhsT=lt, rhs=wvv[:, k, 512:1024],
                                start=(k == 0), stop=(k == 7),
                                skip_group_check=True)
            return last
        s.task("pe", vmm, reads=(f"AtT{p}", "WB"), writes=("pb2", "pb3"),
               name=f"{t}.vmm", stage=3, chunk=c)
        def evV(e, x4=x4):
            e.copy(out=Vb[x4][:, 0:512], in_=pb[2][:])
            return e.copy(out=Vb[x4][:, 512:1024], in_=pb[3][:])
        s.task("act", evV, reads=("pb2", "pb3", "csb", "csbb"),
               writes=(f"AVb{x4}",), name=f"{t}.evV", stage=3, chunk=c)

        if active:
            def qmm(e, p=p):
                last = None
                for k in range(8):
                    lt = tT[p][:, k * 128:(k + 1) * 128]
                    last = e.matmul(out=pb[0][:], lhsT=lt,
                                    rhs=wqv[:, k, 0:512],
                                    start=(k == 0), stop=(k == 7),
                                    skip_group_check=True)
                    last = e.matmul(out=pb[1][:], lhsT=lt,
                                    rhs=wqv[:, k, 512:1024], start=(k == 0),
                                    stop=(k == 7), skip_group_check=True)
                return last
            s.task("pe", qmm, reads=(f"AtT{p}", "WB"), writes=("pb0", "pb1"),
                   name=f"{t}.qmm", stage=4, chunk=c)
            def evQ(e):
                e.copy(out=Qb[:, 0:512], in_=pb[0][:])
                return e.copy(out=Qb[:, 512:1024], in_=pb[1][:])
            s.task("act", evQ, reads=("pb0", "pb1", "csb", "csbb"),
                   writes=("AQb",), name=f"{t}.evQ", stage=4, chunk=c)
            def tq8(e):
                last = None
                for k in range(8):
                    last = e.transpose(out=pb[6][:, k * 128:(k + 1) * 128],
                                       in_=Qb[:, k * 128:(k + 1) * 128],
                                       identity=kb.identb[:])
                return last
            s.task("pe", tq8, reads=("AQb", "identb"), writes=("pb6",),
                   name=f"{t}.tq8", stage=4, chunk=c)
            def evqT(e, p=p):
                e.copy(out=qT[p][:, 0:512], in_=pb[6][:, 0:512])
                return e.copy(out=qT[p][:, 512:1024],
                              in_=pb[6][:, 512:1024])
            s.task("act", evqT, reads=("pb6", "csb", "csbb"),
                   writes=(f"AqT{p}",),
                   name=f"{t}.evqT", stage=4, chunk=c)
            def tk8(e, p=p):
                last = None
                for k in range(8):
                    last = e.transpose(out=pb[7][:, k * 128:(k + 1) * 128],
                                       in_=Kb[p][:, k * 128:(k + 1) * 128],
                                       identity=kb.identb[:])
                return last
            s.task("pe", tk8, reads=(f"AKb{p}", "identb"), writes=("pb7",),
                   name=f"{t}.tk8", stage=4, chunk=c)
            def evkT(e):
                e.copy(out=kT[:, 0:512], in_=pb[7][:, 0:512])
                return e.copy(out=kT[:, 512:1024],
                              in_=pb[7][:, 512:1024])
            s.task("act", evkT, reads=("pb7", "csb", "csbb"),
                   writes=("AkT",),
                   name=f"{t}.evkT", stage=4, chunk=c)
            def scopy(e):
                e.tensor_copy(out=Ssb[:, 0:512], in_=pb[4][:])
                return e.tensor_copy(out=Ssb[:, 512:1024], in_=pb[5][:])
            s.task("dve", scopy, reads=("pb4", "pb5"), writes=("ASsb",),
                   name=f"{t}.scopy", stage=5, chunk=c)
            def smm(e, p=p):
                last = None
                for hh in range(H):
                    bank = 0 if hh < 4 else 1
                    off = (hh % 4) * 128
                    last = e.matmul(out=pb[bank][:, off:off + 128],
                                    lhsT=kT[:, hh * 128:(hh + 1) * 128],
                                    rhs=qT[p][:, hh * 128:(hh + 1) * 128],
                                    start=True, stop=True,
                                    skip_group_check=True)
                return last
            s.task("pe", smm, reads=("AkT", f"AqT{p}"), writes=("pb0", "pb1"),
                   name=f"{t}.smm", stage=5, chunk=c)
            def mskev(e):
                last = None
                for hh in range(H):
                    bank = 0 if hh < 4 else 1
                    off = (hh % 4) * 128
                    last = e.tensor_tensor(
                        out=atnT[:, hh * 128:(hh + 1) * 128],
                        in0=pb[bank][:, off:off + 128], in1=kb.trimaskT[:],
                        op=ALU.mult)
                return last
            s.task("dve", mskev, reads=("pb0", "pb1", "trimaskT"),
                   writes=("AatnT",), name=f"{t}.mskev", stage=5, chunk=c)
            def omm(e, p=p, x4=x4):
                last = None
                for hh in range(H):
                    bank = 2 if hh < 4 else 3
                    off = (hh % 4) * 128
                    last = e.matmul(out=pb[bank][:, off:off + 128],
                                    lhsT=atnT[:, hh * 128:(hh + 1) * 128],
                                    rhs=Vb[x4][:, hh * 128:(hh + 1) * 128],
                                    start=True, stop=False,
                                    skip_group_check=True)
                    last = e.matmul(out=pb[bank][:, off:off + 128],
                                    lhsT=qT[p][:, hh * 128:(hh + 1) * 128],
                                    rhs=Ssb[:, hh * 128:(hh + 1) * 128],
                                    start=False, stop=True,
                                    skip_group_check=True)
                return last
            s.task("pe", omm, reads=("AatnT", f"AqT{p}", f"AVb{x4}", "ASsb"),
                   writes=("pb2", "pb3"), name=f"{t}.omm", stage=5, chunk=c)
            def oev(e):
                e.copy(out=o_sb[:, 0:512], in_=pb[2][:])
                return e.copy(out=o_sb[:, 512:1024], in_=pb[3][:])
            s.task("act", oev, reads=("pb2", "pb3", "csb", "csbb"),
                   writes=("Aosb",), name=f"{t}.oev", stage=5, chunk=c)
            def osq(e):
                return e.scalar_tensor_tensor(out=o2f[:], in0=o_sb[:],
                                              scalar=1.0, in1=o_sb[:],
                                              op0=ALU.mult, op1=ALU.mult)
            s.task("dve", osq, reads=("Aosb",), writes=("stage",),
                   name=f"{t}.osq", stage=6, chunk=c)
            def ored(e, p=p):
                return e.tensor_reduce(
                    out=st2[p][:, O_SS:O_SS + 8].rearrange(
                        "p (g o) -> p g o", g=8),
                    in_=o2f[:].rearrange("p (g f) -> p g f", g=8),
                    axis=AX.X, op=ALU.add)
            s.task("dve", ored, reads=("stage",), writes=(f"Ast2{p}",),
                   name=f"{t}.ored", stage=6, chunk=c)
            def orstat(e, p=p):
                e = VG(nc)
                e.tensor_scalar(out=st2[p][:, O_ARG:O_ARG + 8],
                                in0=st2[p][:, O_SS:O_SS + 8],
                                scalar1=1.0 / 128, scalar2=1e-5,
                                op0=ALU.mult, op1=ALU.add)
                return emit_rsqrt(nc, st2[p][:, O_ARG:O_ARG + 8],
                                  st2[p][:, O_RS:O_RS + 8],
                                  st2[p][:, O_X0:O_X0 + 8],
                                  st2[p][:, O_X1:O_X1 + 8],
                                  st2[p][:, O_X2:O_X2 + 8], e=e)
            s.task("dve", orstat, reads=(f"Ast2{p}",), writes=(f"Ast2{p}",),
                   name=f"{t}.orstat", stage=6, chunk=c)
            def oscale(e, p=p):
                last = None
                for hh in range(H):
                    last = e.tensor_scalar(
                        out=on[:, hh * 128:(hh + 1) * 128],
                        in0=o_sb[:, hh * 128:(hh + 1) * 128],
                        scalar1=st2[p][:, O_RS + hh:O_RS + hh + 1],
                        scalar2=None, op0=ALU.mult)
                return last
            s.task("dve", oscale, reads=("Aosb", f"Ast2{p}"),
                   writes=("Aon",), name=f"{t}.oscale", stage=6, chunk=c)
            def ton(e):
                last = None
                for k in range(8):
                    last = e.transpose(out=pb[6][:, k * 128:(k + 1) * 128],
                                       in_=on[:, k * 128:(k + 1) * 128],
                                       identity=kb.identb[:])
                return last
            s.task("pe", ton, reads=("Aon", "identb"), writes=("pb6",),
                   name=f"{t}.ton", stage=6, chunk=c)
            def evonT(e):
                e.copy(out=onT[:, 0:512], in_=pb[6][:, 0:512])
                return e.copy(out=onT[:, 512:1024],
                              in_=pb[6][:, 512:1024])
            s.task("act", evonT, reads=("pb6", "csb", "csbb"),
                   writes=("AonT",),
                   name=f"{t}.evonT", stage=6, chunk=c)
            def mint(e, x8=x8):
                return e.transpose(out=pb[7][0:4, 0:128], in_=stq[x8][:],
                                   identity=kb.identb[:])
            s.task("pe", mint, reads=(f"Astq{x8}", "identb"), writes=("pb7",),
                   name=f"{t}.minT", stage=6, chunk=c)
            def evrow(e, p=p):
                return e.tensor_copy(out=strow[p][:], in_=pb[7][0:4, 0:128])
            s.task("dve", evrow, reads=("pb7",), writes=(f"Astrow{p}",),
                   name=f"{t}.evrow", stage=6, chunk=c)
            def womm(e, p=p):
                last = None
                for k in range(8):
                    lt = onT[:, k * 128:(k + 1) * 128]
                    last = e.matmul(out=pb[0][:], lhsT=lt,
                                    rhs=wov[:, k, 0:512],
                                    start=(k == 0), stop=False,
                                    skip_group_check=True)
                    last = e.matmul(out=pb[1][:], lhsT=lt,
                                    rhs=wov[:, k, 512:1024], start=(k == 0),
                                    stop=False, skip_group_check=True)
                last = e.matmul(out=pb[0][:], lhsT=strow[p][0:2, :],
                                rhs=kb.ones2[0:2, 0:512], start=False,
                                stop=True, skip_group_check=True)
                last = e.matmul(out=pb[1][:], lhsT=strow[p][0:2, :],
                                rhs=kb.ones2[0:2, 512:1024], start=False,
                                stop=True, skip_group_check=True)
                return last
            s.task("pe", womm, reads=("AonT", f"Astrow{p}", "WB", "ones2"),
                   writes=("pb0", "pb1"), name=f"{t}.womm", stage=6, chunk=c)
            s.task("qg", lambda e, c=c: e.dma_start(
                out=X1r[:], in_=x1r[c * CH:(c + 1) * CH, :]),
                reads=(f"dX1.{c}",), writes=("AX1r",), name=f"{t}.ldX1r",
                stage=6, chunk=c)
            def y1ev(e, x8=x8):
                e.scalar_tensor_tensor(out=y1[:, 0:512],
                                       in0=X1r[:, 0:512],
                                       scalar=late[x8][:],
                                       in1=pb[0][:], op0=ALU.mult,
                                       op1=ALU.add)
                return e.scalar_tensor_tensor(out=y1[:, 512:1024],
                                              in0=X1r[:, 512:1024],
                                              scalar=late[x8][:],
                                              in1=pb[1][:], op0=ALU.mult,
                                              op1=ALU.add)
            s.task("dve", y1ev, reads=("pb0", "pb1", f"Alate{x8}", "AX1r"),
                   writes=("Ay1",), name=f"{t}.y1ev", stage=6, chunk=c)
            def s1ev(e):
                return e.tensor_tensor(out=s1[:], in0=X1r[:], in1=y1[:],
                                       op=ALU.add)
            s.task("dve", s1ev, reads=("Ay1", "AX1r"), writes=("As1",),
                   name=f"{t}.s1ev", stage=6, chunk=c)
            s.task("qa", lambda e, j=j: e.dma_start(
                out=y1r[j * CH:(j + 1) * CH, :], in_=y1[:]),
                reads=("Ay1",), writes=(f"dY1.{j}",), name=f"{t}.stY1",
                stage=6, chunk=c)
            s.task("qa", lambda e, j=j: e.dma_start(
                out=s1r[j * CH:(j + 1) * CH, :], in_=s1[:]),
                reads=("As1",), writes=(f"dS1.{j}",), name=f"{t}.stS1",
                stage=6, chunk=c)

        if kb.debug == "A1":
            dd = kb.dram
            if c == 0:
                s.task("qg", lambda e, p=p: e.dma_start(out=dd["dbg_K0"],
                                                        in_=Kb[p][:]),
                       reads=(f"AKb{p}",), writes=("ddbg",), name=f"{t}.dK0",
                       stage=4, chunk=c)
                s.task("qg", lambda e, p=p: e.dma_start(out=dd["dbg_V0"],
                                                        in_=Vb[p][:]),
                       reads=(f"AVb{p}",), writes=("ddbg",), name=f"{t}.dV0",
                       stage=4, chunk=c)
                s.task("qg", lambda e: e.dma_start(out=dd["dbg_t0"],
                                                   in_=tb[:]),
                       reads=("Atb",), writes=("ddbg",), name=f"{t}.dt0",
                       stage=2, chunk=c)
            if c == ACT0:
                s.task("qg", lambda e: e.dma_start(out=dd["dbg_S32"],
                                                   in_=Ssb[:]),
                       reads=("ASsb",), writes=("ddbg",), name=f"{t}.dS32",
                       stage=5, chunk=c)
                s.task("qg", lambda e, p=p: e.dma_start(out=dd["dbg_q32"],
                                                        in_=qT[p][:]),
                       reads=(f"AqT{p}",), writes=("ddbg",), name=f"{t}.dq32",
                       stage=5, chunk=c)
                s.task("qg", lambda e: e.dma_start(out=dd["dbg_at32"],
                                                   in_=atnT[:]),
                       reads=("AatnT",), writes=("ddbg",), name=f"{t}.dat32",
                       stage=5, chunk=c)
                s.task("qg", lambda e: e.dma_start(out=dd["dbg_on32"],
                                                   in_=on[:]),
                       reads=("Aon",), writes=("ddbg",), name=f"{t}.don32",
                       stage=6, chunk=c)
        def supd(e, p=p, c=c, x4=x4):
            last = None
            for hh in range(H):
                bank = 4 if hh < 4 else 5
                off = (hh % 4) * 128
                last = e.matmul(out=pb[bank][:, off:off + 128],
                                lhsT=Kb[p][:, hh * 128:(hh + 1) * 128],
                                rhs=Vb[x4][:, hh * 128:(hh + 1) * 128],
                                start=(c == 0 and hh % 4 == 0),
                                stop=(c == NS - 1),
                                skip_group_check=True)
            return last
        s.task("pe", supd, reads=(f"AKb{p}", f"AVb{x4}"),
               writes=("pb4", "pb5"), name=f"{t}.supd", stage=5, chunk=c)
    s.flush_stages()


def phase_a2a(kb):
    nc, s, dr = kb.nc, kb.s, kb.dram
    SC = kb.SC
    SC.reset(kb.p1_sc_base)
    s.fence()
    pb = kb.pb

    kb.WA.reset()
    wgv = kb.WA.view([128, 8, IN], BF16)
    kb.WB.reset()
    wuv = kb.WB.view([128, 8, IN], BF16)
    wgr = dr["wgate"].rearrange("(c p) n -> p c n", p=128)
    wur = dr["wup"].rearrange("(c p) n -> p c n", p=128)
    s.task("qg", lambda e: e.dma_start(out=wgv[:], in_=wgr),
           writes=("WA",), name="a2_ldwg", nofence=True)
    s.task("qg", lambda e: e.dma_start(out=wuv[:], in_=wur),
           writes=("WB",), name="a2_ldwu", nofence=True)

    Y1s = [SC.view([128, D], F32) for _ in range(2)]
    zb = SC.view([128, D], BF16)
    zT = [SC.view([128, D], BF16) for _ in range(2)]
    sg = [SC.view([128, 512], F32) for _ in range(2)]
    g = [SC.view([128, IN], BF16) for _ in range(2)]
    dumm = SC.view([128, D], F32)
    st = [SC.view([128, 8], F32) for _ in range(2)]
    Z_SQ, Z_ARG, Z_RR, Z_X0, Z_X1, Z_X2 = 0, 1, 2, 3, 4, 5

    y1r, gr = dr["Y1"], dr["G"]

    s.begin_stages()
    for j in range(NACT):
        p = j % 2
        t = f"A2a.{j}"
        s.task("qs", lambda e, j=j, p=p: e.dma_start(
            out=Y1s[p][:], in_=y1r[j * CH:(j + 1) * CH, :]),
            reads=(f"dY1.{j}",), writes=(f"BYs{p}",), name=f"{t}.ld",
            stage=0, chunk=j)
        def stats(e, p=p):
            e = VG(nc)
            e.scalar_tensor_tensor(out=dumm[:], in0=Y1s[p][:], scalar=1.0,
                                   in1=Y1s[p][:], op0=ALU.mult, op1=ALU.mult,
                                   accum_out=st[p][:, Z_SQ:Z_SQ + 1])
            e.tensor_scalar(out=st[p][:, Z_ARG:Z_ARG + 1],
                            in0=st[p][:, Z_SQ:Z_SQ + 1], scalar1=1.0 / D,
                            scalar2=1e-5, op0=ALU.mult, op1=ALU.add)
            return emit_rsqrt(nc, st[p][:, Z_ARG:Z_ARG + 1],
                              st[p][:, Z_RR:Z_RR + 1],
                              st[p][:, Z_X0:Z_X0 + 1],
                              st[p][:, Z_X1:Z_X1 + 1],
                              st[p][:, Z_X2:Z_X2 + 1], e=e)
        s.task("dve", stats, reads=(f"BYs{p}",), writes=(f"Bst{p}", "Bdumm"),
               name=f"{t}.stats", stage=1, chunk=j)
        def zq(e, p=p):
            return e.scalar_tensor_tensor(out=zb[:], in0=Y1s[p][:],
                                          scalar=st[p][:, Z_RR:Z_RR + 1],
                                          in1=kb.mnwb[:], op0=ALU.mult,
                                          op1=ALU.mult)
        s.task("dve", zq, reads=(f"BYs{p}", f"Bst{p}", "mnwb"),
               writes=("Bzb",), name=f"{t}.zq", stage=2, chunk=j)
        def tz(e):
            last = None
            for k in range(8):
                last = e.transpose(out=pb[6][:, k * 128:(k + 1) * 128],
                                   in_=zb[:, k * 128:(k + 1) * 128],
                                   identity=kb.identb[:])
            return last
        s.task("pe", tz, reads=("Bzb", "identb"), writes=("pb6",),
               name=f"{t}.tz", stage=2, chunk=j)
        def evzT(e, p=p):
            e.copy(out=zT[p][:, 0:512], in_=pb[6][:, 0:512])
            return e.copy(out=zT[p][:, 512:1024],
                          in_=pb[6][:, 512:1024])
        s.task("act", evzT, reads=("pb6", "csb", "csbb"),
               writes=(f"BzT{p}",),
               name=f"{t}.evzT", stage=2, chunk=j)
        for sl in range(8):
            bG = 0 if sl % 2 == 0 else 2
            bU = 1 if sl % 2 == 0 else 3
            def gu(e, p=p, sl=sl, bG=bG, bU=bU):
                last = None
                for k in range(8):
                    lt = zT[p][:, k * 128:(k + 1) * 128]
                    last = e.matmul(out=pb[bG][:], lhsT=lt,
                                    rhs=wgv[:, k, sl * 512:(sl + 1) * 512],
                                    start=(k == 0), stop=(k == 7),
                                    skip_group_check=True)
                    last = e.matmul(out=pb[bU][:], lhsT=lt,
                                    rhs=wuv[:, k, sl * 512:(sl + 1) * 512],
                                    start=(k == 0), stop=(k == 7),
                                    skip_group_check=True)
                return last
            s.task("pe", gu, reads=(f"BzT{p}", "WA", "WB"),
                   writes=(f"pb{bG}", f"pb{bU}"), name=f"{t}.gu{sl}",
                   stage=3, chunk=j)
            def sil(e, p=p, bG=bG):
                return e.activation(out=sg[p][:], in_=pb[bG][:], func=AF.Silu)
            s.task("act", sil, reads=(f"pb{bG}", "csb", "csbb"),
                   writes=(f"Bsg{p}",), name=f"{t}.sil{sl}", stage=3, chunk=j)
            def gm(e, p=p, sl=sl, bU=bU):
                return e.tensor_tensor(out=g[p][:, sl * 512:(sl + 1) * 512],
                                       in0=sg[p][:], in1=pb[bU][:],
                                       op=ALU.mult)
            s.task("dve", gm, reads=(f"Bsg{p}", f"pb{bU}"),
                   writes=(f"Bg{p}",), name=f"{t}.gm{sl}", stage=3, chunk=j)
        s.task("qg", lambda e, j=j, p=p: e.dma_start(
            out=gr[j * CH:(j + 1) * CH, :], in_=g[p][:]),
            reads=(f"Bg{p}",), writes=(f"dG.{j}",), name=f"{t}.stG",
            stage=4, chunk=j)
    s.flush_stages()


def phase_a2b(kb):
    nc, s, dr = kb.nc, kb.s, kb.dram
    SC = kb.SC
    SC.reset(kb.p1_sc_base)
    s.fence()
    pb = kb.pb

    kb.WB.reset()
    wdv = kb.WB.view([128, 32, D], BF16)
    wdr = dr["wdown"].rearrange("(c p) n -> p c n", p=128)
    s.task("qg", lambda e: e.dma_start(out=wdv[:], in_=wdr),
           writes=("WB",), name="a2b_ldwd", nofence=True)

    g2 = [SC.view([128, IN], BF16) for _ in range(2)]
    gT = [SC.view([128, IN], BF16) for _ in range(2)]
    S1s = [SC.view([128, D], F32) for _ in range(4)]
    X2o_ = SC.view([128, D], F32)
    X2o = [X2o_, X2o_]

    gr, s1r, x2r = dr["G"], dr["S1"], dr["X2"]

    s.begin_stages()
    for j in range(NACT):
        p = j % 2
        x4 = j % 4
        t = f"A2b.{j}"
        s.task("qs", lambda e, j=j, p=p: e.dma_start(
            out=g2[p][:], in_=gr[j * CH:(j + 1) * CH, :]),
            reads=(f"dG.{j}",), writes=(f"Cg{p}",), name=f"{t}.ldg",
            stage=0, chunk=j)
        s.task("qa", lambda e, j=j, x4=x4: e.dma_start(
            out=S1s[x4][:], in_=s1r[j * CH:(j + 1) * CH, :]),
            reads=(f"dS1.{j}",), writes=(f"CS1{x4}",), name=f"{t}.lds",
            stage=0, chunk=j)
        for r in range(4):
            bank = 6 if r % 2 == 0 else 7
            def tg(e, p=p, r=r, bank=bank):
                last = None
                for tt_ in range(8):
                    kk = r * 8 + tt_
                    last = e.transpose(
                        out=pb[bank][:, tt_ * 128:(tt_ + 1) * 128],
                        in_=g2[p][:, kk * 128:(kk + 1) * 128],
                        identity=kb.identb[:])
                return last
            s.task("pe", tg, reads=(f"Cg{p}", "identb"),
                   writes=(f"pb{bank}",), name=f"{t}.tg{r}", stage=1, chunk=j)
            def evgT(e, p=p, r=r, bank=bank):
                return e.copy(out=gT[p][:, r * 1024:(r + 1) * 1024],
                              in_=pb[bank][:])
            s.task("act", evgT, reads=(f"pb{bank}", "csb", "csbb"),
                   writes=(f"CgT{p}",),
                   name=f"{t}.evgT{r}", stage=1, chunk=j)
        def dmm(e, p=p):
            last = None
            for k in range(32):
                lt = gT[p][:, k * 128:(k + 1) * 128]
                last = e.matmul(out=pb[0][:], lhsT=lt, rhs=wdv[:, k, 0:512],
                                start=(k == 0), stop=(k == 31),
                                skip_group_check=True)
                last = e.matmul(out=pb[1][:], lhsT=lt,
                                rhs=wdv[:, k, 512:1024],
                                start=(k == 0), stop=(k == 31),
                                skip_group_check=True)
            return last
        s.task("pe", dmm, reads=(f"CgT{p}", "WB"), writes=("pb0", "pb1"),
               name=f"{t}.dmm", stage=2, chunk=j)
        def x2ev(e, p=p, x4=x4):
            e.tensor_tensor(out=X2o[p][:, 0:512], in0=S1s[x4][:, 0:512],
                            in1=pb[0][:], op=ALU.add)
            return e.tensor_tensor(out=X2o[p][:, 512:1024],
                                   in0=S1s[x4][:, 512:1024], in1=pb[1][:],
                                   op=ALU.add)
        s.task("dve", x2ev, reads=("pb0", "pb1", f"CS1{x4}"),
               writes=("CX2",), name=f"{t}.x2ev", stage=3, chunk=j)
        s.task("qs", lambda e, j=j, p=p: e.dma_start(
            out=x2r[j * CH:(j + 1) * CH, :], in_=X2o[p][:]),
            reads=("CX2",), writes=(f"dX2.{j}",), name=f"{t}.stX2",
            stage=3, chunk=j)
    s.flush_stages()



_CACHE = {}


def _build(debug=None):
    if debug not in _CACHE:
        _CACHE[debug] = build_program(debug)
    return _CACHE[debug]


def make_in_maps(inputs):
    X = np.ascontiguousarray(np.asarray(inputs["L0_attn_hidden_states"],
                                        dtype=np.float32))
    B, T, _ = X.shape
    w = {k: np.ascontiguousarray(np.asarray(v, dtype=np.float32))
         for k, v in inputs.items() if k != "L0_attn_hidden_states"}

    consts = np.zeros((128, 8), np.float32)
    consts[:, 1] = 1.0
    constsb = np.zeros((128, 4), ml_dtypes.bfloat16)
    constsb[:, 1] = 1.0
    base = {
        "consts": consts, "constsb": np.asarray(constsb),
        "identb": np.asarray(np.eye(128, dtype=ml_dtypes.bfloat16)),
        "trimaskT": np.asarray(np.triu(np.ones((128, 128), np.float32))
                               .astype(ml_dtypes.bfloat16)),
        "ones2": np.asarray(np.ones((2, D), ml_dtypes.bfloat16)),
        "mnwb": np.asarray(np.tile(
            np.asarray(w["L0_attn_mlp_norm_w"], np.float32)[None, :],
            (128, 1)).astype(ml_dtypes.bfloat16)),
        "anw": np.ascontiguousarray(
            np.asarray(w["L0_attn_attn_norm_w"], np.float32)
            .reshape(8, 128).T),
        "onw": np.ascontiguousarray(
            np.tile(np.asarray(w["L0_attn_attn_out_norm_w"], np.float32), 8)
            .reshape(8, 128).T),
        "wq": w["L0_attn_Wq"], "wk": w["L0_attn_Wk"], "wv": w["L0_attn_Wv"],
        "wo": w["L0_attn_Wo"], "wgate": w["L0_attn_Wgate"],
        "wup": w["L0_attn_Wup"], "wdown": w["L0_attn_Wdown"],
        "w1t": np.ascontiguousarray(w["L0_ffn_w1"].T),
        "w2t": np.ascontiguousarray(w["L0_ffn_w2"].T),
        "b1": np.ascontiguousarray(w["L0_ffn_b1"].reshape(1, IN)),
        "b2": np.ascontiguousarray(w["L0_ffn_b2"].reshape(1, D)),
    }

    half = T // 2
    in_maps = []
    for b in range(B):
        for role in range(2):
            m = dict(base)
            if role == 0:
                xin = np.concatenate(
                    [np.zeros((half, D), np.float32), X[b, :half]], axis=0)
                mk = np.concatenate([np.zeros(half, np.float32),
                                     np.ones(half, np.float32)])
            else:
                xin = X[b]
                mk = np.ones(T, np.float32)
            m["x"] = np.ascontiguousarray(xin)
            m["maskc"] = np.ascontiguousarray(mk.reshape(NS, CH).T)
            in_maps.append(m)
    return X, in_maps


def kernel(**inputs):
    X, in_maps = make_in_maps(inputs)
    B, T, _ = X.shape
    kb = _build(DEBUG_PHASE)
    res = run_bass_kernel_spmd(kb.nc, in_maps, core_ids=list(range(8)))
    if DEBUG_PHASE is not None:
        return [res.results[c] for c in range(8)]
    half = T // 2
    out = np.empty_like(X)
    for b in range(B):
        out[b, :half] = res.results[2 * b]["out"]
        out[b, half:] = res.results[2 * b + 1]["out"]
    return out
